# version 01
# baseline (speedup 1.0000x reference)
import contextlib
import math
import numpy as np
import ml_dtypes
import concourse.bass as bass
import concourse.mybir as mybir
from concourse.bass_utils import run_bass_kernel_spmd

F32 = mybir.dt.float32
BF16 = mybir.dt.bfloat16
I32 = mybir.dt.int32
U8 = mybir.dt.uint8
F32R = mybir.dt.float32r
OP = mybir.AluOpType
ACT = mybir.ActivationFunctionType
AX = mybir.AxisListType.X
BF = ml_dtypes.bfloat16

D = 2048
KC = 16
T = 1024
NT = 8
MAGIC = 1.5 * 2.0 ** 23
BH, BHD = 16, 128
B_BASE = 10000.0
B_EPS = 1e-6
LN_EPS = 1e-5
FFN_EPS = 1e-5
I_FFN = 4096
W_CNT = float(I_FFN * D)
_ESZ = {F32: 4, BF16: 2, F32R: 4, I32: 4, U8: 1}


class Sched:
    ENG = ("tensor", "vector", "scalar", "sync", "gpsimd")

    def __init__(self):
        self.recs = []
        self.lastw = {}
        self.readers = {}
        self.n_ovr = 0
        self.alias_pending = {}

    def alias(self, tok, olds):
        self.alias_pending.setdefault(tok, [])
        self.alias_pending[tok].extend(olds)

    def add(self, eng, emit, r=(), w=(), dma=False, ovr=False):
        w = list(w)
        for t in list(w):
            if t in self.alias_pending:
                w.extend(self.alias_pending.pop(t))
        idx = len(self.recs)
        deps = set()
        for t in r:
            j = self.lastw.get(t)
            if j is not None:
                deps.add(j)
        for t in w:
            j = self.lastw.get(t)
            if j is not None:
                deps.add(j)
            for j in self.readers.get(t, ()):
                deps.add(j)
        ovr_id = None
        if ovr:
            ovr_id = self.n_ovr
            self.n_ovr += 1
        self.recs.append({"idx": idx, "eng": eng, "emit": emit, "deps": deps,
                          "dma": dma, "inc": 16 if dma else 1, "ovr": ovr_id})
        for t in r:
            self.readers.setdefault(t, []).append(idx)
        for t in w:
            self.lastw[t] = idx
            self.readers[t] = []
        return idx

    def finalize(self, nc, sems, ovr_sems):
        recs = self.recs
        for rc in recs:
            rc["xdeps"] = [d for d in rc["deps"]
                           if recs[d]["eng"] != rc["eng"]
                           or recs[d]["dma"] or recs[d]["ovr"] is not None
                           or rc["eng"] in ("vector", "scalar")]
        need = set()
        for rc in recs:
            if rc["dma"]:
                need.add(rc["idx"])
            for d in rc["xdeps"]:
                need.add(d)
        cum = {}
        for e in self.ENG:
            c = 0
            for rc in recs:
                if rc["eng"] != e or rc["ovr"] is not None:
                    continue
                if rc["idx"] in need:
                    c += rc["inc"]
                cum[rc["idx"]] = c
        per = {e: [rc for rc in recs if rc["eng"] == e] for e in self.ENG}

        def runner(en):
            def body(eng):
                hw = {e: 0 for e in self.ENG}
                ohw = set()
                for rc in per[en]:
                    for d in sorted(rc["xdeps"]):
                        dr = recs[d]
                        if dr["ovr"] is not None:
                            if dr["ovr"] not in ohw:
                                eng.wait_ge(ovr_sems[dr["ovr"]], 1)
                                ohw.add(dr["ovr"])
                        else:
                            v = cum[d]
                            if v > hw[dr["eng"]]:
                                eng.wait_ge(sems[dr["eng"]], v)
                                hw[dr["eng"]] = v
                    inst = rc["emit"](eng)
                    if rc["ovr"] is not None:
                        inst.then_inc(ovr_sems[rc["ovr"]], 1)
                    elif rc["idx"] in need:
                        inst.then_inc(sems[en], rc["inc"])
            return body

        g = nc.Block()
        blk = g.__enter__()
        blk.tensor(runner("tensor"))
        blk.vector(runner("vector"))
        blk.scalar(runner("scalar"))
        blk.sync(runner("sync"))
        blk.gpsimd(runner("gpsimd"))
        g.__exit__(None, None, None)
        return {"n_ops": len(recs), "n_incs": len(need),
                "n_waits": sum(len(rc["xdeps"]) for rc in recs)}


class Arena:
    def __init__(self, ap, nbytes, sched):
        self.ap = ap
        self.nbytes = nbytes
        self.free = [(0, nbytes)]
        self.live = {}
        self.owners = []
        self.s = sched
        self.peak = 0

    def alloc(self, name, parts, freedim, dt, align=64, toks=None):
        toks = tuple(toks) if toks else (name,)
        nb = freedim * _ESZ[dt]
        nba = (nb + align - 1) // align * align
        for i, (off, sz) in enumerate(self.free):
            o = (off + align - 1) // align * align
            if o + nba <= off + sz:
                repl = [p for p in ((off, o - off), (o + nba, off + sz - o - nba))
                        if p[1] > 0]
                self.free[i:i + 1] = repl
                self.live[name] = (o, nba, toks)
                used = sum(x[1] for x in self.live.values())
                self.peak = max(self.peak, used)
                olds = []
                for (fo, fs, tks) in self.owners:
                    if fo < o + nba and o < fo + fs:
                        olds.extend(tks)
                if olds:
                    self.owners = [x for x in self.owners
                                   if not (x[0] < o + nba and o < x[0] + x[1])]
                    for tk in toks:
                        self.s.alias(tk, olds)
                return self.ap[0:parts, o:o + nb].bitcast(dt)
        raise MemoryError(f"arena: no room for {name} ({nba}B) free={self.free}")

    def freebuf(self, *names):
        for name in names:
            off, sz, toks = self.live.pop(name)
            self.free.append((off, sz))
            self.owners.append((off, sz, toks))
        self.free.sort()
        m = []
        for off, sz in self.free:
            if m and m[-1][0] + m[-1][1] == off:
                m[-1] = (m[-1][0], m[-1][1] + sz)
            else:
                m.append([off, sz])
        self.free = [tuple(x) for x in m]


def _prep_lhsT(w_io, out_block=128):
    In, Out = w_io.shape
    kc = In // 128
    nb = Out // out_block
    a = w_io.reshape(kc, 128, nb, out_block).transpose(2, 1, 0, 3)
    return np.ascontiguousarray(a.reshape(nb, 128, kc * out_block),
                                dtype=np.float32)


def _alibi_lam():
    return -(2.0 ** (-8.0 * (np.arange(BH, dtype=np.float64) + 1.0) / BH))


def host_constants(core):
    hh = core % 2
    H = 1024 * hh
    c = {}
    c["qpos"] = (H + np.arange(T, dtype=np.float32)).reshape(1, T)
    kpos = (H - 1024 + np.arange(2048, dtype=np.float32)).reshape(1, 2048)
    c["kposr"] = kpos
    c["invalidr"] = (2.0 * (kpos < 0)).astype(np.float32)
    c["idb"] = np.eye(128, dtype=np.float32).astype(BF)
    c["idf"] = np.eye(128, dtype=np.float32)
    d = np.arange(128)
    c["diagm"] = (d[None, :] >= d[:, None]).astype(BF)
    c["onesb"] = np.ones((1, 128), BF)
    c["onesf"] = np.ones((1, 128), np.float32)
    lam = _alibi_lam()
    i = np.arange(128, dtype=np.float64)
    dt_tab = np.zeros((128, BH * 128), np.float32)
    ea = np.zeros((128, BH * 128), np.float32)
    eb = np.zeros((128, BH * NT), np.float32)
    dk = np.zeros((128, BH), np.float32)
    for h in range(BH):
        m = np.exp(lam[h] * (i[None, :] - i[:, None])) * (i[None, :] >= i[:, None])
        dt_tab[:, h * 128:(h + 1) * 128] = m.astype(np.float32)
        ea[:, h * 128:(h + 1) * 128] = \
            np.exp(lam[h] * (i + 1.0))[None, :].astype(np.float32)
        if hh == 1:
            for cc in range(NT):
                eb[:, h * NT + cc] = math.exp(lam[h] * 128.0 * cc)
        dk[:, h] = np.exp(lam[h] * (127.0 - i)).astype(np.float32)
    c["dt_tab"] = dt_tab
    c["ea_rep"] = ea
    c["ebcol"] = eb
    c["dkcol"] = dk
    j = np.arange(BHD // 2, dtype=np.float64)
    theta = B_BASE ** (-2.0 * j / BHD)
    cs = np.zeros((128, NT * 128), np.float32)
    for t in range(NT):
        pos = H + t * 128 + np.arange(128, dtype=np.float64)
        ang = pos[:, None] * theta[None, :]
        cs[:, t * 128:t * 128 + 64] = np.cos(ang).astype(np.float32)
        cs[:, t * 128 + 64:t * 128 + 128] = np.sin(ang).astype(np.float32)
    c["cs_tm"] = cs
    return c


_C0 = {"qpos": (1, T), "kposr": (1, 2048), "invalidr": (1, 2048),
       "idb": (128, 128), "idf": (128, 128), "diagm": (128, 128),
       "onesb": (1, 128), "onesf": (1, 128)}
_CB = {"dt_tab": (128, BH * 128), "ea_rep": (128, BH * 128),
       "ebcol": (128, BH * NT), "dkcol": (128, BH), "cs_tm": (128, NT * 128)}
_CDT = {"idb": BF16, "diagm": BF16, "onesb": BF16}

_NC_CACHE = {}
DBG_TAPS = []


def build_nc(n_sub=4):
    if n_sub in _NC_CACHE:
        return _NC_CACHE[n_sub]
    nc = bass.Bass("TRN2", target_bir_lowering=False)
    s = Sched()

    def din(name, shape, dt):
        return nc.dram_tensor(name, list(shape), dt, kind="ExternalInput").ap()

    xo_d = din("xo", (NT, 128, D), F32)
    xp_d = din("xp", (NT, 128, D), F32)
    cu_d = din("cu", (1, 3), I32)
    nw_d = din("nwrow", (1, D), F32)
    out_d = nc.dram_tensor("out", [NT, 128, D], F32, kind="ExternalOutput").ap()
    cdr = {k: din(k, sh, _CDT.get(k, F32)) for k, sh in {**_C0, **_CB}.items()}
    wdr = {
        "wa_q": din("wa_q", (16, 128, D), F32),
        "wa_k": din("wa_k", (4, 128, D), F32),
        "wa_v": din("wa_v", (1, 128, KC * 512), F32),
        "wa_o": din("wa_o", (4, 128, KC * 512), F32),
        "wb_q": din("wb_q", (4, 128, KC * 512), F32),
        "wb_k": din("wb_k", (4, 128, KC * 512), F32),
        "wb_v": din("wb_v", (4, 128, KC * 512), F32),
        "wb_o": din("wb_o", (4, 128, KC * 512), F32),
        "wb_g1": din("wb_g1", (1, 128, D), F32),
        "wb_g2": din("wb_g2", (128, D), F32),
        "wua": din("wua", (4, 128, D), F32),
        "wda": din("wda", (2, 128, I_FFN), F32),
    }
    hsp_d = nc.dram_tensor("hsp", [32, 128, T], F32).ap()
    xsp_d = nc.dram_tensor("xsp", [NT, 128, D], F32).ap()
    xg_d = nc.dram_tensor("xgsp", [NT, 128, D], BF16).ap()
    ar_in = nc.dram_tensor("ar_in", [1, 2], F32).ap()
    ar_out = nc.dram_tensor("ar_out", [1, 2], F32).ap()
    agu_in = nc.dram_tensor("agu_in", [4, 128, D], BF16).ap()
    agu_out = nc.dram_tensor("agu_out", [32, 128, D], BF16).ap()
    agd_in = nc.dram_tensor("agd_in", [2, 128, I_FFN], BF16).ap()
    agd_out = nc.dram_tensor("agd_out", [16, 128, I_FFN], BF16).ap()
    ccs_in = nc.dram_tensor("ccs_in", [BH * 128, 128], F32).ap()
    ccs_out = nc.dram_tensor("ccs_out", [2 * BH * 128, 128], F32).ap()

    total = (nc.sbuf_bytes_remaining - 512) // 64 * 64
    A = Arena(nc.alloc_sbuf_tensor("arena", [128, total], U8).ap(), total, s)
    pb = [nc.alloc_psum_tensor(f"pb{i}", [128, 512], F32).ap() for i in range(8)]
    PT = [f"ps{i}" for i in range(8)]

    stk = contextlib.ExitStack()
    sems = {e: stk.enter_context(nc.semaphore(f"sem_{e}")) for e in Sched.ENG}
    ovr_sems = [stk.enter_context(nc.semaphore(f"ovr{i}")) for i in range(4)]

    Xh = {}
    XTOKS = tuple(f"x{t}" for t in range(NT))
    Xh["v"] = A.alloc("X", 128, NT * D, F32, toks=XTOKS)
    C = {k: A.alloc(k, sh[0], sh[1], _CDT.get(k, F32)) for k, sh in _C0.items()}
    cu_s = A.alloc("cu", 1, 4, I32)
    smal = A.alloc("smal", 128, 256, F32)
    bn6 = A.alloc("bn6", 128, 24, F32)
    lnsc = A.alloc("lnsc", 128, 16, F32)
    sq_rep = A.alloc("sq_rep", 128, T, BF16)
    skcol = A.alloc("skcol", 128, 16, F32)
    sqrow_b = A.alloc("sqrow_b", 1, T, BF16)
    skprow = A.alloc("skprow", 1, 2048, F32)
    sval = A.alloc("sval", 1, 8, F32)
    svrep = A.alloc("svrep", 128, 4, F32)
    su127 = A.alloc("su127", 1, 128, F32)
    brow = A.alloc("brow", 1, 1024, F32)
    m1t = A.alloc("m1t", 8, 128, F32)
    wpart = A.alloc("wpart", 2, 1, F32)
    onesp = A.alloc("onesp", 128, 128, F32)

    def smv(off, n=1):
        return smal[:, off:off + n]

    def X():
        return Xh["v"]

    def dma(eng, out, in_, r=(), w=()):
        s.add(eng, lambda E, o=out, i=in_: E.dma_start(out=o, in_=i),
              r=r, w=w, dma=True)

    def vop(fn, r=(), w=()):
        s.add("vector", fn, r=r, w=w)

    def sop(fn, r=(), w=()):
        s.add("scalar", fn, r=r, w=w)

    def top(fn, r=(), w=()):
        s.add("tensor", fn, r=r, w=w)

    def tap(name, ap, rtoks):
        if name not in DBG_TAPS:
            return
        sh = list(ap.shape)
        dbg = nc.dram_tensor("dbg_" + name, sh, ap.dtype,
                             kind="ExternalOutput").ap()
        dma("sync", dbg, ap, r=tuple(rtoks), w=("dbg",))

    for k in _C0:
        dma("sync", C[k], cdr[k], w=(k,))
    dma("sync", cu_s[0:1, 0:3], cu_d, w=("cu",))
    for t in range(NT):
        dma("sync", X()[:, t * D:(t + 1) * D], xo_d[t], w=(f"x{t}",))
    vop(lambda E: E.memset(onesp, 1.0), w=("onesp",))

    vop(lambda E: E.tensor_copy(smal[0:1, 200:203], cu_s[0:1, 0:3]),
        r=("cu",), w=("l0",))
    vop(lambda E: E.tensor_tensor(smal[0:1, 204:205], smal[0:1, 201:202],
                                  smal[0:1, 200:201], OP.subtract),
        r=("l0",), w=("l0",))
    vop(lambda E: E.tensor_scalar(sqrow_b[0:1, :], C["qpos"][0:1, :],
                                  smal[0:1, 204:205], None, OP.is_ge),
        r=("l0", "qpos"), w=("sqrow",))
    vop(lambda E: E.tensor_scalar(skprow[0:1, :], C["kposr"][0:1, :],
                                  smal[0:1, 204:205], None, OP.is_ge),
        r=("l0", "kposr"), w=("skprow",))
    vop(lambda E: E.tensor_tensor(skprow[0:1, :], skprow[0:1, :],
                                  C["invalidr"][0:1, :], OP.add),
        r=("invalidr", "skprow"), w=("skprow",))
    for half in range(2):
        top(lambda E, h=half: E.matmul(
            pb[6][:, 0:512], C["onesb"][0:1, :],
            sqrow_b[0:1, h * 512:(h + 1) * 512], start=True, stop=True),
            r=("sqrow", "onesb"), w=(PT[6],))
        vop(lambda E, h=half: E.tensor_copy(sq_rep[:, h * 512:(h + 1) * 512],
                                            pb[6][:, 0:512]),
            r=(PT[6],), w=("sq_rep",))
    for m in range(16):
        top(lambda E, m=m: E.transpose(pb[6][:, 0:1],
                                       skprow[0:1, m * 128:(m + 1) * 128],
                                       C["idf"][0:1, 0:1]),
            r=("skprow", "idf"), w=(PT[6],))
        vop(lambda E, m=m: E.tensor_copy(skcol[:, m:m + 1], pb[6][:, 0:1]),
            r=(PT[6],), w=("skcol",))

    tap("sq_rep", sq_rep, ("sq_rep",))
    tap("skcol", skcol, ("skcol",))
    tap("l0", smal[0:1, 200:208], ("l0",))
    A.freebuf("qpos", "kposr", "invalidr", "skprow", "sqrow_b")

    def ln_stats(xa, rtok):
        for q in range(4):
            vop(lambda E, q=q: E.bn_stats(bn6[:, q * 6:(q + 1) * 6],
                                          xa[:, q * 512:(q + 1) * 512]),
                r=rtok, w=("bn6",))
        vop(lambda E: E.tensor_reduce(lnsc[:, 0:1], bn6[:, 1:24:3], AX, OP.add),
            r=("bn6",), w=("lnsc",))
        vop(lambda E: E.tensor_reduce(lnsc[:, 1:2], bn6[:, 2:24:3], AX, OP.add),
            r=("bn6",), w=("lnsc",))
        vop(lambda E: E.tensor_tensor(lnsc[:, 2:10], bn6[:, 1:24:3],
                                      bn6[:, 1:24:3], OP.mult),
            r=("bn6",), w=("lnsc",))
        vop(lambda E: E.tensor_reduce(lnsc[:, 10:11], lnsc[:, 2:10], AX, OP.add),
            r=("lnsc",), w=("lnsc",))
        vop(lambda E: E.tensor_scalar(lnsc[:, 0:1], lnsc[:, 0:1], 0.125, None,
                                      OP.mult), r=("lnsc",), w=("lnsc",))
        vop(lambda E: E.tensor_tensor(lnsc[:, 3:4], lnsc[:, 0:1], lnsc[:, 0:1],
                                      OP.mult), r=("lnsc",), w=("lnsc",))
        vop(lambda E: E.tensor_scalar(lnsc[:, 1:2], lnsc[:, 1:2], 1.0 / 2048.0,
                                      None, OP.mult), r=("lnsc",), w=("lnsc",))
        vop(lambda E: E.tensor_scalar(lnsc[:, 10:11], lnsc[:, 10:11], 0.125, None,
                                      OP.mult), r=("lnsc",), w=("lnsc",))
        vop(lambda E: E.tensor_tensor(lnsc[:, 1:2], lnsc[:, 1:2], lnsc[:, 10:11],
                                      OP.add), r=("lnsc",), w=("lnsc",))
        vop(lambda E: E.tensor_tensor(lnsc[:, 1:2], lnsc[:, 1:2], lnsc[:, 3:4],
                                      OP.subtract), r=("lnsc",), w=("lnsc",))
        vop(lambda E: E.tensor_scalar(lnsc[:, 1:2], lnsc[:, 1:2], LN_EPS, None,
                                      OP.add), r=("lnsc",), w=("lnsc",))
        sop(lambda E: E.activation(lnsc[:, 4:5], lnsc[:, 1:2], ACT.Sqrt),
            r=("lnsc",), w=("lnsq",))
        vop(lambda E: E.reciprocal(smv(64), lnsc[:, 4:5]), r=("lnsq",), w=("lnA",))
        vop(lambda E: E.tensor_tensor(smv(80), lnsc[:, 0:1], smv(64), OP.mult),
            r=("lnA", "lnsc"), w=("lnB",))
        vop(lambda E: E.tensor_scalar(smv(80), smv(80), -1.0, None, OP.mult),
            r=("lnB",), w=("lnB",))

    def tm_to_fm(src_bf, rtoks, dst_fm, colf, wtok, bank_i=7, nwc=None):
        bank = pb[bank_i]
        for g in range(4):
            for kk in range(4):
                c = g * 4 + kk
                top(lambda E, c=c, kk=kk: E.transpose(
                    bank.bitcast(BF16)[:, kk * 128:(kk + 1) * 128],
                    src_bf[:, c * 128:(c + 1) * 128], C["idb"]),
                    r=tuple(rtoks) + ("idb",), w=(PT[bank_i],))
            for kk in range(4):
                c = g * 4 + kk
                col = colf(c)
                if nwc is None:
                    vop(lambda E, kk=kk, col=col: E.tensor_copy(
                        dst_fm[:, col:col + 128],
                        bank.bitcast(BF16)[:, kk * 128:(kk + 1) * 128]),
                        r=(PT[bank_i],), w=(wtok,))
                else:
                    vop(lambda E, kk=kk, col=col, c=c: E.tensor_scalar(
                        dst_fm[:, col:col + 128],
                        bank.bitcast(BF16)[:, kk * 128:(kk + 1) * 128],
                        nwc[:, c:c + 1], None, OP.mult),
                        r=(PT[bank_i], "nwcol"), w=(wtok,))

    wps = A.alloc("wps", 128, 2048, F32, toks=("wpsb0", "wl2"))
    wps2 = A.alloc("wps2", 128, 2048, F32, toks=("wpsb1", "tern"))
    gout = A.alloc("gout", 128, 2048, BF16)
    vop(lambda E: E.memset(smv(210, 2), 0.0), w=("wsum",))
    slices = [(wdr["wua"][j], 210, f"wl{j}") for j in range(4)]
    slices += [(wdr["wda"][j][:, hf * 2048:(hf + 1) * 2048], 211,
                f"wl{4 + 2 * j + hf}") for j in range(2) for hf in range(2)]
    for j, (src, acc, tk) in enumerate(slices):
        buf = wps if j % 2 == 0 else wps2
        bt = f"wpsb{j % 2}"
        dma("sync", buf, src, w=(bt,))
        vop(lambda E, b=buf: E.tensor_reduce(smv(212), b, AX, OP.add,
                                             apply_absolute_value=True),
            r=(bt,), w=("wred",))
        vop(lambda E, acc=acc: E.tensor_tensor(smv(acc), smv(acc), smv(212),
                                               OP.add),
            r=("wred", "wsum"), w=("wsum",))
    vop(lambda E: E.memset(smv(213), 1.0), w=("wone",))
    top(lambda E: E.matmul(pb[6][0:2, 0:1], smal[:, 210:212], smal[:, 213:214],
                           start=True, stop=True),
        r=("wsum", "wone"), w=(PT[6],))
    vop(lambda E: E.tensor_copy(wpart[0:2, 0:1], pb[6][0:2, 0:1]),
        r=(PT[6],), w=("wpartt",))
    dma("sync", ar_in, wpart[0:2, 0:1], r=("wpartt",), w=("arin",))
    s.add("gpsimd", lambda E: E.collective_compute(
        "AllReduce", OP.add, replica_groups=[list(range(8))],
        ins=[ar_in.opt()], outs=[ar_out.opt()]),
        r=("arin",), w=("arout",), ovr=True)
    dma("sync", sval[0:1, 0:2], ar_out, r=("arout",), w=("sval",))
    vop(lambda E: E.tensor_scalar(sval[0:1, 0:2], sval[0:1, 0:2], 1.0 / W_CNT,
                                  FFN_EPS, OP.mult, OP.max),
        r=("sval",), w=("sval",))
    vop(lambda E: E.reciprocal(sval[0:1, 2:4], sval[0:1, 0:2]),
        r=("sval",), w=("sval",))
    top(lambda E: E.matmul(pb[6][:, 0:4], C["onesf"][0:1, :], sval[0:1, 0:4],
                           start=True, stop=True),
        r=("sval", "onesf"), w=(PT[6],))
    vop(lambda E: E.tensor_copy(svrep[:, 0:4], pb[6][:, 0:4]),
        r=(PT[6],), w=("svrep",))
    for j, (src, acc, tk) in enumerate(slices):
        rcol = svrep[:, 2:3] if acc == 210 else svrep[:, 3:4]
        dma("sync", wps, src, w=("wl2",))
        vop(lambda E, rcol=rcol: E.tensor_scalar(wps2, wps, rcol, MAGIC,
                                                 OP.mult, OP.add),
            r=("wl2", "svrep"), w=("tern",))
        vop(lambda E: E.tensor_scalar(wps2, wps2, -MAGIC, -1.0, OP.add, OP.max),
            r=("tern",), w=("tern",))
        vop(lambda E: E.tensor_scalar(gout, wps2, 1.0, None, OP.min),
            r=("tern",), w=("gout",))
        if j < 4:
            dma("sync", agu_in[j], gout, r=("gout",), w=("aguin",))
        else:
            jj = j - 4
            dma("sync", agd_in[jj // 2][:, (jj % 2) * 2048:(jj % 2 + 1) * 2048],
                gout, r=("gout",), w=("agdin",))
    s.add("gpsimd", lambda E: E.collective_compute(
        "AllGather", OP.bypass, replica_groups=[list(range(8))],
        ins=[agu_in.opt()], outs=[agu_out.opt()]),
        r=("aguin",), w=("agu",), ovr=True)
    s.add("gpsimd", lambda E: E.collective_compute(
        "AllGather", OP.bypass, replica_groups=[list(range(8))],
        ins=[agd_in.opt()], outs=[agd_out.opt()]),
        r=("agdin",), w=("agd",), ovr=True)
    tap("gup", agu_out, ("agu",))
    tap("gdn", agd_out, ("agd",))
    tap("sval", sval[0:1, 0:4], ("sval",))
    A.freebuf("wps", "wps2", "gout")

    def build_attn_a():
        k2 = A.alloc("k2", 128, 4 * 2048, BF16)
        vv = A.alloc("vv", 128, 16 * 520, BF16)
        wblk0 = A.alloc("awb0", 128, D, BF16)
        wblk1 = A.alloc("awb1", 128, D, BF16)
        wvs = A.alloc("awvs", 128, KC * 512, BF16)
        ytmp = A.alloc("aytmp", 128, D, BF16)
        xpt0 = A.alloc("xpt0", 128, D, F32)
        xpt1 = A.alloc("xpt1", 128, D, F32)
        y1p = A.alloc("y1p", 128, KC * 512, BF16)
        WB = [wblk0, wblk1]
        XP = [xpt0, xpt1]
        vop(lambda E: E.memset(vv[:, 64::65], 1.0), w=("vv",))
        dma("gpsimd", wvs, wdr["wa_v"][0], w=("awvs",))

        def kv_proj(ysl, ytok, keyt0):
            for mc in range(4):
                dma("gpsimd", WB[mc % 2], wdr["wa_k"][mc], w=(f"awb{mc % 2}",))
                bk = pb[4 + mc % 2]
                for kc in range(KC):
                    top(lambda E, mc=mc, kc=kc, bk=bk: E.matmul(
                        bk[:, 0:512], WB[mc % 2][:, kc * 128:(kc + 1) * 128],
                        ysl(kc), start=(kc == 0), stop=(kc == KC - 1)),
                        r=(f"awb{mc % 2}", ytok), w=(PT[4 + mc % 2],))
                for g2 in range(2):
                    sop(lambda E, mc=mc, g2=g2, bk=bk, keyt0=keyt0: E.copy(
                        k2[g2 * 64:g2 * 64 + 64,
                           mc * 2048 + keyt0 * 128: mc * 2048 + keyt0 * 128 + 512],
                        bk[g2 * 64:g2 * 64 + 64, 0:512]),
                        r=(PT[4 + mc % 2],), w=("k2",))
            for t4 in range(4):
                kt = keyt0 + t4
                bk = pb[4 + t4 % 2]
                for kc in range(KC):
                    top(lambda E, t4=t4, kc=kc, bk=bk: E.matmul(
                        bk[:, 0:512], ysl(kc)[:, t4 * 128:t4 * 128 + 128],
                        wvs[:, kc * 512:(kc + 1) * 512],
                        start=(kc == 0), stop=(kc == KC - 1)),
                        r=("awvs", ytok), w=(PT[4 + t4 % 2],))
                for g in range(8):
                    sop(lambda E, g=g, kt=kt, bk=bk: E.copy(
                        vv[:, kt * 520 + g * 65: kt * 520 + g * 65 + 64],
                        bk[:, g * 64:(g + 1) * 64]),
                        r=(PT[4 + t4 % 2],), w=("vv",))

        for half in range(2):
            for t4 in range(4):
                tg = half * 4 + t4
                xb = XP[t4 % 2]
                xt = f"xpt{t4 % 2}"
                dma("sync", xb, xp_d[tg], w=(xt,))
                ln_stats(xb, (xt,))
                vop(lambda E, xb=xb: E.tensor_scalar(ytmp, xb, smv(64), smv(80),
                                                     OP.mult, OP.add),
                    r=(xt, "lnA", "lnB"), w=("aytmp",))
                tm_to_fm(ytmp, ("aytmp",), y1p,
                         lambda c, t4=t4: c * 512 + t4 * 128, "y1p")
            kv_proj(lambda kc, half=half:
                    y1p[:, kc * 512:(kc + 1) * 512], "y1p", half * 4)
        A.freebuf("xpt0", "xpt1", "y1p")
        y1f = A.alloc("y1f", 128, KC * T, BF16)
        qf = A.alloc("qf", 128, KC * T, BF16)
        for t in range(NT):
            ln_stats(X()[:, t * D:(t + 1) * D], (f"x{t}",))
            vop(lambda E, t=t: E.tensor_scalar(ytmp, X()[:, t * D:(t + 1) * D],
                                               smv(64), smv(80), OP.mult, OP.add),
                r=(f"x{t}", "lnA", "lnB"), w=("aytmp",))
            tm_to_fm(ytmp, ("aytmp",), y1f,
                     lambda c, t=t: c * T + t * 128, "y1f")
        for half in range(2):
            kv_proj(lambda kc, half=half:
                    y1f[:, kc * T + half * 512: kc * T + half * 512 + 512],
                    "y1f", 8 + half * 4)
        for mc in range(KC):
            dma("gpsimd", WB[mc % 2], wdr["wa_q"][mc], w=(f"awb{mc % 2}",))
            for half in range(2):
                bk = pb[4 + half]
                for kc in range(KC):
                    top(lambda E, mc=mc, kc=kc, half=half, bk=bk: E.matmul(
                        bk[:, 0:512], WB[mc % 2][:, kc * 128:(kc + 1) * 128],
                        y1f[:, kc * T + half * 512: kc * T + half * 512 + 512],
                        start=(kc == 0), stop=(kc == KC - 1)),
                        r=(f"awb{mc % 2}", "y1f"), w=(PT[4 + half],))
                sop(lambda E, mc=mc, half=half, bk=bk: E.copy(
                    qf[:, mc * T + half * 512: mc * T + half * 512 + 512],
                    bk[:, 0:512]), r=(PT[4 + half],), w=("qf",))
        tap("y1f", y1f, ("y1f",))
        tap("qf", qf, ("qf",))
        tap("k2", k2, ("k2",))
        tap("vv", vv, ("vv",))
        A.freebuf("y1f", "aytmp", "awb0", "awb1", "awvs")
        k2d = A.alloc("k2d", 128, 4 * 2048, BF16)
        dma("sync", k2d[0:64, :], k2[64:128, :], r=("k2",), w=("k2d",))
        dma("sync", k2d[64:128, :], k2[0:64, :], r=("k2",), w=("k2d",))
        zf = A.alloc("zf", 128, KC * T, BF16)
        pbh = A.alloc("pbh", 128, 4 * 512, BF16)
        maskb = A.alloc("maskb", 128, 512, BF16)
        recbuf = A.alloc("recbuf", 128, 512, F32)
        rsb = A.alloc("rsb", 128, 512, F32)
        zstage = A.alloc("zstage", 128, 512, BF16)
        for g in range(8):
            for qh in range(2):
                started = [False] * 4
                for m in range(1, 16):
                    jlo = max(max(0, m - 8), qh * 4)
                    jhi = min(min(7, m - 1), qh * 4 + 3)
                    if jlo > jhi:
                        continue
                    wc = (jhi - jlo + 1) * 128
                    woff = (jlo - qh * 4) * 128
                    vop(lambda E, m=m, jlo=jlo, wc=wc: E.tensor_scalar(
                        maskb[:, 0:wc], sq_rep[:, jlo * 128:jlo * 128 + wc],
                        skcol[:, m:m + 1], None, OP.is_equal),
                        r=("sq_rep", "skcol"), w=("maskb",))
                    if jlo == m - 8:
                        vop(lambda E: E.tensor_tensor(
                            maskb[:, 0:128], maskb[:, 0:128], C["diagm"],
                            OP.mult), r=("diagm", "maskb"), w=("maskb",))
                    for h4 in range(4):
                        h = 4 * g + h4
                        rb = 64 * (h % 2)
                        ksrc = k2 if (g % 2) == (h % 2) else k2d
                        ktok = "k2" if (g % 2) == (h % 2) else "k2d"
                        sbk = pb[4 + h4 % 2]
                        top(lambda E, g=g, m=m, rb=rb, jlo=jlo, wc=wc, h=h,
                            ksrc=ksrc, sbk=sbk: E.matmul(
                            sbk[:, 0:wc],
                            ksrc[rb:rb + 64,
                                 (g // 2) * 2048 + m * 128:
                                 (g // 2) * 2048 + m * 128 + 128],
                            qf[rb:rb + 64, (h // 2) * T + jlo * 128:
                               (h // 2) * T + jlo * 128 + wc],
                            start=True, stop=True),
                            r=(ktok, "qf"), w=(PT[4 + h4 % 2],))
                        sop(lambda E, h4=h4, wc=wc, sbk=sbk: E.activation(
                            pbh[:, h4 * 512:h4 * 512 + wc], sbk[:, 0:wc],
                            ACT.Exp, scale=0.125),
                            r=(PT[4 + h4 % 2],), w=("pbh",))
                        vop(lambda E, h4=h4, wc=wc: E.tensor_tensor(
                            pbh[:, h4 * 512:h4 * 512 + wc],
                            pbh[:, h4 * 512:h4 * 512 + wc], maskb[:, 0:wc],
                            OP.mult), r=("maskb", "pbh"), w=("pbh",))
                        top(lambda E, g=g, m=m, h4=h4, woff=woff, wc=wc,
                            st=(not started[h4]): E.matmul(
                            pb[h4][0:65, woff:woff + wc],
                            vv[:, m * 520 + 65 * g: m * 520 + 65 * g + 65],
                            pbh[:, h4 * 512:h4 * 512 + wc],
                            start=st, stop=False),
                            r=("vv", "pbh"), w=(PT[h4],))
                        started[h4] = True
                for h4 in range(4):
                    h = 4 * g + h4
                    zc = h // 2
                    vop(lambda E, h4=h4: E.reciprocal(
                        recbuf[64:65, :], pb[h4][64:65, 0:512]),
                        r=(PT[h4],), w=("recbuf",))
                    top(lambda E: E.matmul(
                        pb[6][:, 0:512], onesp[64:65, 0:128],
                        recbuf[64:65, :], start=True, stop=True),
                        r=("recbuf", "onesp"), w=(PT[6],))
                    vop(lambda E: E.tensor_copy(rsb, pb[6][:, 0:512]),
                        r=(PT[6],), w=("rsb",))
                    if h % 2 == 0:
                        vop(lambda E, h4=h4, zc=zc, qh=qh: E.tensor_tensor(
                            zf[0:64, zc * T + qh * 512: zc * T + qh * 512 + 512],
                            pb[h4][0:64, 0:512], rsb[0:64, :], OP.mult),
                            r=(PT[h4], "rsb"), w=("zf",))
                    else:
                        vop(lambda E, h4=h4: E.tensor_tensor(
                            zstage[0:64, :], pb[h4][0:64, 0:512], rsb[0:64, :],
                            OP.mult), r=(PT[h4], "rsb"), w=("zstage",))
                        dma("sync",
                            zf[64:128, zc * T + qh * 512: zc * T + qh * 512 + 512],
                            zstage[0:64, :], r=("zstage",), w=("zf",))
        A.freebuf("qf", "k2", "k2d", "vv", "pbh", "maskb", "recbuf", "rsb",
                  "zstage")
        tap("zf", zf, ("zf",))
        wo0 = A.alloc("awo0", 128, KC * 512, BF16)
        wo1 = A.alloc("awo1", 128, KC * 512, BF16)
        WO = [wo0, wo1]
        for nb in range(4):
            dma("gpsimd", WO[nb % 2], wdr["wa_o"][nb], w=(f"awo{nb % 2}",))
            for t in range(NT):
                bk = pb[4 + t % 2]
                for kc in range(KC):
                    top(lambda E, nb=nb, t=t, kc=kc, bk=bk: E.matmul(
                        bk[:, 0:512],
                        zf[:, kc * T + t * 128: kc * T + t * 128 + 128],
                        WO[nb % 2][:, kc * 512:(kc + 1) * 512],
                        start=(kc == 0), stop=(kc == KC - 1)),
                        r=(f"awo{nb % 2}", "zf"), w=(PT[4 + t % 2],))
                vop(lambda E, nb=nb, t=t, bk=bk: E.tensor_tensor(
                    X()[:, t * D + nb * 512: t * D + nb * 512 + 512],
                    X()[:, t * D + nb * 512: t * D + nb * 512 + 512],
                    bk[:, 0:512], OP.add),
                    r=(PT[4 + t % 2], f"x{t}"), w=(f"x{t}",))
        A.freebuf("zf", "awo0", "awo1")

    def build_ffn(call):
        sfx = f"f{call}"
        a1f = A.alloc("a1f" + sfx, 128, KC * T, BF16)
        ytf = A.alloc("ytf" + sfx, 128, D, F32)
        atm = A.alloc("atm" + sfx, 128, D, BF16)
        c1b = A.alloc("c1b" + sfx, 128, T, F32)
        s2b = A.alloc("s2b" + sfx, 128, T, F32)
        m2p = A.alloc("m2p" + sfx, 128, T, F32)
        hst0 = A.alloc("hst0" + sfx, 128, T, F32)
        hst1 = A.alloc("hst1" + sfx, 128, T, F32)
        gb0 = A.alloc("gb0" + sfx, 128, D, BF16)
        gb1 = A.alloc("gb1" + sfx, 128, D, BF16)
        HS = [hst0, hst1]
        GB = [gb0, gb1]
        TA1, TYT, TATM = "a1f" + sfx, "ytf" + sfx, "atm" + sfx
        for t in range(NT):
            ln_stats(X()[:, t * D:(t + 1) * D], (f"x{t}",))
            vop(lambda E, t=t: E.tensor_scalar(ytf, X()[:, t * D:(t + 1) * D],
                                               smv(64), smv(80), OP.mult,
                                               OP.add),
                r=(f"x{t}", "lnA", "lnB"), w=(TYT,))
            vop(lambda E, t=t: E.tensor_reduce(smv(0 + t), ytf, AX, OP.max,
                                               apply_absolute_value=True),
                r=(TYT,), w=("m1c",))
            vop(lambda E, t=t: E.tensor_scalar(smv(0 + t), smv(0 + t), FFN_EPS,
                                               None, OP.max),
                r=("m1c",), w=("m1c",))
            vop(lambda E, t=t: E.reciprocal(smv(16 + t), smv(0 + t)),
                r=("m1c",), w=("s1c",))
            vop(lambda E, t=t: E.tensor_scalar(smv(16 + t), smv(16 + t), 127.0,
                                               None, OP.mult),
                r=("s1c",), w=("s1c",))
            vop(lambda E, t=t: E.tensor_scalar(ytf, ytf, smv(16 + t), MAGIC,
                                               OP.mult, OP.add),
                r=(TYT, "s1c"), w=(TYT,))
            vop(lambda E: E.tensor_scalar(atm, ytf, -MAGIC, None, OP.add),
                r=(TYT,), w=(TATM,))
            tm_to_fm(atm, (TATM,), a1f, lambda c, t=t: c * T + t * 128, TA1)
        top(lambda E: E.transpose(pb[6][0:8, 0:128], smal[:, 0:8], C["idf"]),
            r=("m1c", "idf"), w=(PT[6],))
        vop(lambda E: E.tensor_copy(m1t[0:8, :], pb[6][0:8, 0:128]),
            r=(PT[6],), w=("m1tt",))
        dma("sync", brow[0:1, :], m1t[0:8, :], r=("m1tt",), w=("browt",))
        vop(lambda E: E.tensor_scalar(su127[0:1, :], C["onesf"][0:1, :],
                                      sval[0:1, 0:1], 1.0 / 127.0,
                                      OP.mult, OP.mult),
            r=("sval", "onesf"), w=("su127t",))
        for g8 in range(NT):
            bk = pb[6 if g8 % 2 == 0 else 7]
            top(lambda E, g8=g8, bk=bk: E.matmul(
                bk[:, 0:128], su127[0:1, :], brow[0:1, g8 * 128:(g8 + 1) * 128],
                start=True, stop=True),
                r=("su127t", "browt"), w=(PT[6 + g8 % 2],))
            vop(lambda E, g8=g8, bk=bk: E.tensor_copy(
                c1b[:, g8 * 128:(g8 + 1) * 128], bk[:, 0:128]),
                r=(PT[6 + g8 % 2],), w=("c1b" + sfx,))
        vop(lambda E: E.memset(m2p, 0.0), w=("m2p" + sfx,))
        for mb in range(32):
            dma("sync", GB[mb % 2], agu_out[mb], r=("agu",),
                w=(f"gb{mb % 2}" + sfx,))
            hs = HS[mb % 2]
            ht = f"hst{mb % 2}" + sfx
            for half in range(2):
                bk = pb[half]
                for kc in range(KC):
                    top(lambda E, mb=mb, kc=kc, half=half, bk=bk: E.matmul(
                        bk[:, 0:512], GB[mb % 2][:, kc * 128:(kc + 1) * 128],
                        a1f[:, kc * T + half * 512: kc * T + half * 512 + 512],
                        start=(kc == 0), stop=(kc == KC - 1)),
                        r=(f"gb{mb % 2}" + sfx, TA1), w=(PT[half],))
                vop(lambda E, half=half, hs=hs, bk=bk: E.tensor_tensor(
                    hs[:, half * 512:half * 512 + 512], bk[:, 0:512],
                    c1b[:, half * 512:half * 512 + 512], OP.mult),
                    r=(PT[half], "c1b" + sfx), w=(ht,))
                vop(lambda E, half=half, hs=hs: E.scalar_tensor_tensor(
                    hs[:, half * 512:half * 512 + 512],
                    hs[:, half * 512:half * 512 + 512], 0.0,
                    hs[:, half * 512:half * 512 + 512], OP.max, OP.mult),
                    r=(ht,), w=(ht,))
                vop(lambda E, half=half, hs=hs: E.tensor_tensor(
                    m2p[:, half * 512:half * 512 + 512],
                    m2p[:, half * 512:half * 512 + 512],
                    hs[:, half * 512:half * 512 + 512], OP.max),
                    r=(ht, "m2p" + sfx), w=("m2p" + sfx,))
            dma("sync", hsp_d[mb], hs, r=(ht,), w=("hsp",))
        for g8 in range(NT):
            top(lambda E, g8=g8: E.transpose(
                pb[6][:, 0:128], m2p[:, g8 * 128:(g8 + 1) * 128], C["idf"]),
                r=("m2p" + sfx, "idf"), w=(PT[6],))
            vop(lambda E, g8=g8: E.tensor_reduce(smv(32 + g8), pb[6][:, 0:128],
                                                 AX, OP.max),
                r=(PT[6],), w=("m2c",))
        vop(lambda E: E.tensor_scalar(smal[:, 32:40], smal[:, 32:40], FFN_EPS,
                                      None, OP.max), r=("m2c",), w=("m2c",))
        vop(lambda E: E.reciprocal(smal[:, 40:48], smal[:, 32:40]),
            r=("m2c",), w=("s2c",))
        vop(lambda E: E.tensor_scalar(smal[:, 40:48], smal[:, 40:48], 127.0,
                                      None, OP.mult), r=("s2c",), w=("s2c",))
        vop(lambda E: E.tensor_scalar(smal[:, 48:56], smal[:, 32:40],
                                      svrep[:, 1:2], 1.0 / 127.0,
                                      OP.mult, OP.mult),
            r=("m2c", "svrep"), w=("c3c",))
        top(lambda E: E.transpose(pb[6][0:8, 0:128], smal[:, 40:48], C["idf"]),
            r=("s2c", "idf"), w=(PT[6],))
        vop(lambda E: E.tensor_copy(m1t[0:8, :], pb[6][0:8, 0:128]),
            r=(PT[6],), w=("m1tt",))
        dma("sync", brow[0:1, :], m1t[0:8, :], r=("m1tt",), w=("browt",))
        for g8 in range(NT):
            bk = pb[6 if g8 % 2 == 0 else 7]
            top(lambda E, g8=g8, bk=bk: E.matmul(
                bk[:, 0:128], C["onesf"][0:1, :],
                brow[0:1, g8 * 128:(g8 + 1) * 128], start=True, stop=True),
                r=("onesf", "browt"), w=(PT[6 + g8 % 2],))
            vop(lambda E, g8=g8, bk=bk: E.tensor_copy(
                s2b[:, g8 * 128:(g8 + 1) * 128], bk[:, 0:128]),
                r=(PT[6 + g8 % 2],), w=("s2b" + sfx,))
        A.freebuf("a1f" + sfx, "ytf" + sfx, "atm" + sfx, "c1b" + sfx,
                  "m2p" + sfx)
        a2fA = A.alloc("a2fA" + sfx, 128, 16 * T, BF16)
        a2fB = A.alloc("a2fB" + sfx, 128, 16 * T, BF16)
        def a2tok(k):
            return ("a2fA" + sfx) if k < 16 else ("a2fB" + sfx)

        def a2sl(k, lo, n):
            buf = a2fA if k < 16 else a2fB
            kk = k % 16
            return buf[:, kk * T + lo: kk * T + lo + n]
        for mb in range(32):
            hs = HS[mb % 2]
            ht = f"hst{mb % 2}" + sfx
            dma("sync", hs, hsp_d[mb], r=("hsp",), w=(ht,))
            vop(lambda E, hs=hs: E.tensor_tensor(hs, hs, s2b, OP.mult),
                r=(ht, "s2b" + sfx), w=(ht,))
            vop(lambda E, hs=hs: E.tensor_scalar(hs, hs, MAGIC, None, OP.add),
                r=(ht,), w=(ht,))
            vop(lambda E, mb=mb, hs=hs: E.tensor_scalar(
                a2sl(mb, 0, T), hs, -MAGIC, None, OP.add),
                r=(ht,), w=(a2tok(mb),))
        A.freebuf("hst0" + sfx, "hst1" + sfx, "s2b" + sfx, "gb0" + sfx,
                  "gb1" + sfx)
        gd0 = A.alloc("gd0" + sfx, 128, 8 * 512, BF16)
        gd1 = A.alloc("gd1" + sfx, 128, 8 * 512, BF16)
        GD = [gd0, gd1]
        for nb in range(4):
            for kg in range(4):
                sb_i = nb * 4 + kg
                dma("sync", GD[sb_i % 2], agd_out[sb_i], r=("agd",),
                    w=(f"gd{sb_i % 2}" + sfx,))
                for t in range(NT):
                    for kk in range(8):
                        k = 8 * kg + kk
                        top(lambda E, t=t, k=k, kk=kk, sb_i=sb_i, kg=kg:
                            E.matmul(
                            pb[t][:, 0:512],
                            a2sl(k, t * 128, 128),
                            GD[sb_i % 2][:, kk * 512:(kk + 1) * 512],
                            start=(kg == 0 and kk == 0),
                            stop=(kg == 3 and kk == 7)),
                            r=(f"gd{sb_i % 2}" + sfx, a2tok(k)), w=(PT[t],))
            for t in range(NT):
                vop(lambda E, nb=nb, t=t: E.scalar_tensor_tensor(
                    X()[:, t * D + nb * 512: t * D + nb * 512 + 512],
                    pb[t][:, 0:512], smal[:, 48 + t:49 + t],
                    X()[:, t * D + nb * 512: t * D + nb * 512 + 512],
                    OP.mult, OP.add),
                    r=(PT[t], "c3c", f"x{t}"), w=(f"x{t}",))
        A.freebuf("a2fA" + sfx, "a2fB" + sfx, "gd0" + sfx, "gd1" + sfx)

    def build_attn_b():
        CBv = {k: A.alloc(k, sh[0], sh[1], F32) for k, sh in _CB.items()}
        for k in _CB:
            dma("sync", CBv[k], cdr[k], w=(k,))
        nwrow_s = A.alloc("nwrow_s", 1, D, F32)
        nwcol = A.alloc("nwcol", 128, 16, F32)
        dma("sync", nwrow_s, nw_d, w=("nwrow_s",))
        for m in range(KC):
            top(lambda E, m=m: E.transpose(
                pb[6][:, 0:1], nwrow_s[0:1, m * 128:(m + 1) * 128],
                C["idf"][0:1, 0:1]),
                r=("nwrow_s", "idf"), w=(PT[6],))
            vop(lambda E, m=m: E.tensor_copy(nwcol[:, m:m + 1], pb[6][:, 0:1]),
                r=(PT[6],), w=("nwcol",))
        A.freebuf("nwrow_s")
        y3f = A.alloc("y3f", 128, KC * T, BF16)
        ytmp = A.alloc("bytmp", 128, D, BF16)
        tap("xb0", X()[:, 0:D], ("x0",))
        for t in range(NT):
            ln_stats(X()[:, t * D:(t + 1) * D], (f"x{t}",))
            vop(lambda E, t=t: E.tensor_scalar(ytmp, X()[:, t * D:(t + 1) * D],
                                               smv(64), smv(80), OP.mult,
                                               OP.add),
                r=(f"x{t}", "lnA", "lnB"), w=("bytmp",))
            if t == 0:
                tap("ab0", smal[:, 60:90], ("lnB",))
                tap("ytmp0", ytmp, ("bytmp",))
                tap("lnsc0", lnsc, ("lnsc",))
            tm_to_fm(ytmp, ("bytmp",), y3f, lambda c, t=t: c * T + t * 128,
                     "y3f")
        tap("y3f", y3f, ("y3f",))
        for t in range(NT):
            dma("sync", xsp_d[t], X()[:, t * D:(t + 1) * D],
                r=(f"x{t}",), w=("xspill",))
        A.freebuf("X", "bytmp")
        qf = A.alloc("qbf", 128, BH * T, BF16)
        kf = A.alloc("kbf", 128, BH * T, BF16)
        g1f = A.alloc("g1f", 128, T, BF16)
        silf = A.alloc("bsil", 128, 512, F32)
        rt1 = A.alloc("brt1", 128, 512, F32)
        rt2 = A.alloc("brt2", 128, 512, F32)
        qtms = A.alloc("qtms", 128, 512, BF16)
        wrhs = A.alloc("bwrhs", 128, KC * 512, BF16)
        wblk = A.alloc("bwblk", 128, D, BF16)

        def proj_rot4(wkey, dst, dtok):
            for nb in range(4):
                dma("gpsimd", wrhs, wdr[wkey][nb], w=("bwrhs",))
                for t in range(NT):
                    bk = pb[4 + t % 2]
                    for kc in range(KC):
                        top(lambda E, t=t, kc=kc, bk=bk: E.matmul(
                            bk[:, 0:512],
                            y3f[:, kc * T + t * 128: kc * T + t * 128 + 128],
                            wrhs[:, kc * 512:(kc + 1) * 512],
                            start=(kc == 0), stop=(kc == KC - 1)),
                            r=("bwrhs", "y3f"), w=(PT[4 + t % 2],))
                    sop(lambda E, bk=bk: E.activation(silf, bk[:, 0:512],
                                                      ACT.Silu),
                        r=(PT[4 + t % 2],), w=("bsil",))
                    csl = CBv["cs_tm"][:, t * 128:(t + 1) * 128]
                    for hb in range(4):
                        o = hb * 128
                        vop(lambda E, o=o, csl=csl: E.tensor_tensor(
                            rt1[:, 0:64], silf[:, o:o + 64], csl[:, 0:64],
                            OP.mult), r=("bsil", "cs_tm"), w=("brt1",))
                        vop(lambda E, o=o, csl=csl: E.tensor_tensor(
                            rt2[:, 0:64], silf[:, o + 64:o + 128],
                            csl[:, 64:128], OP.mult),
                            r=("bsil", "cs_tm"), w=("brt2",))
                        vop(lambda E, o=o: E.tensor_tensor(
                            qtms[:, o:o + 64], rt1[:, 0:64], rt2[:, 0:64],
                            OP.subtract), r=("brt1", "brt2"), w=("qtms",))
                        vop(lambda E, o=o, csl=csl: E.tensor_tensor(
                            rt1[:, 0:64], silf[:, o:o + 64], csl[:, 64:128],
                            OP.mult), r=("bsil", "cs_tm"), w=("brt1",))
                        vop(lambda E, o=o, csl=csl: E.tensor_tensor(
                            rt2[:, 0:64], silf[:, o + 64:o + 128], csl[:, 0:64],
                            OP.mult), r=("bsil", "cs_tm"), w=("brt2",))
                        vop(lambda E, o=o: E.tensor_tensor(
                            qtms[:, o + 64:o + 128], rt1[:, 0:64], rt2[:, 0:64],
                            OP.add), r=("brt1", "brt2"), w=("qtms",))
                    bank = pb[7]
                    for hb in range(4):
                        top(lambda E, hb=hb, bank=bank: E.transpose(
                            bank.bitcast(BF16)[:, hb * 128:(hb + 1) * 128],
                            qtms[:, hb * 128:(hb + 1) * 128], C["idb"]),
                            r=("qtms", "idb"), w=(PT[7],))
                    for hb in range(4):
                        vop(lambda E, hb=hb, nb=nb, t=t, bank=bank, dst=dst:
                            E.tensor_copy(
                            dst[:, (nb * 4 + hb) * T + t * 128:
                                (nb * 4 + hb) * T + t * 128 + 128],
                            bank.bitcast(BF16)[:, hb * 128:(hb + 1) * 128]),
                            r=(PT[7],), w=(dtok,))

        proj_rot4("wb_q", qf, "qbf")
        proj_rot4("wb_k", kf, "kbf")
        A.freebuf("brt2", "qtms")
        vtm = A.alloc("vtm", 128, NT * D, BF16)
        for nb in range(4):
            dma("gpsimd", wrhs, wdr["wb_v"][nb], w=("bwrhs",))
            for t in range(NT):
                bk = pb[4 + t % 2]
                for kc in range(KC):
                    top(lambda E, t=t, kc=kc, bk=bk: E.matmul(
                        bk[:, 0:512],
                        y3f[:, kc * T + t * 128: kc * T + t * 128 + 128],
                        wrhs[:, kc * 512:(kc + 1) * 512],
                        start=(kc == 0), stop=(kc == KC - 1)),
                        r=("bwrhs", "y3f"), w=(PT[4 + t % 2],))
                sop(lambda E, nb=nb, t=t, bk=bk: E.copy(
                    vtm[:, t * D + nb * 512: t * D + nb * 512 + 512],
                    bk[:, 0:512]), r=(PT[4 + t % 2],), w=("vtm",))
        dma("gpsimd", wblk, wdr["wb_g1"][0], w=("bwblk",))
        for half in range(2):
            bk = pb[4 + half]
            for kc in range(KC):
                top(lambda E, kc=kc, half=half, bk=bk: E.matmul(
                    bk[:, 0:512], wblk[:, kc * 128:(kc + 1) * 128],
                    y3f[:, kc * T + half * 512: kc * T + half * 512 + 512],
                    start=(kc == 0), stop=(kc == KC - 1)),
                    r=("bwblk", "y3f"), w=(PT[4 + half],))
            vop(lambda E, half=half, bk=bk: E.tensor_copy(
                g1f[:, half * 512:half * 512 + 512], bk[:, 0:512]),
                r=(PT[4 + half],), w=("g1f",))
        wg2s = A.alloc("wg2s", 128, D, BF16)
        dma("gpsimd", wg2s, wdr["wb_g2"], w=("wg2s",))
        tap("qbf", qf, ("qbf",))
        tap("kbf", kf, ("kbf",))
        tap("vtm", vtm, ("vtm",))
        tap("g1f", g1f, ("g1f",))
        A.freebuf("y3f", "bwrhs")
        Sf = A.alloc("Sf", 128, BH * 128, F32)
        Sall = A.alloc("Sall", 128, NT * BH * 128, BF16)
        khat = A.alloc("khat", 128, 128, BF16)
        lam = _alibi_lam()
        vop(lambda E: E.memset(Sf, 0.0), w=("Sf",))
        for c in range(NT):
            vop(lambda E, c=c: E.tensor_copy(
                Sall[:, c * BH * 128:(c + 1) * BH * 128], Sf),
                r=("Sf",), w=("Sall",))
            for h in range(BH):
                top(lambda E, c=c, h=h: E.transpose(
                    pb[7].bitcast(BF16)[:, 0:128],
                    kf[:, h * T + c * 128: h * T + c * 128 + 128], C["idb"]),
                    r=("kbf", "idb"), w=(PT[7],))
                vop(lambda E, h=h: E.tensor_scalar(
                    khat, pb[7].bitcast(BF16)[:, 0:128],
                    CBv["dkcol"][:, h:h + 1], None, OP.mult),
                    r=(PT[7], "dkcol"), w=("khat",))
                top(lambda E, c=c, h=h: E.matmul(
                    pb[6][:, 0:128], khat,
                    vtm[:, c * D + h * 128: c * D + h * 128 + 128],
                    start=True, stop=True), r=("khat", "vtm"), w=(PT[6],))
                lamC = math.exp(lam[h] * 128.0)
                vop(lambda E, h=h, lamC=lamC: E.scalar_tensor_tensor(
                    Sf[:, h * 128:(h + 1) * 128], Sf[:, h * 128:(h + 1) * 128],
                    lamC, pb[6][:, 0:128], OP.mult, OP.add),
                    r=(PT[6], "Sf"), w=("Sf",))
        tap("Sf8", Sf, ("Sf",))
        dma("sync", ccs_in.rearrange("(h d) e -> d h e", h=BH),
            Sf.rearrange("p (h e) -> p h e", h=BH), r=("Sf",), w=("ccsin",))
        s.add("gpsimd", lambda E: E.collective_compute(
            "AllGather", OP.bypass,
            replica_groups=[[0, 1], [2, 3], [4, 5], [6, 7]],
            ins=[ccs_in.opt()], outs=[ccs_out.opt()]),
            r=("ccsin",), w=("ccso",), ovr=True)
        srecv = A.alloc("srecv", 128, BH * 128, BF16)
        dma("gpsimd", srecv.rearrange("p (h e) -> p h e", h=BH),
            ccs_out[0:BH * 128].rearrange("(h d) e -> d h e", h=BH),
            r=("ccso",), w=("srecv",))
        tap("srecv", srecv, ("srecv",))
        tap("Sall", Sall, ("Sall",))
        A.freebuf("Sf")
        otm = A.alloc("otm", 128, D, F32)
        pbb = A.alloc("pbb", 128, 128, BF16)
        qa = A.alloc("qa", 128, 128, BF16)
        srs = A.alloc("srs", 128, 128, BF16)
        xgb = A.alloc("xgb", 128, D, F32)
        xnb = A.alloc("xnb", 128, D, BF16)
        msb = A.alloc("msb", 128, 64, F32)
        for c in range(NT):
            for h in range(BH):
                sbk = pb[4 + h % 2]
                top(lambda E, c=c, h=h, sbk=sbk: E.matmul(
                    sbk[:, 0:128], kf[:, h * T + c * 128: h * T + c * 128 + 128],
                    qf[:, h * T + c * 128: h * T + c * 128 + 128],
                    start=True, stop=True), r=("kbf", "qbf"),
                    w=(PT[4 + h % 2],))
                vop(lambda E, h=h, sbk=sbk: E.tensor_tensor(
                    pbb, sbk[:, 0:128],
                    CBv["dt_tab"][:, h * 128:(h + 1) * 128], OP.mult),
                    r=(PT[4 + h % 2], "dt_tab"), w=("pbb",))
                vop(lambda E, c=c, h=h: E.tensor_tensor(
                    qa, qf[:, h * T + c * 128: h * T + c * 128 + 128],
                    CBv["ea_rep"][:, h * 128:(h + 1) * 128], OP.mult),
                    r=("qbf", "ea_rep"), w=("qa",))
                vop(lambda E, c=c, h=h: E.tensor_scalar(
                    srs, srecv[:, h * 128:(h + 1) * 128],
                    CBv["ebcol"][:, h * NT + c: h * NT + c + 1], None, OP.mult),
                    r=("srecv", "ebcol"), w=("srs",))
                ob = pb[h % 4]
                top(lambda E, c=c, h=h, ob=ob: E.matmul(
                    ob[:, 0:128], pbb,
                    vtm[:, c * D + h * 128: c * D + h * 128 + 128],
                    start=True, stop=False), r=("pbb", "vtm"), w=(PT[h % 4],))
                top(lambda E, c=c, h=h, ob=ob: E.matmul(
                    ob[:, 0:128], qa,
                    Sall[:, (c * BH + h) * 128: (c * BH + h) * 128 + 128],
                    start=False, stop=False), r=("qa", "Sall"), w=(PT[h % 4],))
                top(lambda E, ob=ob: E.matmul(
                    ob[:, 0:128], qa, srs, start=False, stop=True),
                    r=("qa", "srs"), w=(PT[h % 4],))
                vop(lambda E, h=h, ob=ob: E.tensor_copy(
                    otm[:, h * 128:(h + 1) * 128], ob[:, 0:128]),
                    r=(PT[h % 4],), w=("otm",))
            for nb in range(4):
                gk = pb[4 + nb % 2]
                top(lambda E, c=c, nb=nb, gk=gk: E.matmul(
                    gk[:, 0:512], g1f[:, c * 128:(c + 1) * 128],
                    wg2s[:, nb * 512:(nb + 1) * 512], start=True, stop=True),
                    r=("g1f", "wg2s"), w=(PT[4 + nb % 2],))
                sop(lambda E, gk=gk: E.activation(silf, gk[:, 0:512], ACT.Exp,
                                                  scale=-1.0),
                    r=(PT[4 + nb % 2],), w=("bsil",))
                vop(lambda E: E.tensor_scalar(silf, silf, 1.0, None, OP.add),
                    r=("bsil",), w=("bsil",))
                vop(lambda E: E.reciprocal(rt1, silf), r=("bsil",), w=("brt1",))
                vop(lambda E, nb=nb: E.tensor_tensor(
                    xgb[:, nb * 512:(nb + 1) * 512],
                    otm[:, nb * 512:(nb + 1) * 512], rt1, OP.mult),
                    r=("otm", "brt1"), w=("xgb",))
            for h in range(BH):
                vop(lambda E, h=h: E.tensor_tensor(
                    rt1[:, 0:128], xgb[:, h * 128:(h + 1) * 128],
                    xgb[:, h * 128:(h + 1) * 128], OP.mult),
                    r=("xgb",), w=("brt1",))
                vop(lambda E, h=h: E.tensor_reduce(
                    msb[:, h:h + 1], rt1[:, 0:128], AX, OP.add),
                    r=("brt1",), w=("msb",))
            vop(lambda E: E.tensor_scalar(msb[:, 0:16], msb[:, 0:16],
                                          1.0 / 128.0, B_EPS, OP.mult, OP.add),
                r=("msb",), w=("msb",))
            sop(lambda E: E.activation(msb[:, 16:32], msb[:, 0:16], ACT.Ln),
                r=("msb",), w=("msb2",))
            sop(lambda E: E.activation(msb[:, 16:32], msb[:, 16:32], ACT.Exp,
                                       scale=-0.5), r=("msb2",), w=("msb2",))
            for h in range(BH):
                vop(lambda E, h=h: E.tensor_scalar(
                    xnb[:, h * 128:(h + 1) * 128],
                    xgb[:, h * 128:(h + 1) * 128],
                    msb[:, 16 + h:17 + h], None, OP.mult),
                    r=("xgb", "msb2"), w=("xnb",))
            dma("sync", xg_d[c], xnb, r=("xnb",), w=("xgsp",))
        A.freebuf("qbf", "kbf", "vtm", "g1f", "bsil", "brt1",
                  "Sall", "khat", "srecv", "otm", "pbb", "qa", "srs", "xgb",
                  "xnb", "msb", "bwblk", "wg2s")
        tap("xgd", xg_d, ("xgsp",))
        Xh["v"] = A.alloc("X", 128, NT * D, F32, toks=XTOKS)
        zf = A.alloc("zfb", 128, KC * T, BF16)
        xnr0 = A.alloc("xnr0", 128, D, BF16)
        xnr1 = A.alloc("xnr1", 128, D, BF16)
        wo0 = A.alloc("bwo0", 128, KC * 512, BF16)
        wo1 = A.alloc("bwo1", 128, KC * 512, BF16)
        XN = [xnr0, xnr1]
        WO = [wo0, wo1]
        for t in range(NT):
            dma("sync", X()[:, t * D:(t + 1) * D], xsp_d[t],
                r=("xspill",), w=(f"x{t}",))
        for c in range(NT):
            xb = XN[c % 2]
            xt = f"xnr{c % 2}"
            dma("sync", xb, xg_d[c], r=("xgsp",), w=(xt,))
            tm_to_fm(xb, (xt,), zf, lambda ch, c=c: ch * T + c * 128,
                     "zfb", 7, nwc=nwcol)
        tap("zfb", zf, ("zfb",))
        for nb in range(4):
            dma("gpsimd", WO[nb % 2], wdr["wb_o"][nb], w=(f"bwo{nb % 2}",))
            for t in range(NT):
                bk = pb[4 + t % 2]
                for kc in range(KC):
                    top(lambda E, nb=nb, t=t, kc=kc, bk=bk: E.matmul(
                        bk[:, 0:512],
                        zf[:, kc * T + t * 128: kc * T + t * 128 + 128],
                        WO[nb % 2][:, kc * 512:(kc + 1) * 512],
                        start=(kc == 0), stop=(kc == KC - 1)),
                        r=(f"bwo{nb % 2}", "zfb"), w=(PT[4 + t % 2],))
                vop(lambda E, nb=nb, t=t, bk=bk: E.tensor_tensor(
                    X()[:, t * D + nb * 512: t * D + nb * 512 + 512],
                    X()[:, t * D + nb * 512: t * D + nb * 512 + 512],
                    bk[:, 0:512], OP.add),
                    r=(PT[4 + t % 2], f"x{t}"), w=(f"x{t}",))
        A.freebuf("zfb", "xnr0", "xnr1", "bwo0", "bwo1", "nwcol")
        for k in _CB:
            A.freebuf(k)

    if n_sub >= 1:
        build_attn_a()
    if n_sub >= 2:
        build_ffn(0)
    if n_sub >= 3:
        build_attn_b()
    if n_sub >= 4:
        build_ffn(1)

    for t in range(NT):
        dma("sync", out_d[t], X()[:, t * D:(t + 1) * D], r=(f"x{t}",),
            w=("outd",))
    vop(lambda E: E.memset(smv(255), 0.0), r=("outd",))

    stats = s.finalize(nc, sems, ovr_sems)
    stats["sbuf_peak"] = A.peak
    stk.close()
    _NC_CACHE[n_sub] = (nc, stats)
    return nc, stats


def _host_weights(inputs):
    def arr(k):
        return np.ascontiguousarray(np.asarray(inputs[k], dtype=np.float32))
    w = {}
    w["wa_q"] = _prep_lhsT(arr("L0_attn_a_Wq").T, 128)
    w["wa_k"] = _prep_lhsT(arr("L0_attn_a_Wk").T, 128)
    w["wa_v"] = _prep_lhsT(arr("L0_attn_a_Wv").T, 512)
    w["wa_o"] = _prep_lhsT(arr("L0_attn_a_Wo").T, 512)
    w["wb_q"] = _prep_lhsT(arr("L0_attn_b_Wq"), 512)
    w["wb_k"] = _prep_lhsT(arr("L0_attn_b_Wk"), 512)
    w["wb_v"] = _prep_lhsT(arr("L0_attn_b_Wv"), 512)
    w["wb_o"] = _prep_lhsT(arr("L0_attn_b_Wo"), 512)
    w["wb_g1"] = _prep_lhsT(arr("L0_attn_b_Wg1"), 128)
    w["wb_g2"] = arr("L0_attn_b_Wg2")
    wu = _prep_lhsT(arr("L0_ffn_w_up").T, 128)
    wdT = arr("L0_ffn_w_down").T
    a = wdT.reshape(32, 128, 4, 512)
    sub = np.empty((4, 4, 128, 8 * 512), np.float32)
    for nb in range(4):
        for kg in range(4):
            for kk in range(8):
                sub[nb, kg, :, kk * 512:(kk + 1) * 512] = a[8 * kg + kk, :, nb, :]
    wd = np.ascontiguousarray(sub.reshape(16, 128, I_FFN))
    return w, wu, wd


def kernel(**inputs):
    nc, stats = build_nc(4)
    x = np.ascontiguousarray(np.asarray(inputs["L0_attn_a_hidden_states"]),
                             dtype=np.float32)
    cu = np.asarray(inputs["L0_attn_a_cu_seqlens_q"],
                    dtype=np.int32).reshape(1, 3)
    w, wu, wd = _host_weights(inputs)
    nw = np.ascontiguousarray(np.asarray(inputs["L0_attn_b_norm_weight"]),
                              dtype=np.float32).reshape(1, D)
    in_maps = []
    for core in range(8):
        b, hh = core // 2, core % 2
        m = dict(w)
        m["cu"] = cu
        m["nwrow"] = nw
        m["xo"] = np.ascontiguousarray(
            x[b, hh * 1024:(hh + 1) * 1024].reshape(NT, 128, D))
        m["xp"] = np.ascontiguousarray(
            x[b, (1 - hh) * 1024:(2 - hh) * 1024].reshape(NT, 128, D))
        m["wua"] = np.ascontiguousarray(wu[4 * core:4 * core + 4])
        m["wda"] = np.ascontiguousarray(wd[2 * core:2 * core + 2])
        m.update(host_constants(core))
        in_maps.append(m)
    res = run_bass_kernel_spmd(nc, in_maps, core_ids=list(range(8)))
    out = np.empty((4, 2048, D), np.float32)
    for core in range(8):
        b, hh = core // 2, core % 2
        out[b, hh * 1024:(hh + 1) * 1024] = \
            res.results[core]["out"].reshape(1024, D)
    return out


# version 02
# speedup vs baseline: 1.0796x; 1.0796x over previous
import contextlib
import math
import numpy as np
import ml_dtypes
import concourse.bass as bass
import concourse.mybir as mybir
from concourse.bass_utils import run_bass_kernel_spmd

F32 = mybir.dt.float32
BF16 = mybir.dt.bfloat16
I32 = mybir.dt.int32
U8 = mybir.dt.uint8
F32R = mybir.dt.float32r
OP = mybir.AluOpType
ACT = mybir.ActivationFunctionType
AX = mybir.AxisListType.X
BF = ml_dtypes.bfloat16

D = 2048
KC = 16
T = 1024
NT = 8
MAGIC = 1.5 * 2.0 ** 23
BH, BHD = 16, 128
B_BASE = 10000.0
B_EPS = 1e-6
LN_EPS = 1e-5
FFN_EPS = 1e-5
I_FFN = 4096
W_CNT = float(I_FFN * D)
_ESZ = {F32: 4, BF16: 2, F32R: 4, I32: 4, U8: 1}


class Sched:
    ENG = ("tensor", "vector", "scalar", "sync", "gpsimd")

    def __init__(self):
        self.recs = []
        self.lastw = {}
        self.readers = {}
        self.n_ovr = 0
        self.alias_pending = {}

    def alias(self, tok, olds):
        self.alias_pending.setdefault(tok, [])
        self.alias_pending[tok].extend(olds)

    def add(self, eng, emit, r=(), w=(), dma=False, ovr=False):
        w = list(w)
        for t in list(w):
            if t in self.alias_pending:
                w.extend(self.alias_pending.pop(t))
        idx = len(self.recs)
        deps = set()
        for t in r:
            j = self.lastw.get(t)
            if j is not None:
                deps.add(j)
        for t in w:
            j = self.lastw.get(t)
            if j is not None:
                deps.add(j)
            for j in self.readers.get(t, ()):
                deps.add(j)
        ovr_id = None
        if ovr:
            ovr_id = self.n_ovr
            self.n_ovr += 1
        self.recs.append({"idx": idx, "eng": eng, "emit": emit, "deps": deps,
                          "dma": dma, "inc": 16 if dma else 1, "ovr": ovr_id})
        for t in r:
            self.readers.setdefault(t, []).append(idx)
        for t in w:
            self.lastw[t] = idx
            self.readers[t] = []
        return idx

    def finalize(self, nc, sems, ovr_sems):
        recs = self.recs
        for rc in recs:
            rc["xdeps"] = [d for d in rc["deps"]
                           if recs[d]["eng"] != rc["eng"]
                           or recs[d]["dma"] or recs[d]["ovr"] is not None
                           or rc["eng"] in ("vector", "scalar")]
        need = set()
        for rc in recs:
            if rc["dma"]:
                need.add(rc["idx"])
            for d in rc["xdeps"]:
                need.add(d)
        cum = {}
        for e in self.ENG:
            c = 0
            for rc in recs:
                if rc["eng"] != e or rc["ovr"] is not None:
                    continue
                if rc["idx"] in need:
                    c += rc["inc"]
                cum[rc["idx"]] = c
        per = {e: [rc for rc in recs if rc["eng"] == e] for e in self.ENG}

        def runner(en):
            def body(eng):
                hw = {e: 0 for e in self.ENG}
                ohw = set()
                for rc in per[en]:
                    for d in sorted(rc["xdeps"]):
                        dr = recs[d]
                        if dr["ovr"] is not None:
                            if dr["ovr"] not in ohw:
                                eng.wait_ge(ovr_sems[dr["ovr"]], 1)
                                ohw.add(dr["ovr"])
                        else:
                            v = cum[d]
                            if v > hw[dr["eng"]]:
                                eng.wait_ge(sems[dr["eng"]], v)
                                hw[dr["eng"]] = v
                    inst = rc["emit"](eng)
                    if rc["ovr"] is not None:
                        inst.then_inc(ovr_sems[rc["ovr"]], 1)
                    elif rc["idx"] in need:
                        inst.then_inc(sems[en], rc["inc"])
            return body

        g = nc.Block()
        blk = g.__enter__()
        blk.tensor(runner("tensor"))
        blk.vector(runner("vector"))
        blk.scalar(runner("scalar"))
        blk.sync(runner("sync"))
        blk.gpsimd(runner("gpsimd"))
        g.__exit__(None, None, None)
        return {"n_ops": len(recs), "n_incs": len(need),
                "n_waits": sum(len(rc["xdeps"]) for rc in recs)}


class Arena:
    def __init__(self, ap, nbytes, sched):
        self.ap = ap
        self.nbytes = nbytes
        self.free = [(0, nbytes)]
        self.live = {}
        self.owners = []
        self.s = sched
        self.peak = 0

    def alloc(self, name, parts, freedim, dt, align=64, toks=None):
        toks = tuple(toks) if toks else (name,)
        nb = freedim * _ESZ[dt]
        nba = (nb + align - 1) // align * align
        for i, (off, sz) in enumerate(self.free):
            o = (off + align - 1) // align * align
            if o + nba <= off + sz:
                repl = [p for p in ((off, o - off), (o + nba, off + sz - o - nba))
                        if p[1] > 0]
                self.free[i:i + 1] = repl
                self.live[name] = (o, nba, toks)
                used = sum(x[1] for x in self.live.values())
                self.peak = max(self.peak, used)
                olds = []
                for (fo, fs, tks) in self.owners:
                    if fo < o + nba and o < fo + fs:
                        olds.extend(tks)
                if olds:
                    self.owners = [x for x in self.owners
                                   if not (x[0] < o + nba and o < x[0] + x[1])]
                    for tk in toks:
                        self.s.alias(tk, olds)
                return self.ap[0:parts, o:o + nb].bitcast(dt)
        raise MemoryError(f"arena: no room for {name} ({nba}B) free={self.free}")

    def freebuf(self, *names):
        for name in names:
            off, sz, toks = self.live.pop(name)
            self.free.append((off, sz))
            self.owners.append((off, sz, toks))
        self.free.sort()
        m = []
        for off, sz in self.free:
            if m and m[-1][0] + m[-1][1] == off:
                m[-1] = (m[-1][0], m[-1][1] + sz)
            else:
                m.append([off, sz])
        self.free = [tuple(x) for x in m]


def _prep_lhsT(w_io, out_block=128):
    In, Out = w_io.shape
    kc = In // 128
    nb = Out // out_block
    a = w_io.reshape(kc, 128, nb, out_block).transpose(2, 1, 0, 3)
    return np.ascontiguousarray(a.reshape(nb, 128, kc * out_block),
                                dtype=np.float32)


def _alibi_lam():
    return -(2.0 ** (-8.0 * (np.arange(BH, dtype=np.float64) + 1.0) / BH))


def host_constants(core):
    hh = core % 2
    H = 1024 * hh
    c = {}
    c["qpos"] = (H + np.arange(T, dtype=np.float32)).reshape(1, T)
    kpos = (H - 1024 + np.arange(2048, dtype=np.float32)).reshape(1, 2048)
    c["kposr"] = kpos
    c["invalidr"] = (2.0 * (kpos < 0)).astype(np.float32)
    c["idb"] = np.eye(128, dtype=np.float32).astype(BF)
    c["idf"] = np.eye(128, dtype=np.float32)
    d = np.arange(128)
    c["diagm"] = (d[None, :] >= d[:, None]).astype(BF)
    c["onesb"] = np.ones((1, 128), BF)
    c["onesf"] = np.ones((1, 128), np.float32)
    lam = _alibi_lam()
    i = np.arange(128, dtype=np.float64)
    dt_tab = np.zeros((128, BH * 128), np.float32)
    ea = np.zeros((128, BH * 128), np.float32)
    eb = np.zeros((128, BH * NT), np.float32)
    dk = np.zeros((128, BH), np.float32)
    for h in range(BH):
        m = np.exp(lam[h] * (i[None, :] - i[:, None])) * (i[None, :] >= i[:, None])
        dt_tab[:, h * 128:(h + 1) * 128] = m.astype(np.float32)
        ea[:, h * 128:(h + 1) * 128] = \
            np.exp(lam[h] * (i + 1.0))[None, :].astype(np.float32)
        if hh == 1:
            for cc in range(NT):
                eb[:, h * NT + cc] = math.exp(lam[h] * 128.0 * cc)
        dk[:, h] = np.exp(lam[h] * (127.0 - i)).astype(np.float32)
    c["dt_tab"] = dt_tab
    c["ea_rep"] = ea
    c["ebcol"] = eb
    c["dkcol"] = dk
    j = np.arange(BHD // 2, dtype=np.float64)
    theta = B_BASE ** (-2.0 * j / BHD)
    cs = np.zeros((128, NT * 128), np.float32)
    for t in range(NT):
        pos = H + t * 128 + np.arange(128, dtype=np.float64)
        ang = pos[:, None] * theta[None, :]
        cs[:, t * 128:t * 128 + 64] = np.cos(ang).astype(np.float32)
        cs[:, t * 128 + 64:t * 128 + 128] = np.sin(ang).astype(np.float32)
    c["cs_tm"] = cs
    return c


_C0 = {"qpos": (1, T), "kposr": (1, 2048), "invalidr": (1, 2048),
       "idb": (128, 128), "idf": (128, 128), "diagm": (128, 128),
       "onesb": (1, 128), "onesf": (1, 128)}
_CB = {"dt_tab": (128, BH * 128), "ea_rep": (128, BH * 128),
       "ebcol": (128, BH * NT), "dkcol": (128, BH), "cs_tm": (128, NT * 128)}
_CDT = {"idb": BF16, "diagm": BF16, "onesb": BF16}

_NC_CACHE = {}
DBG_TAPS = []


def build_nc(n_sub=4):
    if n_sub in _NC_CACHE:
        return _NC_CACHE[n_sub]
    nc = bass.Bass("TRN2", target_bir_lowering=False)
    s = Sched()

    def din(name, shape, dt):
        return nc.dram_tensor(name, list(shape), dt, kind="ExternalInput").ap()

    xo_d = din("xo", (NT, 128, D), F32)
    xp_d = din("xp", (NT, 128, D), F32)
    cu_d = din("cu", (1, 3), I32)
    nw_d = din("nwrow", (1, D), F32)
    out_d = nc.dram_tensor("out", [NT, 128, D], F32, kind="ExternalOutput").ap()
    cdr = {k: din(k, sh, _CDT.get(k, F32)) for k, sh in {**_C0, **_CB}.items()}
    wdr = {
        "wa_q": din("wa_q", (16, 128, D), F32),
        "wa_k": din("wa_k", (4, 128, D), F32),
        "wa_v": din("wa_v", (1, 128, KC * 512), F32),
        "wa_o": din("wa_o", (4, 128, KC * 512), F32),
        "wb_q": din("wb_q", (4, 128, KC * 512), F32),
        "wb_k": din("wb_k", (4, 128, KC * 512), F32),
        "wb_v": din("wb_v", (4, 128, KC * 512), F32),
        "wb_o": din("wb_o", (4, 128, KC * 512), F32),
        "wb_g1": din("wb_g1", (1, 128, D), F32),
        "wb_g2": din("wb_g2", (128, D), F32),
        "wua": din("wua", (4, 128, D), F32),
        "wda": din("wda", (2, 128, I_FFN), F32),
    }
    hsp_d = nc.dram_tensor("hsp", [32, 128, T], F32).ap()
    xsp_d = nc.dram_tensor("xsp", [NT, 128, D], F32).ap()
    xg_d = nc.dram_tensor("xgsp", [NT, 128, D], BF16).ap()
    ar_in = nc.dram_tensor("ar_in", [1, 2], F32).ap()
    ar_out = nc.dram_tensor("ar_out", [1, 2], F32).ap()
    agu_in = nc.dram_tensor("agu_in", [4, 128, D], BF16).ap()
    agu_out = nc.dram_tensor("agu_out", [32, 128, D], BF16).ap()
    agd_in = nc.dram_tensor("agd_in", [2, 128, I_FFN], BF16).ap()
    agd_out = nc.dram_tensor("agd_out", [16, 128, I_FFN], BF16).ap()
    ccs_in = nc.dram_tensor("ccs_in", [BH * 128, 128], F32).ap()
    ccs_out = nc.dram_tensor("ccs_out", [2 * BH * 128, 128], F32).ap()

    total = (nc.sbuf_bytes_remaining - 512) // 64 * 64
    A = Arena(nc.alloc_sbuf_tensor("arena", [128, total], U8).ap(), total, s)
    pb = [nc.alloc_psum_tensor(f"pb{i}", [128, 512], F32).ap() for i in range(8)]
    PT = [f"ps{i}" for i in range(8)]

    stk = contextlib.ExitStack()
    sems = {e: stk.enter_context(nc.semaphore(f"sem_{e}")) for e in Sched.ENG}
    ovr_sems = [stk.enter_context(nc.semaphore(f"ovr{i}")) for i in range(4)]

    Xh = {}
    XTOKS = tuple(f"x{t}" for t in range(NT))
    Xh["v"] = A.alloc("X", 128, NT * D, F32, toks=XTOKS)
    C = {k: A.alloc(k, sh[0], sh[1], _CDT.get(k, F32)) for k, sh in _C0.items()}
    cu_s = A.alloc("cu", 1, 4, I32)
    smal = A.alloc("smal", 128, 256, F32)
    bn6 = A.alloc("bn6", 128, 24, F32)
    lnsc = A.alloc("lnsc", 128, 16, F32)
    sq_rep = A.alloc("sq_rep", 128, T, BF16)
    skcol = A.alloc("skcol", 128, 16, F32)
    sqrow_b = A.alloc("sqrow_b", 1, T, BF16)
    skprow = A.alloc("skprow", 1, 2048, F32)
    sval = A.alloc("sval", 1, 8, F32)
    svrep = A.alloc("svrep", 128, 4, F32)
    su127 = A.alloc("su127", 1, 128, F32)
    brow = A.alloc("brow", 1, 1024, F32)
    m1t = A.alloc("m1t", 8, 128, F32)
    wpart = A.alloc("wpart", 2, 1, F32)
    onesp = A.alloc("onesp", 128, 128, F32)

    def smv(off, n=1):
        return smal[:, off:off + n]

    def X():
        return Xh["v"]

    def dma(eng, out, in_, r=(), w=()):
        s.add(eng, lambda E, o=out, i=in_: E.dma_start(out=o, in_=i),
              r=r, w=w, dma=True)

    def vop(fn, r=(), w=()):
        s.add("vector", fn, r=r, w=w)

    def sop(fn, r=(), w=()):
        s.add("scalar", fn, r=r, w=w)

    def top(fn, r=(), w=()):
        s.add("tensor", fn, r=r, w=w)

    def tap(name, ap, rtoks):
        if name not in DBG_TAPS:
            return
        sh = list(ap.shape)
        dbg = nc.dram_tensor("dbg_" + name, sh, ap.dtype,
                             kind="ExternalOutput").ap()
        dma("sync", dbg, ap, r=tuple(rtoks), w=("dbg",))

    for k in _C0:
        dma("sync", C[k], cdr[k], w=(k,))
    dma("sync", cu_s[0:1, 0:3], cu_d, w=("cu",))
    for t in range(NT):
        dma("sync", X()[:, t * D:(t + 1) * D], xo_d[t], w=(f"x{t}",))
    vop(lambda E: E.memset(onesp, 1.0), w=("onesp",))

    vop(lambda E: E.tensor_copy(smal[0:1, 200:203], cu_s[0:1, 0:3]),
        r=("cu",), w=("l0",))
    vop(lambda E: E.tensor_tensor(smal[0:1, 204:205], smal[0:1, 201:202],
                                  smal[0:1, 200:201], OP.subtract),
        r=("l0",), w=("l0",))
    vop(lambda E: E.tensor_scalar(sqrow_b[0:1, :], C["qpos"][0:1, :],
                                  smal[0:1, 204:205], None, OP.is_ge),
        r=("l0", "qpos"), w=("sqrow",))
    vop(lambda E: E.tensor_scalar(skprow[0:1, :], C["kposr"][0:1, :],
                                  smal[0:1, 204:205], None, OP.is_ge),
        r=("l0", "kposr"), w=("skprow",))
    vop(lambda E: E.tensor_tensor(skprow[0:1, :], skprow[0:1, :],
                                  C["invalidr"][0:1, :], OP.add),
        r=("invalidr", "skprow"), w=("skprow",))
    for half in range(2):
        top(lambda E, h=half: E.matmul(
            pb[6][:, 0:512], C["onesb"][0:1, :],
            sqrow_b[0:1, h * 512:(h + 1) * 512], start=True, stop=True),
            r=("sqrow", "onesb"), w=(PT[6],))
        vop(lambda E, h=half: E.tensor_copy(sq_rep[:, h * 512:(h + 1) * 512],
                                            pb[6][:, 0:512]),
            r=(PT[6],), w=("sq_rep",))
    for m in range(16):
        top(lambda E, m=m: E.transpose(pb[6][:, 0:1],
                                       skprow[0:1, m * 128:(m + 1) * 128],
                                       C["idf"][0:1, 0:1]),
            r=("skprow", "idf"), w=(PT[6],))
        vop(lambda E, m=m: E.tensor_copy(skcol[:, m:m + 1], pb[6][:, 0:1]),
            r=(PT[6],), w=("skcol",))

    tap("sq_rep", sq_rep, ("sq_rep",))
    tap("skcol", skcol, ("skcol",))
    tap("l0", smal[0:1, 200:208], ("l0",))
    A.freebuf("qpos", "kposr", "invalidr", "skprow", "sqrow_b")

    def ln_stats(xa, rtok):
        for q in range(4):
            vop(lambda E, q=q: E.bn_stats(bn6[:, q * 6:(q + 1) * 6],
                                          xa[:, q * 512:(q + 1) * 512]),
                r=rtok, w=("bn6",))
        vop(lambda E: E.tensor_reduce(lnsc[:, 0:1], bn6[:, 1:24:3], AX, OP.add),
            r=("bn6",), w=("lnsc",))
        vop(lambda E: E.tensor_reduce(lnsc[:, 1:2], bn6[:, 2:24:3], AX, OP.add),
            r=("bn6",), w=("lnsc",))
        vop(lambda E: E.tensor_tensor(lnsc[:, 2:10], bn6[:, 1:24:3],
                                      bn6[:, 1:24:3], OP.mult),
            r=("bn6",), w=("lnsc",))
        vop(lambda E: E.tensor_reduce(lnsc[:, 10:11], lnsc[:, 2:10], AX, OP.add),
            r=("lnsc",), w=("lnsc",))
        vop(lambda E: E.tensor_scalar(lnsc[:, 0:1], lnsc[:, 0:1], 0.125, None,
                                      OP.mult), r=("lnsc",), w=("lnsc",))
        vop(lambda E: E.tensor_tensor(lnsc[:, 3:4], lnsc[:, 0:1], lnsc[:, 0:1],
                                      OP.mult), r=("lnsc",), w=("lnsc",))
        vop(lambda E: E.tensor_scalar(lnsc[:, 1:2], lnsc[:, 1:2], 1.0 / 2048.0,
                                      None, OP.mult), r=("lnsc",), w=("lnsc",))
        vop(lambda E: E.tensor_scalar(lnsc[:, 10:11], lnsc[:, 10:11], 0.125, None,
                                      OP.mult), r=("lnsc",), w=("lnsc",))
        vop(lambda E: E.tensor_tensor(lnsc[:, 1:2], lnsc[:, 1:2], lnsc[:, 10:11],
                                      OP.add), r=("lnsc",), w=("lnsc",))
        vop(lambda E: E.tensor_tensor(lnsc[:, 1:2], lnsc[:, 1:2], lnsc[:, 3:4],
                                      OP.subtract), r=("lnsc",), w=("lnsc",))
        vop(lambda E: E.tensor_scalar(lnsc[:, 1:2], lnsc[:, 1:2], LN_EPS, None,
                                      OP.add), r=("lnsc",), w=("lnsc",))
        sop(lambda E: E.activation(lnsc[:, 4:5], lnsc[:, 1:2], ACT.Sqrt),
            r=("lnsc",), w=("lnsq",))
        vop(lambda E: E.reciprocal(smv(64), lnsc[:, 4:5]), r=("lnsq",), w=("lnA",))
        vop(lambda E: E.tensor_tensor(smv(80), lnsc[:, 0:1], smv(64), OP.mult),
            r=("lnA", "lnsc"), w=("lnB",))
        vop(lambda E: E.tensor_scalar(smv(80), smv(80), -1.0, None, OP.mult),
            r=("lnB",), w=("lnB",))

    def tm_to_fm(src_bf, rtoks, dst_fm, colf, wtok, bank_i=7, nwc=None):
        for g in range(4):
            bank = pb[bank_i - (g % 2)]
            bt = PT[bank_i - (g % 2)]
            for kk in range(4):
                c = g * 4 + kk
                top(lambda E, c=c, kk=kk, bank=bank: E.transpose(
                    bank.bitcast(BF16)[:, kk * 128:(kk + 1) * 128],
                    src_bf[:, c * 128:(c + 1) * 128], C["idb"]),
                    r=tuple(rtoks) + ("idb",), w=(bt,))
            for kk in range(4):
                c = g * 4 + kk
                col = colf(c)
                if nwc is None:
                    sop(lambda E, kk=kk, col=col, bank=bank: E.copy(
                        dst_fm[:, col:col + 128],
                        bank.bitcast(BF16)[:, kk * 128:(kk + 1) * 128]),
                        r=(bt,), w=(wtok,))
                else:
                    sop(lambda E, kk=kk, col=col, c=c, bank=bank: E.mul(
                        dst_fm[:, col:col + 128],
                        bank.bitcast(BF16)[:, kk * 128:(kk + 1) * 128],
                        nwc[:, c:c + 1]),
                        r=(bt, "nwcol"), w=(wtok,))

    wps = A.alloc("wps", 128, 2048, F32, toks=("wpsb0", "wl2"))
    wps2 = A.alloc("wps2", 128, 2048, F32, toks=("wpsb1", "tern"))
    gout = A.alloc("gout", 128, 2048, BF16)
    vop(lambda E: E.memset(smv(210, 2), 0.0), w=("wsum",))
    slices = [(wdr["wua"][j], 210, f"wl{j}") for j in range(4)]
    slices += [(wdr["wda"][j][:, hf * 2048:(hf + 1) * 2048], 211,
                f"wl{4 + 2 * j + hf}") for j in range(2) for hf in range(2)]
    for j, (src, acc, tk) in enumerate(slices):
        buf = wps if j % 2 == 0 else wps2
        bt = f"wpsb{j % 2}"
        dma("sync", buf, src, w=(bt,))
        vop(lambda E, b=buf: E.tensor_reduce(smv(212), b, AX, OP.add,
                                             apply_absolute_value=True),
            r=(bt,), w=("wred",))
        vop(lambda E, acc=acc: E.tensor_tensor(smv(acc), smv(acc), smv(212),
                                               OP.add),
            r=("wred", "wsum"), w=("wsum",))
    vop(lambda E: E.memset(smv(213), 1.0), w=("wone",))
    top(lambda E: E.matmul(pb[6][0:2, 0:1], smal[:, 210:212], smal[:, 213:214],
                           start=True, stop=True),
        r=("wsum", "wone"), w=(PT[6],))
    vop(lambda E: E.tensor_copy(wpart[0:2, 0:1], pb[6][0:2, 0:1]),
        r=(PT[6],), w=("wpartt",))
    dma("sync", ar_in, wpart[0:2, 0:1], r=("wpartt",), w=("arin",))
    s.add("gpsimd", lambda E: E.collective_compute(
        "AllReduce", OP.add, replica_groups=[list(range(8))],
        ins=[ar_in.opt()], outs=[ar_out.opt()]),
        r=("arin",), w=("arout",), ovr=True)
    dma("sync", sval[0:1, 0:2], ar_out, r=("arout",), w=("sval",))
    vop(lambda E: E.tensor_scalar(sval[0:1, 0:2], sval[0:1, 0:2], 1.0 / W_CNT,
                                  FFN_EPS, OP.mult, OP.max),
        r=("sval",), w=("sval",))
    vop(lambda E: E.reciprocal(sval[0:1, 2:4], sval[0:1, 0:2]),
        r=("sval",), w=("sval",))
    top(lambda E: E.matmul(pb[6][:, 0:4], C["onesf"][0:1, :], sval[0:1, 0:4],
                           start=True, stop=True),
        r=("sval", "onesf"), w=(PT[6],))
    vop(lambda E: E.tensor_copy(svrep[:, 0:4], pb[6][:, 0:4]),
        r=(PT[6],), w=("svrep",))
    for j, (src, acc, tk) in enumerate(slices):
        rcol = svrep[:, 2:3] if acc == 210 else svrep[:, 3:4]
        dma("sync", wps, src, w=("wl2",))
        vop(lambda E, rcol=rcol: E.tensor_scalar(wps2, wps, rcol, MAGIC,
                                                 OP.mult, OP.add),
            r=("wl2", "svrep"), w=("tern",))
        vop(lambda E: E.tensor_scalar(wps2, wps2, -MAGIC, -1.0, OP.add, OP.max),
            r=("tern",), w=("tern",))
        vop(lambda E: E.tensor_scalar(gout, wps2, 1.0, None, OP.min),
            r=("tern",), w=("gout",))
        if j < 4:
            dma("sync", agu_in[j], gout, r=("gout",), w=("aguin",))
        else:
            jj = j - 4
            dma("sync", agd_in[jj // 2][:, (jj % 2) * 2048:(jj % 2 + 1) * 2048],
                gout, r=("gout",), w=("agdin",))
    s.add("gpsimd", lambda E: E.collective_compute(
        "AllGather", OP.bypass, replica_groups=[list(range(8))],
        ins=[agu_in.opt()], outs=[agu_out.opt()]),
        r=("aguin",), w=("agu",), ovr=True)
    s.add("gpsimd", lambda E: E.collective_compute(
        "AllGather", OP.bypass, replica_groups=[list(range(8))],
        ins=[agd_in.opt()], outs=[agd_out.opt()]),
        r=("agdin",), w=("agd",), ovr=True)
    tap("gup", agu_out, ("agu",))
    tap("gdn", agd_out, ("agd",))
    tap("sval", sval[0:1, 0:4], ("sval",))
    A.freebuf("wps", "wps2", "gout")

    def build_attn_a():
        k2 = A.alloc("k2", 128, 4 * 2048, BF16)
        vv = A.alloc("vv", 128, 16 * 520, BF16)
        wblk0 = A.alloc("awb0", 128, D, BF16)
        wblk1 = A.alloc("awb1", 128, D, BF16)
        wvs = A.alloc("awvs", 128, KC * 512, BF16)
        ytmp = A.alloc("aytmp", 128, D, BF16)
        xpt0 = A.alloc("xpt0", 128, D, F32)
        xpt1 = A.alloc("xpt1", 128, D, F32)
        y1p = A.alloc("y1p", 128, KC * 512, BF16)
        WB = [wblk0, wblk1]
        XP = [xpt0, xpt1]
        vop(lambda E: E.memset(vv[:, 64::65], 1.0), w=("vv",))
        dma("gpsimd", wvs, wdr["wa_v"][0], w=("awvs",))

        def kv_proj(ysl, ytok, keyt0):
            for mc in range(4):
                dma("gpsimd", WB[mc % 2], wdr["wa_k"][mc], w=(f"awb{mc % 2}",))
                bk = pb[4 + mc % 2]
                for kc in range(KC):
                    top(lambda E, mc=mc, kc=kc, bk=bk: E.matmul(
                        bk[:, 0:512], WB[mc % 2][:, kc * 128:(kc + 1) * 128],
                        ysl(kc), start=(kc == 0), stop=(kc == KC - 1)),
                        r=(f"awb{mc % 2}", ytok), w=(PT[4 + mc % 2],))
                for g2 in range(2):
                    sop(lambda E, mc=mc, g2=g2, bk=bk, keyt0=keyt0: E.copy(
                        k2[g2 * 64:g2 * 64 + 64,
                           mc * 2048 + keyt0 * 128: mc * 2048 + keyt0 * 128 + 512],
                        bk[g2 * 64:g2 * 64 + 64, 0:512]),
                        r=(PT[4 + mc % 2],), w=("k2",))
            for t4 in range(4):
                kt = keyt0 + t4
                bk = pb[4 + t4 % 2]
                for kc in range(KC):
                    top(lambda E, t4=t4, kc=kc, bk=bk: E.matmul(
                        bk[:, 0:512], ysl(kc)[:, t4 * 128:t4 * 128 + 128],
                        wvs[:, kc * 512:(kc + 1) * 512],
                        start=(kc == 0), stop=(kc == KC - 1)),
                        r=("awvs", ytok), w=(PT[4 + t4 % 2],))
                for g in range(8):
                    sop(lambda E, g=g, kt=kt, bk=bk: E.copy(
                        vv[:, kt * 520 + g * 65: kt * 520 + g * 65 + 64],
                        bk[:, g * 64:(g + 1) * 64]),
                        r=(PT[4 + t4 % 2],), w=("vv",))

        for half in range(2):
            for t4 in range(4):
                tg = half * 4 + t4
                xb = XP[t4 % 2]
                xt = f"xpt{t4 % 2}"
                dma("sync", xb, xp_d[tg], w=(xt,))
                ln_stats(xb, (xt,))
                vop(lambda E, xb=xb: E.tensor_scalar(ytmp, xb, smv(64), smv(80),
                                                     OP.mult, OP.add),
                    r=(xt, "lnA", "lnB"), w=("aytmp",))
                tm_to_fm(ytmp, ("aytmp",), y1p,
                         lambda c, t4=t4: c * 512 + t4 * 128, "y1p")
            kv_proj(lambda kc, half=half:
                    y1p[:, kc * 512:(kc + 1) * 512], "y1p", half * 4)
        A.freebuf("xpt0", "xpt1", "y1p")
        y1f = A.alloc("y1f", 128, KC * T, BF16)
        qf = A.alloc("qf", 128, KC * T, BF16)
        for t in range(NT):
            ln_stats(X()[:, t * D:(t + 1) * D], (f"x{t}",))
            vop(lambda E, t=t: E.tensor_scalar(ytmp, X()[:, t * D:(t + 1) * D],
                                               smv(64), smv(80), OP.mult, OP.add),
                r=(f"x{t}", "lnA", "lnB"), w=("aytmp",))
            tm_to_fm(ytmp, ("aytmp",), y1f,
                     lambda c, t=t: c * T + t * 128, "y1f")
        for half in range(2):
            kv_proj(lambda kc, half=half:
                    y1f[:, kc * T + half * 512: kc * T + half * 512 + 512],
                    "y1f", 8 + half * 4)
        for mc in range(KC):
            dma("gpsimd", WB[mc % 2], wdr["wa_q"][mc], w=(f"awb{mc % 2}",))
            for half in range(2):
                bk = pb[4 + half]
                for kc in range(KC):
                    top(lambda E, mc=mc, kc=kc, half=half, bk=bk: E.matmul(
                        bk[:, 0:512], WB[mc % 2][:, kc * 128:(kc + 1) * 128],
                        y1f[:, kc * T + half * 512: kc * T + half * 512 + 512],
                        start=(kc == 0), stop=(kc == KC - 1)),
                        r=(f"awb{mc % 2}", "y1f"), w=(PT[4 + half],))
                sop(lambda E, mc=mc, half=half, bk=bk: E.copy(
                    qf[:, mc * T + half * 512: mc * T + half * 512 + 512],
                    bk[:, 0:512]), r=(PT[4 + half],), w=("qf",))
        tap("y1f", y1f, ("y1f",))
        tap("qf", qf, ("qf",))
        tap("k2", k2, ("k2",))
        tap("vv", vv, ("vv",))
        A.freebuf("y1f", "aytmp", "awb0", "awb1", "awvs")
        k2d = A.alloc("k2d", 128, 4 * 2048, BF16)
        dma("sync", k2d[0:64, :], k2[64:128, :], r=("k2",), w=("k2d",))
        dma("sync", k2d[64:128, :], k2[0:64, :], r=("k2",), w=("k2d",))
        zf = A.alloc("zf", 128, KC * T, BF16)
        pbh = A.alloc("pbh", 128, 8 * 512, BF16,
                      toks=tuple(f"pbh{i}" for i in range(8)))
        maskb = A.alloc("maskb", 128, 2 * 512, BF16,
                        toks=("maskb0", "maskb1"))
        recbuf = A.alloc("recbuf", 128, 512, F32)
        rsb = A.alloc("rsb", 128, 512, F32)
        zstage = A.alloc("zstage", 128, 512, BF16)
        for g in range(8):
            for qh in range(2):
                started = [False] * 4
                for m in range(1, 16):
                    jlo = max(max(0, m - 8), qh * 4)
                    jhi = min(min(7, m - 1), qh * 4 + 3)
                    if jlo > jhi:
                        continue
                    wc = (jhi - jlo + 1) * 128
                    woff = (jlo - qh * 4) * 128
                    mslot = (m % 2) * 512
                    mtok = f"maskb{m % 2}"
                    vop(lambda E, m=m, jlo=jlo, wc=wc, mslot=mslot:
                        E.tensor_scalar(
                        maskb[:, mslot:mslot + wc],
                        sq_rep[:, jlo * 128:jlo * 128 + wc],
                        skcol[:, m:m + 1], None, OP.is_equal),
                        r=("sq_rep", "skcol"), w=(mtok,))
                    if jlo == m - 8:
                        vop(lambda E, mslot=mslot: E.tensor_tensor(
                            maskb[:, mslot:mslot + 128],
                            maskb[:, mslot:mslot + 128], C["diagm"],
                            OP.mult), r=("diagm", mtok), w=(mtok,))
                    for h4 in range(4):
                        h = 4 * g + h4
                        rb = 64 * (h % 2)
                        ksrc = k2 if (g % 2) == (h % 2) else k2d
                        ktok = "k2" if (g % 2) == (h % 2) else "k2d"
                        sbk = pb[4 + h4]
                        pslot = ((m % 2) * 4 + h4) * 512
                        ptok = f"pbh{(m % 2) * 4 + h4}"
                        top(lambda E, g=g, m=m, rb=rb, jlo=jlo, wc=wc, h=h,
                            ksrc=ksrc, sbk=sbk: E.matmul(
                            sbk[:, 0:wc],
                            ksrc[rb:rb + 64,
                                 (g // 2) * 2048 + m * 128:
                                 (g // 2) * 2048 + m * 128 + 128],
                            qf[rb:rb + 64, (h // 2) * T + jlo * 128:
                               (h // 2) * T + jlo * 128 + wc],
                            start=True, stop=True),
                            r=(ktok, "qf"), w=(PT[4 + h4],))
                        sop(lambda E, pslot=pslot, wc=wc, sbk=sbk:
                            E.activation(
                            pbh[:, pslot:pslot + wc], sbk[:, 0:wc],
                            ACT.Exp, scale=0.125),
                            r=(PT[4 + h4],), w=(ptok,))
                        vop(lambda E, pslot=pslot, wc=wc, mslot=mslot:
                            E.tensor_tensor(
                            pbh[:, pslot:pslot + wc],
                            pbh[:, pslot:pslot + wc],
                            maskb[:, mslot:mslot + wc],
                            OP.mult), r=(mtok, ptok), w=(ptok,))
                        top(lambda E, g=g, m=m, h4=h4, woff=woff, wc=wc,
                            pslot=pslot, st=(not started[h4]): E.matmul(
                            pb[h4][0:65, woff:woff + wc],
                            vv[:, m * 520 + 65 * g: m * 520 + 65 * g + 65],
                            pbh[:, pslot:pslot + wc],
                            start=st, stop=False),
                            r=("vv", ptok), w=(PT[h4],))
                        started[h4] = True
                for h4 in range(4):
                    h = 4 * g + h4
                    zc = h // 2
                    sop(lambda E, h4=h4: E.activation(
                        recbuf[64:65, :], pb[h4][64:65, 0:512], ACT.Ln),
                        r=(PT[h4],), w=("recbuf",))
                    sop(lambda E: E.activation(
                        recbuf[64:65, :], recbuf[64:65, :], ACT.Exp,
                        scale=-1.0), r=("recbuf",), w=("recbuf",))
                    top(lambda E: E.matmul(
                        pb[6][:, 0:512], onesp[64:65, 0:128],
                        recbuf[64:65, :], start=True, stop=True),
                        r=("recbuf", "onesp"), w=(PT[6],))
                    vop(lambda E: E.tensor_copy(rsb, pb[6][:, 0:512]),
                        r=(PT[6],), w=("rsb",))
                    if h % 2 == 0:
                        vop(lambda E, h4=h4, zc=zc, qh=qh: E.tensor_tensor(
                            zf[0:64, zc * T + qh * 512: zc * T + qh * 512 + 512],
                            pb[h4][0:64, 0:512], rsb[0:64, :], OP.mult),
                            r=(PT[h4], "rsb"), w=("zf",))
                    else:
                        vop(lambda E, h4=h4: E.tensor_tensor(
                            zstage[0:64, :], pb[h4][0:64, 0:512], rsb[0:64, :],
                            OP.mult), r=(PT[h4], "rsb"), w=("zstage",))
                        dma("sync",
                            zf[64:128, zc * T + qh * 512: zc * T + qh * 512 + 512],
                            zstage[0:64, :], r=("zstage",), w=("zf",))
        A.freebuf("qf", "k2", "k2d", "vv", "pbh", "maskb", "recbuf", "rsb",
                  "zstage")
        tap("zf", zf, ("zf",))
        WO = [A.alloc(f"awo{i}", 128, KC * 512, BF16) for i in range(4)]
        for nb in range(4):
            dma("gpsimd", WO[nb], wdr["wa_o"][nb], w=(f"awo{nb}",))
        for t in range(NT):
            bs = 4 * (t % 2)
            for kc in range(KC):
                for nb in range(4):
                    top(lambda E, t=t, kc=kc, nb=nb, bs=bs: E.matmul(
                        pb[bs + nb][:, 0:512],
                        zf[:, kc * T + t * 128: kc * T + t * 128 + 128],
                        WO[nb][:, kc * 512:(kc + 1) * 512],
                        start=(kc == 0), stop=(kc == KC - 1)),
                        r=(f"awo{nb}", "zf"), w=(PT[bs + nb],))
            for nb in range(4):
                vop(lambda E, nb=nb, t=t, bs=bs: E.tensor_tensor(
                    X()[:, t * D + nb * 512: t * D + nb * 512 + 512],
                    X()[:, t * D + nb * 512: t * D + nb * 512 + 512],
                    pb[bs + nb][:, 0:512], OP.add),
                    r=(PT[bs + nb], f"x{t}"), w=(f"x{t}",))
        A.freebuf("zf", "awo0", "awo1", "awo2", "awo3")

    def build_ffn(call):
        sfx = f"f{call}"
        a1f = A.alloc("a1f" + sfx, 128, KC * T, BF16)
        ytf = A.alloc("ytf" + sfx, 128, D, F32)
        atm = A.alloc("atm" + sfx, 128, D, BF16)
        c1b = A.alloc("c1b" + sfx, 128, T, F32)
        s2b = A.alloc("s2b" + sfx, 128, T, F32)
        m2p = A.alloc("m2p" + sfx, 128, T, F32)
        hst0 = A.alloc("hst0" + sfx, 128, T, F32)
        hst1 = A.alloc("hst1" + sfx, 128, T, F32)
        gb0 = A.alloc("gb0" + sfx, 128, D, BF16)
        gb1 = A.alloc("gb1" + sfx, 128, D, BF16)
        HS = [hst0, hst1]
        GB = [gb0, gb1]
        TA1, TYT, TATM = "a1f" + sfx, "ytf" + sfx, "atm" + sfx
        for t in range(NT):
            ln_stats(X()[:, t * D:(t + 1) * D], (f"x{t}",))
            vop(lambda E, t=t: E.tensor_scalar(ytf, X()[:, t * D:(t + 1) * D],
                                               smv(64), smv(80), OP.mult,
                                               OP.add),
                r=(f"x{t}", "lnA", "lnB"), w=(TYT,))
            vop(lambda E, t=t: E.tensor_reduce(smv(0 + t), ytf, AX, OP.max,
                                               apply_absolute_value=True),
                r=(TYT,), w=("m1c",))
            vop(lambda E, t=t: E.tensor_scalar(smv(0 + t), smv(0 + t), FFN_EPS,
                                               None, OP.max),
                r=("m1c",), w=("m1c",))
            vop(lambda E, t=t: E.reciprocal(smv(16 + t), smv(0 + t)),
                r=("m1c",), w=("s1c",))
            vop(lambda E, t=t: E.tensor_scalar(smv(16 + t), smv(16 + t), 127.0,
                                               None, OP.mult),
                r=("s1c",), w=("s1c",))
            vop(lambda E, t=t: E.tensor_scalar(ytf, ytf, smv(16 + t), MAGIC,
                                               OP.mult, OP.add),
                r=(TYT, "s1c"), w=(TYT,))
            vop(lambda E: E.tensor_scalar(atm, ytf, -MAGIC, None, OP.add),
                r=(TYT,), w=(TATM,))
            tm_to_fm(atm, (TATM,), a1f, lambda c, t=t: c * T + t * 128, TA1)
        top(lambda E: E.transpose(pb[6][0:8, 0:128], smal[:, 0:8], C["idf"]),
            r=("m1c", "idf"), w=(PT[6],))
        vop(lambda E: E.tensor_copy(m1t[0:8, :], pb[6][0:8, 0:128]),
            r=(PT[6],), w=("m1tt",))
        dma("sync", brow[0:1, :], m1t[0:8, :], r=("m1tt",), w=("browt",))
        vop(lambda E: E.tensor_scalar(su127[0:1, :], C["onesf"][0:1, :],
                                      sval[0:1, 0:1], 1.0 / 127.0,
                                      OP.mult, OP.mult),
            r=("sval", "onesf"), w=("su127t",))
        for g8 in range(NT):
            bk = pb[6 if g8 % 2 == 0 else 7]
            top(lambda E, g8=g8, bk=bk: E.matmul(
                bk[:, 0:128], su127[0:1, :], brow[0:1, g8 * 128:(g8 + 1) * 128],
                start=True, stop=True),
                r=("su127t", "browt"), w=(PT[6 + g8 % 2],))
            vop(lambda E, g8=g8, bk=bk: E.tensor_copy(
                c1b[:, g8 * 128:(g8 + 1) * 128], bk[:, 0:128]),
                r=(PT[6 + g8 % 2],), w=("c1b" + sfx,))
        vop(lambda E: E.memset(m2p, 0.0), w=("m2p" + sfx,))
        for mb in range(32):
            dma("sync", GB[mb % 2], agu_out[mb], r=("agu",),
                w=(f"gb{mb % 2}" + sfx,))
            hs = HS[mb % 2]
            ht = f"hst{mb % 2}" + sfx
            for half in range(2):
                bk = pb[half]
                for kc in range(KC):
                    top(lambda E, mb=mb, kc=kc, half=half, bk=bk: E.matmul(
                        bk[:, 0:512], GB[mb % 2][:, kc * 128:(kc + 1) * 128],
                        a1f[:, kc * T + half * 512: kc * T + half * 512 + 512],
                        start=(kc == 0), stop=(kc == KC - 1)),
                        r=(f"gb{mb % 2}" + sfx, TA1), w=(PT[half],))
                vop(lambda E, half=half, hs=hs, bk=bk: E.tensor_tensor(
                    hs[:, half * 512:half * 512 + 512], bk[:, 0:512],
                    c1b[:, half * 512:half * 512 + 512], OP.mult),
                    r=(PT[half], "c1b" + sfx), w=(ht,))
                vop(lambda E, half=half, hs=hs: E.scalar_tensor_tensor(
                    hs[:, half * 512:half * 512 + 512],
                    hs[:, half * 512:half * 512 + 512], 0.0,
                    hs[:, half * 512:half * 512 + 512], OP.max, OP.mult),
                    r=(ht,), w=(ht,))
                vop(lambda E, half=half, hs=hs: E.tensor_tensor(
                    m2p[:, half * 512:half * 512 + 512],
                    m2p[:, half * 512:half * 512 + 512],
                    hs[:, half * 512:half * 512 + 512], OP.max),
                    r=(ht, "m2p" + sfx), w=("m2p" + sfx,))
            dma("sync", hsp_d[mb], hs, r=(ht,), w=("hsp",))
        for g8 in range(NT):
            top(lambda E, g8=g8: E.transpose(
                pb[6][:, 0:128], m2p[:, g8 * 128:(g8 + 1) * 128], C["idf"]),
                r=("m2p" + sfx, "idf"), w=(PT[6],))
            vop(lambda E, g8=g8: E.tensor_reduce(smv(32 + g8), pb[6][:, 0:128],
                                                 AX, OP.max),
                r=(PT[6],), w=("m2c",))
        vop(lambda E: E.tensor_scalar(smal[:, 32:40], smal[:, 32:40], FFN_EPS,
                                      None, OP.max), r=("m2c",), w=("m2c",))
        vop(lambda E: E.reciprocal(smal[:, 40:48], smal[:, 32:40]),
            r=("m2c",), w=("s2c",))
        vop(lambda E: E.tensor_scalar(smal[:, 40:48], smal[:, 40:48], 127.0,
                                      None, OP.mult), r=("s2c",), w=("s2c",))
        vop(lambda E: E.tensor_scalar(smal[:, 48:56], smal[:, 32:40],
                                      svrep[:, 1:2], 1.0 / 127.0,
                                      OP.mult, OP.mult),
            r=("m2c", "svrep"), w=("c3c",))
        top(lambda E: E.transpose(pb[6][0:8, 0:128], smal[:, 40:48], C["idf"]),
            r=("s2c", "idf"), w=(PT[6],))
        vop(lambda E: E.tensor_copy(m1t[0:8, :], pb[6][0:8, 0:128]),
            r=(PT[6],), w=("m1tt",))
        dma("sync", brow[0:1, :], m1t[0:8, :], r=("m1tt",), w=("browt",))
        for g8 in range(NT):
            bk = pb[6 if g8 % 2 == 0 else 7]
            top(lambda E, g8=g8, bk=bk: E.matmul(
                bk[:, 0:128], C["onesf"][0:1, :],
                brow[0:1, g8 * 128:(g8 + 1) * 128], start=True, stop=True),
                r=("onesf", "browt"), w=(PT[6 + g8 % 2],))
            vop(lambda E, g8=g8, bk=bk: E.tensor_copy(
                s2b[:, g8 * 128:(g8 + 1) * 128], bk[:, 0:128]),
                r=(PT[6 + g8 % 2],), w=("s2b" + sfx,))
        A.freebuf("a1f" + sfx, "ytf" + sfx, "atm" + sfx, "c1b" + sfx,
                  "m2p" + sfx)
        a2fA = A.alloc("a2fA" + sfx, 128, 16 * T, BF16)
        a2fB = A.alloc("a2fB" + sfx, 128, 16 * T, BF16)
        def a2tok(k):
            return ("a2fA" + sfx) if k < 16 else ("a2fB" + sfx)

        def a2sl(k, lo, n):
            buf = a2fA if k < 16 else a2fB
            kk = k % 16
            return buf[:, kk * T + lo: kk * T + lo + n]
        for mb in range(32):
            hs = HS[mb % 2]
            ht = f"hst{mb % 2}" + sfx
            dma("sync", hs, hsp_d[mb], r=("hsp",), w=(ht,))
            vop(lambda E, hs=hs: E.tensor_tensor(hs, hs, s2b, OP.mult),
                r=(ht, "s2b" + sfx), w=(ht,))
            vop(lambda E, hs=hs: E.tensor_scalar(hs, hs, MAGIC, None, OP.add),
                r=(ht,), w=(ht,))
            vop(lambda E, mb=mb, hs=hs: E.tensor_scalar(
                a2sl(mb, 0, T), hs, -MAGIC, None, OP.add),
                r=(ht,), w=(a2tok(mb),))
        A.freebuf("hst0" + sfx, "hst1" + sfx, "s2b" + sfx, "gb0" + sfx,
                  "gb1" + sfx)
        gd0 = A.alloc("gd0" + sfx, 128, 8 * 512, BF16)
        gd1 = A.alloc("gd1" + sfx, 128, 8 * 512, BF16)
        GD = [gd0, gd1]
        for nb in range(4):
            for kg in range(4):
                sb_i = nb * 4 + kg
                dma("sync", GD[sb_i % 2], agd_out[sb_i], r=("agd",),
                    w=(f"gd{sb_i % 2}" + sfx,))
                for t in range(NT):
                    for kk in range(8):
                        k = 8 * kg + kk
                        top(lambda E, t=t, k=k, kk=kk, sb_i=sb_i, kg=kg:
                            E.matmul(
                            pb[t][:, 0:512],
                            a2sl(k, t * 128, 128),
                            GD[sb_i % 2][:, kk * 512:(kk + 1) * 512],
                            start=(kg == 0 and kk == 0),
                            stop=(kg == 3 and kk == 7)),
                            r=(f"gd{sb_i % 2}" + sfx, a2tok(k)), w=(PT[t],))
            for t in range(NT):
                vop(lambda E, nb=nb, t=t: E.scalar_tensor_tensor(
                    X()[:, t * D + nb * 512: t * D + nb * 512 + 512],
                    pb[t][:, 0:512], smal[:, 48 + t:49 + t],
                    X()[:, t * D + nb * 512: t * D + nb * 512 + 512],
                    OP.mult, OP.add),
                    r=(PT[t], "c3c", f"x{t}"), w=(f"x{t}",))
        A.freebuf("a2fA" + sfx, "a2fB" + sfx, "gd0" + sfx, "gd1" + sfx)

    def build_attn_b():
        CBv = {k: A.alloc(k, sh[0], sh[1], F32) for k, sh in _CB.items()}
        for k in _CB:
            dma("sync", CBv[k], cdr[k], w=(k,))
        nwrow_s = A.alloc("nwrow_s", 1, D, F32)
        nwcol = A.alloc("nwcol", 128, 16, F32)
        dma("sync", nwrow_s, nw_d, w=("nwrow_s",))
        for m in range(KC):
            top(lambda E, m=m: E.transpose(
                pb[6][:, 0:1], nwrow_s[0:1, m * 128:(m + 1) * 128],
                C["idf"][0:1, 0:1]),
                r=("nwrow_s", "idf"), w=(PT[6],))
            vop(lambda E, m=m: E.tensor_copy(nwcol[:, m:m + 1], pb[6][:, 0:1]),
                r=(PT[6],), w=("nwcol",))
        A.freebuf("nwrow_s")
        y3f = A.alloc("y3f", 128, KC * T, BF16)
        ytmp = A.alloc("bytmp", 128, D, BF16)
        tap("xb0", X()[:, 0:D], ("x0",))
        for t in range(NT):
            ln_stats(X()[:, t * D:(t + 1) * D], (f"x{t}",))
            vop(lambda E, t=t: E.tensor_scalar(ytmp, X()[:, t * D:(t + 1) * D],
                                               smv(64), smv(80), OP.mult,
                                               OP.add),
                r=(f"x{t}", "lnA", "lnB"), w=("bytmp",))
            if t == 0:
                tap("ab0", smal[:, 60:90], ("lnB",))
                tap("ytmp0", ytmp, ("bytmp",))
                tap("lnsc0", lnsc, ("lnsc",))
            tm_to_fm(ytmp, ("bytmp",), y3f, lambda c, t=t: c * T + t * 128,
                     "y3f")
        tap("y3f", y3f, ("y3f",))
        for t in range(NT):
            dma("sync", xsp_d[t], X()[:, t * D:(t + 1) * D],
                r=(f"x{t}",), w=("xspill",))
        A.freebuf("X", "bytmp")
        qf = A.alloc("qbf", 128, BH * T, BF16)
        kf = A.alloc("kbf", 128, BH * T, BF16)
        g1f = A.alloc("g1f", 128, T, BF16)
        silf = A.alloc("bsil", 128, 512, F32)
        rt1 = A.alloc("brt1", 128, 512, F32)
        rt2 = A.alloc("brt2", 128, 512, F32)
        qtms = A.alloc("qtms", 128, 512, BF16)
        wrhs = A.alloc("bwrhs", 128, KC * 512, BF16)
        wblk = A.alloc("bwblk", 128, D, BF16)

        def proj_rot4(wkey, dst, dtok):
            for nb in range(4):
                dma("gpsimd", wrhs, wdr[wkey][nb], w=("bwrhs",))
                for t in range(NT):
                    bk = pb[4 + t % 2]
                    for kc in range(KC):
                        top(lambda E, t=t, kc=kc, bk=bk: E.matmul(
                            bk[:, 0:512],
                            y3f[:, kc * T + t * 128: kc * T + t * 128 + 128],
                            wrhs[:, kc * 512:(kc + 1) * 512],
                            start=(kc == 0), stop=(kc == KC - 1)),
                            r=("bwrhs", "y3f"), w=(PT[4 + t % 2],))
                    sop(lambda E, bk=bk: E.activation(silf, bk[:, 0:512],
                                                      ACT.Silu),
                        r=(PT[4 + t % 2],), w=("bsil",))
                    csl = CBv["cs_tm"][:, t * 128:(t + 1) * 128]
                    for hb in range(4):
                        o = hb * 128
                        vop(lambda E, o=o, csl=csl: E.tensor_tensor(
                            rt1[:, 0:64], silf[:, o:o + 64], csl[:, 0:64],
                            OP.mult), r=("bsil", "cs_tm"), w=("brt1",))
                        vop(lambda E, o=o, csl=csl: E.tensor_tensor(
                            rt2[:, 0:64], silf[:, o + 64:o + 128],
                            csl[:, 64:128], OP.mult),
                            r=("bsil", "cs_tm"), w=("brt2",))
                        vop(lambda E, o=o: E.tensor_tensor(
                            qtms[:, o:o + 64], rt1[:, 0:64], rt2[:, 0:64],
                            OP.subtract), r=("brt1", "brt2"), w=("qtms",))
                        vop(lambda E, o=o, csl=csl: E.tensor_tensor(
                            rt1[:, 0:64], silf[:, o:o + 64], csl[:, 64:128],
                            OP.mult), r=("bsil", "cs_tm"), w=("brt1",))
                        vop(lambda E, o=o, csl=csl: E.tensor_tensor(
                            rt2[:, 0:64], silf[:, o + 64:o + 128], csl[:, 0:64],
                            OP.mult), r=("bsil", "cs_tm"), w=("brt2",))
                        vop(lambda E, o=o: E.tensor_tensor(
                            qtms[:, o + 64:o + 128], rt1[:, 0:64], rt2[:, 0:64],
                            OP.add), r=("brt1", "brt2"), w=("qtms",))
                    bank = pb[7]
                    for hb in range(4):
                        top(lambda E, hb=hb, bank=bank: E.transpose(
                            bank.bitcast(BF16)[:, hb * 128:(hb + 1) * 128],
                            qtms[:, hb * 128:(hb + 1) * 128], C["idb"]),
                            r=("qtms", "idb"), w=(PT[7],))
                    for hb in range(4):
                        vop(lambda E, hb=hb, nb=nb, t=t, bank=bank, dst=dst:
                            E.tensor_copy(
                            dst[:, (nb * 4 + hb) * T + t * 128:
                                (nb * 4 + hb) * T + t * 128 + 128],
                            bank.bitcast(BF16)[:, hb * 128:(hb + 1) * 128]),
                            r=(PT[7],), w=(dtok,))

        proj_rot4("wb_q", qf, "qbf")
        proj_rot4("wb_k", kf, "kbf")
        A.freebuf("brt2", "qtms")
        vtm = A.alloc("vtm", 128, NT * D, BF16)
        for nb in range(4):
            dma("gpsimd", wrhs, wdr["wb_v"][nb], w=("bwrhs",))
            for t in range(NT):
                bk = pb[4 + t % 2]
                for kc in range(KC):
                    top(lambda E, t=t, kc=kc, bk=bk: E.matmul(
                        bk[:, 0:512],
                        y3f[:, kc * T + t * 128: kc * T + t * 128 + 128],
                        wrhs[:, kc * 512:(kc + 1) * 512],
                        start=(kc == 0), stop=(kc == KC - 1)),
                        r=("bwrhs", "y3f"), w=(PT[4 + t % 2],))
                sop(lambda E, nb=nb, t=t, bk=bk: E.copy(
                    vtm[:, t * D + nb * 512: t * D + nb * 512 + 512],
                    bk[:, 0:512]), r=(PT[4 + t % 2],), w=("vtm",))
        dma("gpsimd", wblk, wdr["wb_g1"][0], w=("bwblk",))
        for half in range(2):
            bk = pb[4 + half]
            for kc in range(KC):
                top(lambda E, kc=kc, half=half, bk=bk: E.matmul(
                    bk[:, 0:512], wblk[:, kc * 128:(kc + 1) * 128],
                    y3f[:, kc * T + half * 512: kc * T + half * 512 + 512],
                    start=(kc == 0), stop=(kc == KC - 1)),
                    r=("bwblk", "y3f"), w=(PT[4 + half],))
            vop(lambda E, half=half, bk=bk: E.tensor_copy(
                g1f[:, half * 512:half * 512 + 512], bk[:, 0:512]),
                r=(PT[4 + half],), w=("g1f",))
        wg2s = A.alloc("wg2s", 128, D, BF16)
        dma("gpsimd", wg2s, wdr["wb_g2"], w=("wg2s",))
        tap("qbf", qf, ("qbf",))
        tap("kbf", kf, ("kbf",))
        tap("vtm", vtm, ("vtm",))
        tap("g1f", g1f, ("g1f",))
        A.freebuf("y3f", "bwrhs")
        Sf = A.alloc("Sf", 128, BH * 128, F32)
        Sall = A.alloc("Sall", 128, NT * BH * 128, BF16)
        khat = A.alloc("khat", 128, 128, BF16)
        lam = _alibi_lam()
        vop(lambda E: E.memset(Sf, 0.0), w=("Sf",))
        for c in range(NT):
            vop(lambda E, c=c: E.tensor_copy(
                Sall[:, c * BH * 128:(c + 1) * BH * 128], Sf),
                r=("Sf",), w=("Sall",))
            for h in range(BH):
                top(lambda E, c=c, h=h: E.transpose(
                    pb[7].bitcast(BF16)[:, 0:128],
                    kf[:, h * T + c * 128: h * T + c * 128 + 128], C["idb"]),
                    r=("kbf", "idb"), w=(PT[7],))
                vop(lambda E, h=h: E.tensor_scalar(
                    khat, pb[7].bitcast(BF16)[:, 0:128],
                    CBv["dkcol"][:, h:h + 1], None, OP.mult),
                    r=(PT[7], "dkcol"), w=("khat",))
                top(lambda E, c=c, h=h: E.matmul(
                    pb[6][:, 0:128], khat,
                    vtm[:, c * D + h * 128: c * D + h * 128 + 128],
                    start=True, stop=True), r=("khat", "vtm"), w=(PT[6],))
                lamC = math.exp(lam[h] * 128.0)
                vop(lambda E, h=h, lamC=lamC: E.scalar_tensor_tensor(
                    Sf[:, h * 128:(h + 1) * 128], Sf[:, h * 128:(h + 1) * 128],
                    lamC, pb[6][:, 0:128], OP.mult, OP.add),
                    r=(PT[6], "Sf"), w=("Sf",))
        tap("Sf8", Sf, ("Sf",))
        dma("sync", ccs_in.rearrange("(h d) e -> d h e", h=BH),
            Sf.rearrange("p (h e) -> p h e", h=BH), r=("Sf",), w=("ccsin",))
        s.add("gpsimd", lambda E: E.collective_compute(
            "AllGather", OP.bypass,
            replica_groups=[[0, 1], [2, 3], [4, 5], [6, 7]],
            ins=[ccs_in.opt()], outs=[ccs_out.opt()]),
            r=("ccsin",), w=("ccso",), ovr=True)
        srecv = A.alloc("srecv", 128, BH * 128, BF16)
        dma("gpsimd", srecv.rearrange("p (h e) -> p h e", h=BH),
            ccs_out[0:BH * 128].rearrange("(h d) e -> d h e", h=BH),
            r=("ccso",), w=("srecv",))
        tap("srecv", srecv, ("srecv",))
        tap("Sall", Sall, ("Sall",))
        A.freebuf("Sf")
        A.freebuf("khat")
        OT = [A.alloc("otm0", 128, D, BF16), A.alloc("otm1", 128, D, BF16)]
        PBB = [A.alloc("pbb0", 128, 128, BF16), A.alloc("pbb1", 128, 128, BF16)]
        QA = [A.alloc("qa0", 128, BH * 128, BF16),
              A.alloc("qa1", 128, BH * 128, BF16)]
        SRS = [A.alloc("srs0", 128, 128, BF16), A.alloc("srs1", 128, 128, BF16)]
        XNB = [A.alloc("xnb0", 128, D, BF16), A.alloc("xnb1", 128, D, BF16)]
        SIL = [silf[:, 0:512], rt1[:, 0:512]]
        msq = A.alloc("msq", 128, 128, F32)
        msall = A.alloc("msall", 128, 128, F32)
        rsall = A.alloc("rsall", 128, 128, F32)
        qf3 = qf.rearrange("p (h t) -> p h t", h=BH)
        ea3 = CBv["ea_rep"].rearrange("p (h t) -> p h t", h=BH)
        for c in range(NT):
            par = c % 2
            otc, ot_t = OT[par], f"otm{par}"
            xnc, xn_t = XNB[par], f"xnb{par}"
            qac, qa_t = QA[par], f"qa{par}"
            vop(lambda E, c=c, qac=qac: E.tensor_tensor(
                qac.rearrange("p (h t) -> p h t", h=BH),
                qf3[:, :, c * 128:(c + 1) * 128], ea3, OP.mult),
                r=("qbf", "ea_rep"), w=(qa_t,))
            for h in range(BH):
                hp = h % 2
                pbbc, pbb_t = PBB[hp], f"pbb{hp}"
                srsc, srs_t = SRS[hp], f"srs{hp}"
                sbk = pb[4 + hp]
                top(lambda E, c=c, h=h, sbk=sbk: E.matmul(
                    sbk[:, 0:128],
                    kf[:, h * T + c * 128: h * T + c * 128 + 128],
                    qf[:, h * T + c * 128: h * T + c * 128 + 128],
                    start=True, stop=True), r=("kbf", "qbf"),
                    w=(PT[4 + hp],))
                vop(lambda E, h=h, sbk=sbk, pbbc=pbbc: E.tensor_tensor(
                    pbbc, sbk[:, 0:128],
                    CBv["dt_tab"][:, h * 128:(h + 1) * 128], OP.mult),
                    r=(PT[4 + hp], "dt_tab"), w=(pbb_t,))
                vop(lambda E, c=c, h=h, srsc=srsc: E.tensor_scalar(
                    srsc, srecv[:, h * 128:(h + 1) * 128],
                    CBv["ebcol"][:, h * NT + c: h * NT + c + 1], None,
                    OP.mult), r=("srecv", "ebcol"), w=(srs_t,))
                ob = pb[h % 4]
                top(lambda E, c=c, h=h, ob=ob, pbbc=pbbc: E.matmul(
                    ob[:, 0:128], pbbc,
                    vtm[:, c * D + h * 128: c * D + h * 128 + 128],
                    start=True, stop=False), r=(pbb_t, "vtm"), w=(PT[h % 4],))
                top(lambda E, c=c, h=h, ob=ob, qac=qac: E.matmul(
                    ob[:, 0:128], qac[:, h * 128:(h + 1) * 128],
                    Sall[:, (c * BH + h) * 128: (c * BH + h) * 128 + 128],
                    start=False, stop=False), r=(qa_t, "Sall"), w=(PT[h % 4],))
                top(lambda E, ob=ob, qac=qac, h=h, srsc=srsc: E.matmul(
                    ob[:, 0:128], qac[:, h * 128:(h + 1) * 128], srsc,
                    start=False, stop=True),
                    r=(qa_t, srs_t), w=(PT[h % 4],))
                sop(lambda E, h=h, ob=ob, otc=otc: E.copy(
                    otc[:, h * 128:(h + 1) * 128], ob[:, 0:128]),
                    r=(PT[h % 4],), w=(ot_t,))
            for nb in range(4):
                gk = pb[6 + nb % 2]
                sl = SIL[nb % 2]
                slt = ("bsil", "brt1")[nb % 2]
                top(lambda E, c=c, nb=nb, gk=gk: E.matmul(
                    gk[:, 0:512], g1f[:, c * 128:(c + 1) * 128],
                    wg2s[:, nb * 512:(nb + 1) * 512], start=True, stop=True),
                    r=("g1f", "wg2s"), w=(PT[6 + nb % 2],))
                sop(lambda E, gk=gk, sl=sl: E.activation(sl, gk[:, 0:512],
                                                         ACT.Sigmoid),
                    r=(PT[6 + nb % 2],), w=(slt,))
                vop(lambda E, nb=nb, sl=sl, xnc=xnc, otc=otc: E.tensor_tensor(
                    xnc[:, nb * 512:(nb + 1) * 512],
                    otc[:, nb * 512:(nb + 1) * 512], sl, OP.mult),
                    r=(ot_t, slt), w=(xn_t,))
            for h in range(BH):
                vop(lambda E, h=h, xnc=xnc: E.tensor_tensor(
                    msq, xnc[:, h * 128:(h + 1) * 128],
                    xnc[:, h * 128:(h + 1) * 128], OP.mult),
                    r=(xn_t,), w=("msq",))
                vop(lambda E, c=c, h=h: E.tensor_reduce(
                    msall[:, c * 16 + h:c * 16 + h + 1], msq, AX, OP.add),
                    r=("msq",), w=("msall",))
            dma("sync", xg_d[c], xnc, r=(xn_t,), w=("xgsp",))
        vop(lambda E: E.tensor_scalar(msall, msall, 1.0 / 128.0, B_EPS,
                                      OP.mult, OP.add), r=("msall",),
            w=("msall",))
        sop(lambda E: E.activation(rsall, msall, ACT.Sqrt),
            r=("msall",), w=("rsall",))
        vop(lambda E: E.reciprocal(rsall, rsall), r=("rsall",), w=("rsall",))
        A.freebuf("qbf", "kbf", "vtm", "g1f", "bsil", "brt1", "msq",
                  "Sall", "srecv", "otm0", "otm1", "pbb0", "pbb1",
                  "qa0", "qa1", "srs0", "srs1", "xnb0", "xnb1", "bwblk",
                  "wg2s", "msall")
        tap("xgd", xg_d, ("xgsp",))
        Xh["v"] = A.alloc("X", 128, NT * D, F32, toks=XTOKS)
        zf = A.alloc("zfb", 128, KC * T, BF16)
        xnr0 = A.alloc("xnr0", 128, D, BF16)
        xnr1 = A.alloc("xnr1", 128, D, BF16)
        WO = [A.alloc(f"bwo{i}", 128, KC * 512, BF16) for i in range(4)]
        XN = [xnr0, xnr1]
        for t in range(NT):
            dma("sync", X()[:, t * D:(t + 1) * D], xsp_d[t],
                r=("xspill",), w=(f"x{t}",))
        for c in range(NT):
            xb = XN[c % 2]
            xt = f"xnr{c % 2}"
            dma("sync", xb, xg_d[c], r=("xgsp",), w=(xt,))
            for h in range(BH):
                vop(lambda E, h=h, c=c, xb=xb: E.tensor_scalar(
                    xb[:, h * 128:(h + 1) * 128], xb[:, h * 128:(h + 1) * 128],
                    rsall[:, c * 16 + h:c * 16 + h + 1], None, OP.mult),
                    r=(xt, "rsall"), w=(xt,))
            tm_to_fm(xb, (xt,), zf, lambda ch, c=c: ch * T + c * 128,
                     "zfb", 7, nwc=nwcol)
        tap("zfb", zf, ("zfb",))
        for nb in range(4):
            dma("gpsimd", WO[nb], wdr["wb_o"][nb], w=(f"bwo{nb}",))
        for t in range(NT):
            bs = 4 * (t % 2)
            for kc in range(KC):
                for nb in range(4):
                    top(lambda E, t=t, kc=kc, nb=nb, bs=bs: E.matmul(
                        pb[bs + nb][:, 0:512],
                        zf[:, kc * T + t * 128: kc * T + t * 128 + 128],
                        WO[nb][:, kc * 512:(kc + 1) * 512],
                        start=(kc == 0), stop=(kc == KC - 1)),
                        r=(f"bwo{nb}", "zfb"), w=(PT[bs + nb],))
            for nb in range(4):
                vop(lambda E, nb=nb, t=t, bs=bs: E.tensor_tensor(
                    X()[:, t * D + nb * 512: t * D + nb * 512 + 512],
                    X()[:, t * D + nb * 512: t * D + nb * 512 + 512],
                    pb[bs + nb][:, 0:512], OP.add),
                    r=(PT[bs + nb], f"x{t}"), w=(f"x{t}",))
        A.freebuf("zfb", "xnr0", "xnr1", "bwo0", "bwo1", "bwo2", "bwo3",
                  "nwcol", "rsall")
        for k in _CB:
            A.freebuf(k)

    if n_sub >= 1:
        build_attn_a()
    if n_sub >= 2:
        build_ffn(0)
    if n_sub >= 3:
        build_attn_b()
    if n_sub >= 4:
        build_ffn(1)

    for t in range(NT):
        dma("sync", out_d[t], X()[:, t * D:(t + 1) * D], r=(f"x{t}",),
            w=("outd",))
    vop(lambda E: E.memset(smv(255), 0.0), r=("outd",))

    stats = s.finalize(nc, sems, ovr_sems)
    stats["sbuf_peak"] = A.peak
    stk.close()
    _NC_CACHE[n_sub] = (nc, stats)
    return nc, stats


def _host_weights(inputs):
    def arr(k):
        return np.ascontiguousarray(np.asarray(inputs[k], dtype=np.float32))
    w = {}
    w["wa_q"] = _prep_lhsT(arr("L0_attn_a_Wq").T, 128)
    w["wa_k"] = _prep_lhsT(arr("L0_attn_a_Wk").T, 128)
    w["wa_v"] = _prep_lhsT(arr("L0_attn_a_Wv").T, 512)
    w["wa_o"] = _prep_lhsT(arr("L0_attn_a_Wo").T, 512)
    w["wb_q"] = _prep_lhsT(arr("L0_attn_b_Wq"), 512)
    w["wb_k"] = _prep_lhsT(arr("L0_attn_b_Wk"), 512)
    w["wb_v"] = _prep_lhsT(arr("L0_attn_b_Wv"), 512)
    w["wb_o"] = _prep_lhsT(arr("L0_attn_b_Wo"), 512)
    w["wb_g1"] = _prep_lhsT(arr("L0_attn_b_Wg1"), 128)
    w["wb_g2"] = arr("L0_attn_b_Wg2")
    wu = _prep_lhsT(arr("L0_ffn_w_up").T, 128)
    wdT = arr("L0_ffn_w_down").T
    a = wdT.reshape(32, 128, 4, 512)
    sub = np.empty((4, 4, 128, 8 * 512), np.float32)
    for nb in range(4):
        for kg in range(4):
            for kk in range(8):
                sub[nb, kg, :, kk * 512:(kk + 1) * 512] = a[8 * kg + kk, :, nb, :]
    wd = np.ascontiguousarray(sub.reshape(16, 128, I_FFN))
    return w, wu, wd


def kernel(**inputs):
    nc, stats = build_nc(4)
    x = np.ascontiguousarray(np.asarray(inputs["L0_attn_a_hidden_states"]),
                             dtype=np.float32)
    cu = np.asarray(inputs["L0_attn_a_cu_seqlens_q"],
                    dtype=np.int32).reshape(1, 3)
    w, wu, wd = _host_weights(inputs)
    nw = np.ascontiguousarray(np.asarray(inputs["L0_attn_b_norm_weight"]),
                              dtype=np.float32).reshape(1, D)
    in_maps = []
    for core in range(8):
        b, hh = core // 2, core % 2
        m = dict(w)
        m["cu"] = cu
        m["nwrow"] = nw
        m["xo"] = np.ascontiguousarray(
            x[b, hh * 1024:(hh + 1) * 1024].reshape(NT, 128, D))
        m["xp"] = np.ascontiguousarray(
            x[b, (1 - hh) * 1024:(2 - hh) * 1024].reshape(NT, 128, D))
        m["wua"] = np.ascontiguousarray(wu[4 * core:4 * core + 4])
        m["wda"] = np.ascontiguousarray(wd[2 * core:2 * core + 2])
        m.update(host_constants(core))
        in_maps.append(m)
    res = run_bass_kernel_spmd(nc, in_maps, core_ids=list(range(8)))
    out = np.empty((4, 2048, D), np.float32)
    for core in range(8):
        b, hh = core // 2, core % 2
        out[b, hh * 1024:(hh + 1) * 1024] = \
            res.results[core]["out"].reshape(1024, D)
    return out


# version 03
# speedup vs baseline: 1.3235x; 1.2260x over previous
import contextlib
import math
import numpy as np
import ml_dtypes
import concourse.bass as bass
import concourse.mybir as mybir
from concourse.bass_utils import run_bass_kernel_spmd

F32 = mybir.dt.float32
BF16 = mybir.dt.bfloat16
I32 = mybir.dt.int32
U8 = mybir.dt.uint8
F32R = mybir.dt.float32r
OP = mybir.AluOpType
ACT = mybir.ActivationFunctionType
AX = mybir.AxisListType.X
BF = ml_dtypes.bfloat16

D = 2048
KC = 16
T = 1024
NT = 8
MAGIC = 1.5 * 2.0 ** 23
BH, BHD = 16, 128
B_BASE = 10000.0
B_EPS = 1e-6
LN_EPS = 1e-5
FFN_EPS = 1e-5
I_FFN = 4096
W_CNT = float(I_FFN * D)
_ESZ = {F32: 4, BF16: 2, F32R: 4, I32: 4, U8: 1}


class Sched:
    ENG = ("tensor", "vector", "scalar", "sync", "gpsimd")

    def __init__(self):
        self.recs = []
        self.lastw = {}
        self.readers = {}
        self.n_ovr = 0
        self.alias_pending = {}

    def alias(self, tok, olds):
        self.alias_pending.setdefault(tok, [])
        self.alias_pending[tok].extend(olds)

    def add(self, eng, emit, r=(), w=(), dma=False, ovr=False):
        w = list(w)
        for t in list(w):
            if t in self.alias_pending:
                w.extend(self.alias_pending.pop(t))
        idx = len(self.recs)
        deps = set()
        for t in r:
            j = self.lastw.get(t)
            if j is not None:
                deps.add(j)
        for t in w:
            j = self.lastw.get(t)
            if j is not None:
                deps.add(j)
            for j in self.readers.get(t, ()):
                deps.add(j)
        ovr_id = None
        if ovr:
            ovr_id = self.n_ovr
            self.n_ovr += 1
        self.recs.append({"idx": idx, "eng": eng, "emit": emit, "deps": deps,
                          "dma": dma, "inc": 16 if dma else 1, "ovr": ovr_id})
        for t in r:
            self.readers.setdefault(t, []).append(idx)
        for t in w:
            self.lastw[t] = idx
            self.readers[t] = []
        return idx

    def finalize(self, nc, sems, ovr_sems):
        recs = self.recs
        for rc in recs:
            rc["xdeps"] = [d for d in rc["deps"]
                           if recs[d]["eng"] != rc["eng"]
                           or recs[d]["dma"] or recs[d]["ovr"] is not None
                           or rc["eng"] in ("vector", "scalar")]
        need = set()
        for rc in recs:
            if rc["dma"]:
                need.add(rc["idx"])
            for d in rc["xdeps"]:
                need.add(d)
        cum = {}
        for e in self.ENG:
            c = 0
            for rc in recs:
                if rc["eng"] != e or rc["ovr"] is not None:
                    continue
                if rc["idx"] in need:
                    c += rc["inc"]
                cum[rc["idx"]] = c
        per = {e: [rc for rc in recs if rc["eng"] == e] for e in self.ENG}

        def runner(en):
            def body(eng):
                hw = {e: 0 for e in self.ENG}
                ohw = set()
                for rc in per[en]:
                    for d in sorted(rc["xdeps"]):
                        dr = recs[d]
                        if dr["ovr"] is not None:
                            if dr["ovr"] not in ohw:
                                eng.wait_ge(ovr_sems[dr["ovr"]], 1)
                                ohw.add(dr["ovr"])
                        else:
                            v = cum[d]
                            if v > hw[dr["eng"]]:
                                eng.wait_ge(sems[dr["eng"]], v)
                                hw[dr["eng"]] = v
                    inst = rc["emit"](eng)
                    if rc["ovr"] is not None:
                        inst.then_inc(ovr_sems[rc["ovr"]], 1)
                    elif rc["idx"] in need:
                        inst.then_inc(sems[en], rc["inc"])
            return body

        g = nc.Block()
        blk = g.__enter__()
        blk.tensor(runner("tensor"))
        blk.vector(runner("vector"))
        blk.scalar(runner("scalar"))
        blk.sync(runner("sync"))
        blk.gpsimd(runner("gpsimd"))
        g.__exit__(None, None, None)
        return {"n_ops": len(recs), "n_incs": len(need),
                "n_waits": sum(len(rc["xdeps"]) for rc in recs)}


class Arena:
    def __init__(self, ap, nbytes, sched):
        self.ap = ap
        self.nbytes = nbytes
        self.free = [(0, nbytes)]
        self.live = {}
        self.owners = []
        self.s = sched
        self.peak = 0

    def alloc(self, name, parts, freedim, dt, align=64, toks=None):
        toks = tuple(toks) if toks else (name,)
        nb = freedim * _ESZ[dt]
        nba = (nb + align - 1) // align * align
        for i, (off, sz) in enumerate(self.free):
            o = (off + align - 1) // align * align
            if o + nba <= off + sz:
                repl = [p for p in ((off, o - off), (o + nba, off + sz - o - nba))
                        if p[1] > 0]
                self.free[i:i + 1] = repl
                self.live[name] = (o, nba, toks)
                used = sum(x[1] for x in self.live.values())
                self.peak = max(self.peak, used)
                olds = []
                for (fo, fs, tks) in self.owners:
                    if fo < o + nba and o < fo + fs:
                        olds.extend(tks)
                if olds:
                    self.owners = [x for x in self.owners
                                   if not (x[0] < o + nba and o < x[0] + x[1])]
                    for tk in toks:
                        self.s.alias(tk, olds)
                return self.ap[0:parts, o:o + nb].bitcast(dt)
        raise MemoryError(f"arena: no room for {name} ({nba}B) free={self.free}")

    def freebuf(self, *names):
        for name in names:
            off, sz, toks = self.live.pop(name)
            self.free.append((off, sz))
            self.owners.append((off, sz, toks))
        self.free.sort()
        m = []
        for off, sz in self.free:
            if m and m[-1][0] + m[-1][1] == off:
                m[-1] = (m[-1][0], m[-1][1] + sz)
            else:
                m.append([off, sz])
        self.free = [tuple(x) for x in m]


def _prep_lhsT(w_io, out_block=128):
    In, Out = w_io.shape
    kc = In // 128
    nb = Out // out_block
    a = w_io.reshape(kc, 128, nb, out_block).transpose(2, 1, 0, 3)
    return np.ascontiguousarray(a.reshape(nb, 128, kc * out_block),
                                dtype=np.float32)


def _alibi_lam():
    return -(2.0 ** (-8.0 * (np.arange(BH, dtype=np.float64) + 1.0) / BH))


def host_constants(core):
    hh = core % 2
    H = 1024 * hh
    c = {}
    c["qpos"] = (H + np.arange(T, dtype=np.float32)).reshape(1, T)
    kpos = (H - 1024 + np.arange(2048, dtype=np.float32)).reshape(1, 2048)
    c["kposr"] = kpos
    c["invalidr"] = (2.0 * (kpos < 0)).astype(np.float32)
    c["idb"] = np.eye(128, dtype=np.float32).astype(BF)
    c["idf"] = np.eye(128, dtype=np.float32)
    d = np.arange(128)
    c["diagm"] = (d[None, :] >= d[:, None]).astype(BF)
    c["onesb"] = np.ones((1, 128), BF)
    c["onesf"] = np.ones((1, 128), np.float32)
    lam = _alibi_lam()
    i = np.arange(128, dtype=np.float64)
    dt_tab = np.zeros((128, BH * 128), np.float32)
    ea = np.zeros((128, BH * 128), np.float32)
    eb = np.zeros((128, BH * NT), np.float32)
    dk = np.zeros((128, BH), np.float32)
    for h in range(BH):
        m = np.exp(lam[h] * (i[None, :] - i[:, None])) * (i[None, :] >= i[:, None])
        dt_tab[:, h * 128:(h + 1) * 128] = m.astype(np.float32)
        ea[:, h * 128:(h + 1) * 128] = \
            np.exp(lam[h] * (i + 1.0))[None, :].astype(np.float32)
        if hh == 1:
            for cc in range(NT):
                eb[:, h * NT + cc] = math.exp(lam[h] * 128.0 * cc)
        dk[:, h] = np.exp(lam[h] * (127.0 - i)).astype(np.float32)
    c["dt_tab"] = dt_tab
    c["ea_rep"] = ea
    c["ebcol"] = eb
    c["dkcol"] = dk
    j = np.arange(BHD // 2, dtype=np.float64)
    theta = B_BASE ** (-2.0 * j / BHD)
    cs = np.zeros((128, NT * 128), np.float32)
    for t in range(NT):
        pos = H + t * 128 + np.arange(128, dtype=np.float64)
        ang = pos[:, None] * theta[None, :]
        cs[:, t * 128:t * 128 + 64] = np.cos(ang).astype(np.float32)
        cs[:, t * 128 + 64:t * 128 + 128] = np.sin(ang).astype(np.float32)
    c["cs_tm"] = cs
    return c


_C0 = {"qpos": (1, T), "kposr": (1, 2048), "invalidr": (1, 2048),
       "idb": (128, 128), "idf": (128, 128), "diagm": (128, 128),
       "onesb": (1, 128), "onesf": (1, 128)}
_CB = {"dt_tab": (128, BH * 128), "ea_rep": (128, BH * 128),
       "ebcol": (128, BH * NT), "dkcol": (128, BH), "cs_tm": (128, NT * 128)}
_CDT = {"idb": BF16, "diagm": BF16, "onesb": BF16}

_NC_CACHE = {}
DBG_TAPS = []


def build_nc(n_sub=4):
    if n_sub in _NC_CACHE:
        return _NC_CACHE[n_sub]
    nc = bass.Bass("TRN2", target_bir_lowering=False)
    s = Sched()

    def din(name, shape, dt):
        return nc.dram_tensor(name, list(shape), dt, kind="ExternalInput").ap()

    xo_d = din("xo", (NT, 128, D), F32)
    xp_d = din("xp", (NT, 128, D), F32)
    cu_d = din("cu", (1, 3), I32)
    nw_d = din("nwrow", (1, D), F32)
    out_d = nc.dram_tensor("out", [NT, 128, D], F32, kind="ExternalOutput").ap()
    cdr = {k: din(k, sh, _CDT.get(k, F32)) for k, sh in {**_C0, **_CB}.items()}
    wdr = {
        "wa_q": din("wa_q", (16, 128, D), F32),
        "wa_k": din("wa_k", (4, 128, D), F32),
        "wa_v": din("wa_v", (1, 128, KC * 512), F32),
        "wa_o": din("wa_o", (4, 128, KC * 512), F32),
        "wb_q": din("wb_q", (4, 128, KC * 512), F32),
        "wb_k": din("wb_k", (4, 128, KC * 512), F32),
        "wb_v": din("wb_v", (4, 128, KC * 512), F32),
        "wb_o": din("wb_o", (4, 128, KC * 512), F32),
        "wb_g1": din("wb_g1", (1, 128, D), F32),
        "wb_g2": din("wb_g2", (128, D), F32),
        "wua": din("wua", (4, 128, D), F32),
        "wda": din("wda", (2, 128, I_FFN), F32),
    }
    hsp_d = nc.dram_tensor("hsp", [32, 128, T], F32).ap()
    xsp_d = nc.dram_tensor("xsp", [NT, 128, D], F32).ap()
    xg_d = nc.dram_tensor("xgsp", [NT, 128, D], BF16).ap()
    ar_in = nc.dram_tensor("ar_in", [1, 2], F32).ap()
    ar_out = nc.dram_tensor("ar_out", [1, 2], F32).ap()
    agu_in = nc.dram_tensor("agu_in", [4, 128, D], BF16).ap()
    agu_out = nc.dram_tensor("agu_out", [32, 128, D], BF16).ap()
    agd_in = nc.dram_tensor("agd_in", [2, 128, I_FFN], BF16).ap()
    agd_out = nc.dram_tensor("agd_out", [16, 128, I_FFN], BF16).ap()
    ccs_in = nc.dram_tensor("ccs_in", [BH * 128, 128], F32).ap()
    ccs_out = nc.dram_tensor("ccs_out", [2 * BH * 128, 128], F32).ap()

    total = (nc.sbuf_bytes_remaining - 512) // 64 * 64
    A = Arena(nc.alloc_sbuf_tensor("arena", [128, total], U8).ap(), total, s)
    pb = [nc.alloc_psum_tensor(f"pb{i}", [128, 512], F32).ap() for i in range(8)]
    PT = [f"ps{i}" for i in range(8)]

    stk = contextlib.ExitStack()
    sems = {e: stk.enter_context(nc.semaphore(f"sem_{e}")) for e in Sched.ENG}
    ovr_sems = [stk.enter_context(nc.semaphore(f"ovr{i}")) for i in range(4)]

    Xh = {}
    XTOKS = tuple(f"x{t}" for t in range(NT))
    Xh["v"] = A.alloc("X", 128, NT * D, F32, toks=XTOKS)
    C = {k: A.alloc(k, sh[0], sh[1], _CDT.get(k, F32)) for k, sh in _C0.items()}
    cu_s = A.alloc("cu", 1, 4, I32)
    smal = A.alloc("smal", 128, 256, F32)
    bn6 = A.alloc("bn6", 128, 24, F32)
    lnsc = A.alloc("lnsc", 128, 16, F32)
    sq_rep = A.alloc("sq_rep", 128, T, BF16)
    skcol = A.alloc("skcol", 128, 16, F32)
    sqrow_b = A.alloc("sqrow_b", 1, T, BF16)
    skprow = A.alloc("skprow", 1, 2048, F32)
    sval = A.alloc("sval", 1, 8, F32)
    svrep = A.alloc("svrep", 128, 4, F32)
    su127 = A.alloc("su127", 1, 128, F32)
    brow = A.alloc("brow", 1, 1024, F32)
    m1t = A.alloc("m1t", 8, 128, F32)
    wpart = A.alloc("wpart", 2, 1, F32)
    onesp = A.alloc("onesp", 128, 128, F32)

    def smv(off, n=1):
        return smal[:, off:off + n]

    def X():
        return Xh["v"]

    def dma(eng, out, in_, r=(), w=()):
        s.add(eng, lambda E, o=out, i=in_: E.dma_start(out=o, in_=i),
              r=r, w=w, dma=True)

    def vop(fn, r=(), w=()):
        s.add("vector", fn, r=r, w=w)

    def sop(fn, r=(), w=()):
        s.add("scalar", fn, r=r, w=w)

    def top(fn, r=(), w=()):
        s.add("tensor", fn, r=r, w=w)

    def tap(name, ap, rtoks):
        if name not in DBG_TAPS:
            return
        sh = list(ap.shape)
        dbg = nc.dram_tensor("dbg_" + name, sh, ap.dtype,
                             kind="ExternalOutput").ap()
        dma("sync", dbg, ap, r=tuple(rtoks), w=("dbg",))

    for k in _C0:
        dma("sync", C[k], cdr[k], w=(k,))
    dma("sync", cu_s[0:1, 0:3], cu_d, w=("cu",))
    for t in range(NT):
        dma("sync", X()[:, t * D:(t + 1) * D], xo_d[t], w=(f"x{t}",))
    vop(lambda E: E.memset(onesp, 1.0), w=("onesp",))

    vop(lambda E: E.tensor_copy(smal[0:1, 200:203], cu_s[0:1, 0:3]),
        r=("cu",), w=("l0",))
    vop(lambda E: E.tensor_tensor(smal[0:1, 204:205], smal[0:1, 201:202],
                                  smal[0:1, 200:201], OP.subtract),
        r=("l0",), w=("l0",))
    vop(lambda E: E.tensor_scalar(sqrow_b[0:1, :], C["qpos"][0:1, :],
                                  smal[0:1, 204:205], None, OP.is_ge),
        r=("l0", "qpos"), w=("sqrow",))
    vop(lambda E: E.tensor_scalar(skprow[0:1, :], C["kposr"][0:1, :],
                                  smal[0:1, 204:205], None, OP.is_ge),
        r=("l0", "kposr"), w=("skprow",))
    vop(lambda E: E.tensor_tensor(skprow[0:1, :], skprow[0:1, :],
                                  C["invalidr"][0:1, :], OP.add),
        r=("invalidr", "skprow"), w=("skprow",))
    for half in range(2):
        top(lambda E, h=half: E.matmul(
            pb[6][:, 0:512], C["onesb"][0:1, :],
            sqrow_b[0:1, h * 512:(h + 1) * 512], start=True, stop=True),
            r=("sqrow", "onesb"), w=(PT[6],))
        vop(lambda E, h=half: E.tensor_copy(sq_rep[:, h * 512:(h + 1) * 512],
                                            pb[6][:, 0:512]),
            r=(PT[6],), w=("sq_rep",))
    for m in range(16):
        top(lambda E, m=m: E.transpose(pb[6][:, 0:1],
                                       skprow[0:1, m * 128:(m + 1) * 128],
                                       C["idf"][0:1, 0:1]),
            r=("skprow", "idf"), w=(PT[6],))
        vop(lambda E, m=m: E.tensor_copy(skcol[:, m:m + 1], pb[6][:, 0:1]),
            r=(PT[6],), w=("skcol",))

    tap("sq_rep", sq_rep, ("sq_rep",))
    tap("skcol", skcol, ("skcol",))
    tap("l0", smal[0:1, 200:208], ("l0",))
    A.freebuf("qpos", "kposr", "invalidr", "skprow", "sqrow_b")

    def ln_stats(xa, rtok):
        for q in range(4):
            vop(lambda E, q=q: E.bn_stats(bn6[:, q * 6:(q + 1) * 6],
                                          xa[:, q * 512:(q + 1) * 512]),
                r=rtok, w=("bn6",))
        vop(lambda E: E.tensor_reduce(lnsc[:, 0:1], bn6[:, 1:24:3], AX, OP.add),
            r=("bn6",), w=("lnsc",))
        vop(lambda E: E.tensor_reduce(lnsc[:, 1:2], bn6[:, 2:24:3], AX, OP.add),
            r=("bn6",), w=("lnsc",))
        vop(lambda E: E.tensor_tensor(lnsc[:, 2:10], bn6[:, 1:24:3],
                                      bn6[:, 1:24:3], OP.mult),
            r=("bn6",), w=("lnsc",))
        vop(lambda E: E.tensor_reduce(lnsc[:, 10:11], lnsc[:, 2:10], AX, OP.add),
            r=("lnsc",), w=("lnsc",))
        vop(lambda E: E.tensor_scalar(lnsc[:, 0:1], lnsc[:, 0:1], 0.125, None,
                                      OP.mult), r=("lnsc",), w=("lnsc",))
        vop(lambda E: E.tensor_tensor(lnsc[:, 3:4], lnsc[:, 0:1], lnsc[:, 0:1],
                                      OP.mult), r=("lnsc",), w=("lnsc",))
        vop(lambda E: E.tensor_scalar(lnsc[:, 1:2], lnsc[:, 1:2], 1.0 / 2048.0,
                                      None, OP.mult), r=("lnsc",), w=("lnsc",))
        vop(lambda E: E.tensor_scalar(lnsc[:, 10:11], lnsc[:, 10:11], 0.125, None,
                                      OP.mult), r=("lnsc",), w=("lnsc",))
        vop(lambda E: E.tensor_tensor(lnsc[:, 1:2], lnsc[:, 1:2], lnsc[:, 10:11],
                                      OP.add), r=("lnsc",), w=("lnsc",))
        vop(lambda E: E.tensor_tensor(lnsc[:, 1:2], lnsc[:, 1:2], lnsc[:, 3:4],
                                      OP.subtract), r=("lnsc",), w=("lnsc",))
        vop(lambda E: E.tensor_scalar(lnsc[:, 1:2], lnsc[:, 1:2], LN_EPS, None,
                                      OP.add), r=("lnsc",), w=("lnsc",))
        sop(lambda E: E.activation(lnsc[:, 4:5], lnsc[:, 1:2], ACT.Sqrt),
            r=("lnsc",), w=("lnsq",))
        vop(lambda E: E.reciprocal(smv(64), lnsc[:, 4:5]), r=("lnsq",), w=("lnA",))
        vop(lambda E: E.tensor_tensor(smv(80), lnsc[:, 0:1], smv(64), OP.mult),
            r=("lnA", "lnsc"), w=("lnB",))
        vop(lambda E: E.tensor_scalar(smv(80), smv(80), -1.0, None, OP.mult),
            r=("lnB",), w=("lnB",))

    def tm_to_fm(src_bf, rtoks, dst_fm, colf, wtok, bank_i=7, nwc=None):
        for g in range(4):
            bank = pb[bank_i - (g % 2)]
            bt = PT[bank_i - (g % 2)]
            for kk in range(4):
                c = g * 4 + kk
                top(lambda E, c=c, kk=kk, bank=bank: E.transpose(
                    bank.bitcast(BF16)[:, kk * 128:(kk + 1) * 128],
                    src_bf[:, c * 128:(c + 1) * 128], C["idb"]),
                    r=tuple(rtoks) + ("idb",), w=(bt,))
            for kk in range(4):
                c = g * 4 + kk
                col = colf(c)
                if nwc is None:
                    sop(lambda E, kk=kk, col=col, bank=bank: E.copy(
                        dst_fm[:, col:col + 128],
                        bank.bitcast(BF16)[:, kk * 128:(kk + 1) * 128]),
                        r=(bt,), w=(wtok,))
                else:
                    sop(lambda E, kk=kk, col=col, c=c, bank=bank: E.mul(
                        dst_fm[:, col:col + 128],
                        bank.bitcast(BF16)[:, kk * 128:(kk + 1) * 128],
                        nwc[:, c:c + 1]),
                        r=(bt, "nwcol"), w=(wtok,))

    wps = A.alloc("wps", 128, 2048, F32, toks=("wpsb0", "wl2"))
    wps2 = A.alloc("wps2", 128, 2048, F32, toks=("wpsb1", "tern"))
    gout = A.alloc("gout", 128, 2048, BF16)
    vop(lambda E: E.memset(smv(210, 2), 0.0), w=("wsum",))
    slices = [(wdr["wua"][j], 210, f"wl{j}") for j in range(4)]
    slices += [(wdr["wda"][j][:, hf * 2048:(hf + 1) * 2048], 211,
                f"wl{4 + 2 * j + hf}") for j in range(2) for hf in range(2)]
    for j, (src, acc, tk) in enumerate(slices):
        buf = wps if j % 2 == 0 else wps2
        bt = f"wpsb{j % 2}"
        dma("sync", buf, src, w=(bt,))
        vop(lambda E, b=buf: E.tensor_reduce(smv(212), b, AX, OP.add,
                                             apply_absolute_value=True),
            r=(bt,), w=("wred",))
        vop(lambda E, acc=acc: E.tensor_tensor(smv(acc), smv(acc), smv(212),
                                               OP.add),
            r=("wred", "wsum"), w=("wsum",))
    vop(lambda E: E.memset(smv(213), 1.0), w=("wone",))
    top(lambda E: E.matmul(pb[6][0:2, 0:1], smal[:, 210:212], smal[:, 213:214],
                           start=True, stop=True),
        r=("wsum", "wone"), w=(PT[6],))
    vop(lambda E: E.tensor_copy(wpart[0:2, 0:1], pb[6][0:2, 0:1]),
        r=(PT[6],), w=("wpartt",))
    dma("sync", ar_in, wpart[0:2, 0:1], r=("wpartt",), w=("arin",))
    s.add("gpsimd", lambda E: E.collective_compute(
        "AllReduce", OP.add, replica_groups=[list(range(8))],
        ins=[ar_in.opt()], outs=[ar_out.opt()]),
        r=("arin",), w=("arout",), ovr=True)
    dma("sync", sval[0:1, 0:2], ar_out, r=("arout",), w=("sval",))
    vop(lambda E: E.tensor_scalar(sval[0:1, 0:2], sval[0:1, 0:2], 1.0 / W_CNT,
                                  FFN_EPS, OP.mult, OP.max),
        r=("sval",), w=("sval",))
    vop(lambda E: E.reciprocal(sval[0:1, 2:4], sval[0:1, 0:2]),
        r=("sval",), w=("sval",))
    top(lambda E: E.matmul(pb[6][:, 0:4], C["onesf"][0:1, :], sval[0:1, 0:4],
                           start=True, stop=True),
        r=("sval", "onesf"), w=(PT[6],))
    vop(lambda E: E.tensor_copy(svrep[:, 0:4], pb[6][:, 0:4]),
        r=(PT[6],), w=("svrep",))
    for j, (src, acc, tk) in enumerate(slices):
        rcol = svrep[:, 2:3] if acc == 210 else svrep[:, 3:4]
        dma("sync", wps, src, w=("wl2",))
        vop(lambda E, rcol=rcol: E.tensor_scalar(wps2, wps, rcol, MAGIC,
                                                 OP.mult, OP.add),
            r=("wl2", "svrep"), w=("tern",))
        vop(lambda E: E.tensor_scalar(wps2, wps2, -MAGIC, -1.0, OP.add, OP.max),
            r=("tern",), w=("tern",))
        vop(lambda E: E.tensor_scalar(gout, wps2, 1.0, None, OP.min),
            r=("tern",), w=("gout",))
        if j < 4:
            dma("sync", agu_in[j], gout, r=("gout",), w=("aguin",))
        else:
            jj = j - 4
            dma("sync", agd_in[jj // 2][:, (jj % 2) * 2048:(jj % 2 + 1) * 2048],
                gout, r=("gout",), w=("agdin",))
    s.add("gpsimd", lambda E: E.collective_compute(
        "AllGather", OP.bypass, replica_groups=[list(range(8))],
        ins=[agu_in.opt()], outs=[agu_out.opt()]),
        r=("aguin",), w=("agu",), ovr=True)
    s.add("gpsimd", lambda E: E.collective_compute(
        "AllGather", OP.bypass, replica_groups=[list(range(8))],
        ins=[agd_in.opt()], outs=[agd_out.opt()]),
        r=("agdin",), w=("agd",), ovr=True)
    tap("gup", agu_out, ("agu",))
    tap("gdn", agd_out, ("agd",))
    tap("sval", sval[0:1, 0:4], ("sval",))
    A.freebuf("wps", "wps2", "gout")

    def build_attn_a():
        k2 = A.alloc("k2", 128, 4 * 2048, BF16)
        vv = A.alloc("vv", 128, 16 * 520, BF16)
        wblk0 = A.alloc("awb0", 128, D, BF16)
        wblk1 = A.alloc("awb1", 128, D, BF16)
        wvs = A.alloc("awvs", 128, KC * 512, BF16)
        ytmp = A.alloc("aytmp", 128, D, BF16)
        xpt0 = A.alloc("xpt0", 128, D, F32)
        xpt1 = A.alloc("xpt1", 128, D, F32)
        y1p = A.alloc("y1p", 128, KC * 512, BF16)
        WB = [wblk0, wblk1]
        XP = [xpt0, xpt1]
        vop(lambda E: E.memset(vv[:, 64::65], 1.0), w=("vv",))
        dma("gpsimd", wvs, wdr["wa_v"][0], w=("awvs",))

        def kv_proj(ysl, ytok, keyt0):
            for mc in range(4):
                dma("gpsimd", WB[mc % 2], wdr["wa_k"][mc], w=(f"awb{mc % 2}",))
                bk = pb[4 + mc % 2]
                for kc in range(KC):
                    top(lambda E, mc=mc, kc=kc, bk=bk: E.matmul(
                        bk[:, 0:512], WB[mc % 2][:, kc * 128:(kc + 1) * 128],
                        ysl(kc), start=(kc == 0), stop=(kc == KC - 1)),
                        r=(f"awb{mc % 2}", ytok), w=(PT[4 + mc % 2],))
                for g2 in range(2):
                    sop(lambda E, mc=mc, g2=g2, bk=bk, keyt0=keyt0: E.copy(
                        k2[g2 * 64:g2 * 64 + 64,
                           mc * 2048 + keyt0 * 128: mc * 2048 + keyt0 * 128 + 512],
                        bk[g2 * 64:g2 * 64 + 64, 0:512]),
                        r=(PT[4 + mc % 2],), w=("k2",))
            for t4 in range(4):
                kt = keyt0 + t4
                bk = pb[4 + t4 % 2]
                for kc in range(KC):
                    top(lambda E, t4=t4, kc=kc, bk=bk: E.matmul(
                        bk[:, 0:512], ysl(kc)[:, t4 * 128:t4 * 128 + 128],
                        wvs[:, kc * 512:(kc + 1) * 512],
                        start=(kc == 0), stop=(kc == KC - 1)),
                        r=("awvs", ytok), w=(PT[4 + t4 % 2],))
                for g in range(8):
                    sop(lambda E, g=g, kt=kt, bk=bk: E.copy(
                        vv[:, kt * 520 + g * 65: kt * 520 + g * 65 + 64],
                        bk[:, g * 64:(g + 1) * 64]),
                        r=(PT[4 + t4 % 2],), w=("vv",))

        for half in range(2):
            for t4 in range(4):
                tg = half * 4 + t4
                xb = XP[t4 % 2]
                xt = f"xpt{t4 % 2}"
                dma("sync", xb, xp_d[tg], w=(xt,))
                ln_stats(xb, (xt,))
                vop(lambda E, xb=xb: E.tensor_scalar(ytmp, xb, smv(64), smv(80),
                                                     OP.mult, OP.add),
                    r=(xt, "lnA", "lnB"), w=("aytmp",))
                tm_to_fm(ytmp, ("aytmp",), y1p,
                         lambda c, t4=t4: c * 512 + t4 * 128, "y1p")
            kv_proj(lambda kc, half=half:
                    y1p[:, kc * 512:(kc + 1) * 512], "y1p", half * 4)
        A.freebuf("xpt0", "xpt1", "y1p")
        y1f = A.alloc("y1f", 128, KC * T, BF16)
        qf = A.alloc("qf", 128, KC * T, BF16)
        for t in range(NT):
            ln_stats(X()[:, t * D:(t + 1) * D], (f"x{t}",))
            vop(lambda E, t=t: E.tensor_scalar(ytmp, X()[:, t * D:(t + 1) * D],
                                               smv(64), smv(80), OP.mult, OP.add),
                r=(f"x{t}", "lnA", "lnB"), w=("aytmp",))
            tm_to_fm(ytmp, ("aytmp",), y1f,
                     lambda c, t=t: c * T + t * 128, "y1f")
        for half in range(2):
            kv_proj(lambda kc, half=half:
                    y1f[:, kc * T + half * 512: kc * T + half * 512 + 512],
                    "y1f", 8 + half * 4)
        for mc in range(KC):
            dma("gpsimd", WB[mc % 2], wdr["wa_q"][mc], w=(f"awb{mc % 2}",))
            for half in range(2):
                bk = pb[4 + half]
                for kc in range(KC):
                    top(lambda E, mc=mc, kc=kc, half=half, bk=bk: E.matmul(
                        bk[:, 0:512], WB[mc % 2][:, kc * 128:(kc + 1) * 128],
                        y1f[:, kc * T + half * 512: kc * T + half * 512 + 512],
                        start=(kc == 0), stop=(kc == KC - 1)),
                        r=(f"awb{mc % 2}", "y1f"), w=(PT[4 + half],))
                sop(lambda E, mc=mc, half=half, bk=bk: E.copy(
                    qf[:, mc * T + half * 512: mc * T + half * 512 + 512],
                    bk[:, 0:512]), r=(PT[4 + half],), w=("qf",))
        tap("y1f", y1f, ("y1f",))
        tap("qf", qf, ("qf",))
        tap("k2", k2, ("k2",))
        tap("vv", vv, ("vv",))
        A.freebuf("y1f", "aytmp", "awb0", "awb1", "awvs")
        k2d = A.alloc("k2d", 128, 4 * 2048, BF16)
        dma("sync", k2d[0:64, :], k2[64:128, :], r=("k2",), w=("k2d",))
        dma("sync", k2d[64:128, :], k2[0:64, :], r=("k2",), w=("k2d",))
        zf = A.alloc("zf", 128, KC * T, BF16)
        pbh = A.alloc("pbh", 128, 8 * 512, BF16,
                      toks=tuple(f"pbh{i}" for i in range(8)))
        maskb = A.alloc("maskb", 128, 2 * 512, BF16,
                        toks=("maskb0", "maskb1"))
        recbuf = A.alloc("recbuf", 128, 512, F32)
        rsb = A.alloc("rsb", 128, 512, F32)
        zstage = A.alloc("zstage", 128, 512, BF16)
        for g in range(8):
            for qh in range(2):
                started = [False] * 4
                ms = [m for m in range(1, 16)
                      if max(max(0, m - 8), qh * 4) <= min(min(7, m - 1),
                                                           qh * 4 + 3)]

                def emit_pv(m, g=g, qh=qh):
                    jlo = max(max(0, m - 8), qh * 4)
                    jhi = min(min(7, m - 1), qh * 4 + 3)
                    wc = (jhi - jlo + 1) * 128
                    woff = (jlo - qh * 4) * 128
                    for h4 in range(4):
                        pslot = ((m % 2) * 4 + h4) * 512
                        ptok = f"pbh{(m % 2) * 4 + h4}"
                        top(lambda E, g=g, m=m, h4=h4, woff=woff, wc=wc,
                            pslot=pslot, st=(not started[h4]): E.matmul(
                            pb[h4][0:65, woff:woff + wc],
                            vv[:, m * 520 + 65 * g: m * 520 + 65 * g + 65],
                            pbh[:, pslot:pslot + wc],
                            start=st, stop=False),
                            r=("vv", ptok), w=(PT[h4],))
                        started[h4] = True

                for mi, m in enumerate(ms):
                    jlo = max(max(0, m - 8), qh * 4)
                    jhi = min(min(7, m - 1), qh * 4 + 3)
                    wc = (jhi - jlo + 1) * 128
                    mslot = (m % 2) * 512
                    mtok = f"maskb{m % 2}"
                    vop(lambda E, m=m, jlo=jlo, wc=wc, mslot=mslot:
                        E.tensor_scalar(
                        maskb[:, mslot:mslot + wc],
                        sq_rep[:, jlo * 128:jlo * 128 + wc],
                        skcol[:, m:m + 1], None, OP.is_equal),
                        r=("sq_rep", "skcol"), w=(mtok,))
                    if jlo == m - 8:
                        vop(lambda E, mslot=mslot: E.tensor_tensor(
                            maskb[:, mslot:mslot + 128],
                            maskb[:, mslot:mslot + 128], C["diagm"],
                            OP.mult), r=("diagm", mtok), w=(mtok,))
                    for h4 in range(4):
                        h = 4 * g + h4
                        rb = 64 * (h % 2)
                        ksrc = k2 if (g % 2) == (h % 2) else k2d
                        ktok = "k2" if (g % 2) == (h % 2) else "k2d"
                        sbk = pb[4 + h4]
                        pslot = ((m % 2) * 4 + h4) * 512
                        ptok = f"pbh{(m % 2) * 4 + h4}"
                        top(lambda E, g=g, m=m, rb=rb, jlo=jlo, wc=wc, h=h,
                            ksrc=ksrc, sbk=sbk: E.matmul(
                            sbk[:, 0:wc],
                            ksrc[rb:rb + 64,
                                 (g // 2) * 2048 + m * 128:
                                 (g // 2) * 2048 + m * 128 + 128],
                            qf[rb:rb + 64, (h // 2) * T + jlo * 128:
                               (h // 2) * T + jlo * 128 + wc],
                            start=True, stop=True),
                            r=(ktok, "qf"), w=(PT[4 + h4],))
                        sop(lambda E, pslot=pslot, wc=wc, sbk=sbk:
                            E.activation(
                            pbh[:, pslot:pslot + wc], sbk[:, 0:wc],
                            ACT.Exp, scale=0.125),
                            r=(PT[4 + h4],), w=(ptok,))
                        vop(lambda E, pslot=pslot, wc=wc, mslot=mslot:
                            E.tensor_tensor(
                            pbh[:, pslot:pslot + wc],
                            pbh[:, pslot:pslot + wc],
                            maskb[:, mslot:mslot + wc],
                            OP.mult), r=(mtok, ptok), w=(ptok,))
                    if mi > 0:
                        emit_pv(ms[mi - 1])
                emit_pv(ms[-1])
                for h4 in range(4):
                    h = 4 * g + h4
                    zc = h // 2
                    sop(lambda E, h4=h4: E.activation(
                        recbuf[64:65, :], pb[h4][64:65, 0:512], ACT.Ln),
                        r=(PT[h4],), w=("recbuf",))
                    sop(lambda E: E.activation(
                        recbuf[64:65, :], recbuf[64:65, :], ACT.Exp,
                        scale=-1.0), r=("recbuf",), w=("recbuf",))
                    top(lambda E: E.matmul(
                        pb[6][:, 0:512], onesp[64:65, 0:128],
                        recbuf[64:65, :], start=True, stop=True),
                        r=("recbuf", "onesp"), w=(PT[6],))
                    vop(lambda E: E.tensor_copy(rsb, pb[6][:, 0:512]),
                        r=(PT[6],), w=("rsb",))
                    if h % 2 == 0:
                        vop(lambda E, h4=h4, zc=zc, qh=qh: E.tensor_tensor(
                            zf[0:64, zc * T + qh * 512: zc * T + qh * 512 + 512],
                            pb[h4][0:64, 0:512], rsb[0:64, :], OP.mult),
                            r=(PT[h4], "rsb"), w=("zf",))
                    else:
                        vop(lambda E, h4=h4: E.tensor_tensor(
                            zstage[0:64, :], pb[h4][0:64, 0:512], rsb[0:64, :],
                            OP.mult), r=(PT[h4], "rsb"), w=("zstage",))
                        dma("sync",
                            zf[64:128, zc * T + qh * 512: zc * T + qh * 512 + 512],
                            zstage[0:64, :], r=("zstage",), w=("zf",))
        A.freebuf("qf", "k2", "k2d", "vv", "pbh", "maskb", "recbuf", "rsb",
                  "zstage")
        tap("zf", zf, ("zf",))
        WO = [A.alloc(f"awo{i}", 128, KC * 512, BF16) for i in range(4)]
        for nb in range(4):
            dma("gpsimd", WO[nb], wdr["wa_o"][nb], w=(f"awo{nb}",))
        for t in range(NT):
            bs = 4 * (t % 2)
            for kc in range(KC):
                for nb in range(4):
                    top(lambda E, t=t, kc=kc, nb=nb, bs=bs: E.matmul(
                        pb[bs + nb][:, 0:512],
                        zf[:, kc * T + t * 128: kc * T + t * 128 + 128],
                        WO[nb][:, kc * 512:(kc + 1) * 512],
                        start=(kc == 0), stop=(kc == KC - 1)),
                        r=(f"awo{nb}", "zf"), w=(PT[bs + nb],))
            for nb in range(4):
                vop(lambda E, nb=nb, t=t, bs=bs: E.tensor_tensor(
                    X()[:, t * D + nb * 512: t * D + nb * 512 + 512],
                    X()[:, t * D + nb * 512: t * D + nb * 512 + 512],
                    pb[bs + nb][:, 0:512], OP.add),
                    r=(PT[bs + nb], f"x{t}"), w=(f"x{t}",))
        A.freebuf("zf", "awo0", "awo1", "awo2", "awo3")

    def build_ffn(call):
        sfx = f"f{call}"
        a1f = A.alloc("a1f" + sfx, 128, KC * T, BF16)
        ytf = A.alloc("ytf" + sfx, 128, D, F32)
        atm = A.alloc("atm" + sfx, 128, D, BF16)
        c1b = A.alloc("c1b" + sfx, 128, T, F32)
        s2b = A.alloc("s2b" + sfx, 128, T, F32)
        m2p = A.alloc("m2p" + sfx, 128, T, F32)
        hst0 = A.alloc("hst0" + sfx, 128, T, F32)
        hst1 = A.alloc("hst1" + sfx, 128, T, F32)
        gb0 = A.alloc("gb0" + sfx, 128, D, BF16)
        gb1 = A.alloc("gb1" + sfx, 128, D, BF16)
        HS = [hst0, hst1]
        GB = [gb0, gb1]
        TA1, TYT, TATM = "a1f" + sfx, "ytf" + sfx, "atm" + sfx
        for t in range(NT):
            ln_stats(X()[:, t * D:(t + 1) * D], (f"x{t}",))
            vop(lambda E, t=t: E.tensor_scalar(ytf, X()[:, t * D:(t + 1) * D],
                                               smv(64), smv(80), OP.mult,
                                               OP.add),
                r=(f"x{t}", "lnA", "lnB"), w=(TYT,))
            vop(lambda E, t=t: E.tensor_reduce(smv(0 + t), ytf, AX, OP.max,
                                               apply_absolute_value=True),
                r=(TYT,), w=("m1c",))
            vop(lambda E, t=t: E.tensor_scalar(smv(0 + t), smv(0 + t), FFN_EPS,
                                               None, OP.max),
                r=("m1c",), w=("m1c",))
            vop(lambda E, t=t: E.reciprocal(smv(16 + t), smv(0 + t)),
                r=("m1c",), w=("s1c",))
            vop(lambda E, t=t: E.tensor_scalar(smv(16 + t), smv(16 + t), 127.0,
                                               None, OP.mult),
                r=("s1c",), w=("s1c",))
            vop(lambda E, t=t: E.tensor_scalar(ytf, ytf, smv(16 + t), MAGIC,
                                               OP.mult, OP.add),
                r=(TYT, "s1c"), w=(TYT,))
            vop(lambda E: E.tensor_scalar(atm, ytf, -MAGIC, None, OP.add),
                r=(TYT,), w=(TATM,))
            tm_to_fm(atm, (TATM,), a1f, lambda c, t=t: c * T + t * 128, TA1)
        top(lambda E: E.transpose(pb[6][0:8, 0:128], smal[:, 0:8], C["idf"]),
            r=("m1c", "idf"), w=(PT[6],))
        vop(lambda E: E.tensor_copy(m1t[0:8, :], pb[6][0:8, 0:128]),
            r=(PT[6],), w=("m1tt",))
        dma("sync", brow[0:1, :], m1t[0:8, :], r=("m1tt",), w=("browt",))
        vop(lambda E: E.tensor_scalar(su127[0:1, :], C["onesf"][0:1, :],
                                      sval[0:1, 0:1], 1.0 / 127.0,
                                      OP.mult, OP.mult),
            r=("sval", "onesf"), w=("su127t",))
        for g8 in range(NT):
            bk = pb[6 if g8 % 2 == 0 else 7]
            top(lambda E, g8=g8, bk=bk: E.matmul(
                bk[:, 0:128], su127[0:1, :], brow[0:1, g8 * 128:(g8 + 1) * 128],
                start=True, stop=True),
                r=("su127t", "browt"), w=(PT[6 + g8 % 2],))
            vop(lambda E, g8=g8, bk=bk: E.tensor_copy(
                c1b[:, g8 * 128:(g8 + 1) * 128], bk[:, 0:128]),
                r=(PT[6 + g8 % 2],), w=("c1b" + sfx,))
        vop(lambda E: E.memset(m2p, 0.0), w=("m2p" + sfx,))
        for mb in range(32):
            dma("sync", GB[mb % 2], agu_out[mb], r=("agu",),
                w=(f"gb{mb % 2}" + sfx,))
            hs = HS[mb % 2]
            ht = f"hst{mb % 2}" + sfx
            for half in range(2):
                bk = pb[half]
                for kc in range(KC):
                    top(lambda E, mb=mb, kc=kc, half=half, bk=bk: E.matmul(
                        bk[:, 0:512], GB[mb % 2][:, kc * 128:(kc + 1) * 128],
                        a1f[:, kc * T + half * 512: kc * T + half * 512 + 512],
                        start=(kc == 0), stop=(kc == KC - 1)),
                        r=(f"gb{mb % 2}" + sfx, TA1), w=(PT[half],))
                vop(lambda E, half=half, hs=hs, bk=bk: E.tensor_tensor(
                    hs[:, half * 512:half * 512 + 512], bk[:, 0:512],
                    c1b[:, half * 512:half * 512 + 512], OP.mult),
                    r=(PT[half], "c1b" + sfx), w=(ht,))
                vop(lambda E, half=half, hs=hs: E.scalar_tensor_tensor(
                    hs[:, half * 512:half * 512 + 512],
                    hs[:, half * 512:half * 512 + 512], 0.0,
                    hs[:, half * 512:half * 512 + 512], OP.max, OP.mult),
                    r=(ht,), w=(ht,))
                vop(lambda E, half=half, hs=hs: E.tensor_tensor(
                    m2p[:, half * 512:half * 512 + 512],
                    m2p[:, half * 512:half * 512 + 512],
                    hs[:, half * 512:half * 512 + 512], OP.max),
                    r=(ht, "m2p" + sfx), w=("m2p" + sfx,))
            dma("sync", hsp_d[mb], hs, r=(ht,), w=("hsp",))
        for g8 in range(NT):
            top(lambda E, g8=g8: E.transpose(
                pb[6][:, 0:128], m2p[:, g8 * 128:(g8 + 1) * 128], C["idf"]),
                r=("m2p" + sfx, "idf"), w=(PT[6],))
            vop(lambda E, g8=g8: E.tensor_reduce(smv(32 + g8), pb[6][:, 0:128],
                                                 AX, OP.max),
                r=(PT[6],), w=("m2c",))
        vop(lambda E: E.tensor_scalar(smal[:, 32:40], smal[:, 32:40], FFN_EPS,
                                      None, OP.max), r=("m2c",), w=("m2c",))
        vop(lambda E: E.reciprocal(smal[:, 40:48], smal[:, 32:40]),
            r=("m2c",), w=("s2c",))
        vop(lambda E: E.tensor_scalar(smal[:, 40:48], smal[:, 40:48], 127.0,
                                      None, OP.mult), r=("s2c",), w=("s2c",))
        vop(lambda E: E.tensor_scalar(smal[:, 48:56], smal[:, 32:40],
                                      svrep[:, 1:2], 1.0 / 127.0,
                                      OP.mult, OP.mult),
            r=("m2c", "svrep"), w=("c3c",))
        top(lambda E: E.transpose(pb[6][0:8, 0:128], smal[:, 40:48], C["idf"]),
            r=("s2c", "idf"), w=(PT[6],))
        vop(lambda E: E.tensor_copy(m1t[0:8, :], pb[6][0:8, 0:128]),
            r=(PT[6],), w=("m1tt",))
        dma("sync", brow[0:1, :], m1t[0:8, :], r=("m1tt",), w=("browt",))
        for g8 in range(NT):
            bk = pb[6 if g8 % 2 == 0 else 7]
            top(lambda E, g8=g8, bk=bk: E.matmul(
                bk[:, 0:128], C["onesf"][0:1, :],
                brow[0:1, g8 * 128:(g8 + 1) * 128], start=True, stop=True),
                r=("onesf", "browt"), w=(PT[6 + g8 % 2],))
            vop(lambda E, g8=g8, bk=bk: E.tensor_copy(
                s2b[:, g8 * 128:(g8 + 1) * 128], bk[:, 0:128]),
                r=(PT[6 + g8 % 2],), w=("s2b" + sfx,))
        A.freebuf("a1f" + sfx, "ytf" + sfx, "atm" + sfx, "c1b" + sfx,
                  "m2p" + sfx)
        a2fA = A.alloc("a2fA" + sfx, 128, 16 * T, BF16)
        a2fB = A.alloc("a2fB" + sfx, 128, 16 * T, BF16)
        def a2tok(k):
            return ("a2fA" + sfx) if k < 16 else ("a2fB" + sfx)

        def a2sl(k, lo, n):
            buf = a2fA if k < 16 else a2fB
            kk = k % 16
            return buf[:, kk * T + lo: kk * T + lo + n]
        for mb in range(32):
            hs = HS[mb % 2]
            ht = f"hst{mb % 2}" + sfx
            dma("sync", hs, hsp_d[mb], r=("hsp",), w=(ht,))
            vop(lambda E, hs=hs: E.tensor_tensor(hs, hs, s2b, OP.mult),
                r=(ht, "s2b" + sfx), w=(ht,))
            vop(lambda E, hs=hs: E.tensor_scalar(hs, hs, MAGIC, None, OP.add),
                r=(ht,), w=(ht,))
            vop(lambda E, mb=mb, hs=hs: E.tensor_scalar(
                a2sl(mb, 0, T), hs, -MAGIC, None, OP.add),
                r=(ht,), w=(a2tok(mb),))
        A.freebuf("hst0" + sfx, "hst1" + sfx, "s2b" + sfx, "gb0" + sfx,
                  "gb1" + sfx)
        gd0 = A.alloc("gd0" + sfx, 128, 8 * 512, BF16)
        gd1 = A.alloc("gd1" + sfx, 128, 8 * 512, BF16)
        GD = [gd0, gd1]
        for nb in range(4):
            for kg in range(4):
                sb_i = nb * 4 + kg
                dma("sync", GD[sb_i % 2], agd_out[sb_i], r=("agd",),
                    w=(f"gd{sb_i % 2}" + sfx,))
                for t in range(NT):
                    for kk in range(8):
                        k = 8 * kg + kk
                        top(lambda E, t=t, k=k, kk=kk, sb_i=sb_i, kg=kg:
                            E.matmul(
                            pb[t][:, 0:512],
                            a2sl(k, t * 128, 128),
                            GD[sb_i % 2][:, kk * 512:(kk + 1) * 512],
                            start=(kg == 0 and kk == 0),
                            stop=(kg == 3 and kk == 7)),
                            r=(f"gd{sb_i % 2}" + sfx, a2tok(k)), w=(PT[t],))
            for t in range(NT):
                vop(lambda E, nb=nb, t=t: E.scalar_tensor_tensor(
                    X()[:, t * D + nb * 512: t * D + nb * 512 + 512],
                    pb[t][:, 0:512], smal[:, 48 + t:49 + t],
                    X()[:, t * D + nb * 512: t * D + nb * 512 + 512],
                    OP.mult, OP.add),
                    r=(PT[t], "c3c", f"x{t}"), w=(f"x{t}",))
        A.freebuf("a2fA" + sfx, "a2fB" + sfx, "gd0" + sfx, "gd1" + sfx)

    def build_attn_b():
        CBv = {k: A.alloc(k, sh[0], sh[1], F32) for k, sh in _CB.items()}
        for k in _CB:
            dma("sync", CBv[k], cdr[k], w=(k,))
        nwrow_s = A.alloc("nwrow_s", 1, D, F32)
        nwcol = A.alloc("nwcol", 128, 16, F32)
        dma("sync", nwrow_s, nw_d, w=("nwrow_s",))
        for m in range(KC):
            top(lambda E, m=m: E.transpose(
                pb[6][:, 0:1], nwrow_s[0:1, m * 128:(m + 1) * 128],
                C["idf"][0:1, 0:1]),
                r=("nwrow_s", "idf"), w=(PT[6],))
            vop(lambda E, m=m: E.tensor_copy(nwcol[:, m:m + 1], pb[6][:, 0:1]),
                r=(PT[6],), w=("nwcol",))
        A.freebuf("nwrow_s")
        y3f = A.alloc("y3f", 128, KC * T, BF16)
        ytmp = A.alloc("bytmp", 128, D, BF16)
        tap("xb0", X()[:, 0:D], ("x0",))
        for t in range(NT):
            ln_stats(X()[:, t * D:(t + 1) * D], (f"x{t}",))
            vop(lambda E, t=t: E.tensor_scalar(ytmp, X()[:, t * D:(t + 1) * D],
                                               smv(64), smv(80), OP.mult,
                                               OP.add),
                r=(f"x{t}", "lnA", "lnB"), w=("bytmp",))
            if t == 0:
                tap("ab0", smal[:, 60:90], ("lnB",))
                tap("ytmp0", ytmp, ("bytmp",))
                tap("lnsc0", lnsc, ("lnsc",))
            tm_to_fm(ytmp, ("bytmp",), y3f, lambda c, t=t: c * T + t * 128,
                     "y3f")
        tap("y3f", y3f, ("y3f",))
        for t in range(NT):
            dma("sync", xsp_d[t], X()[:, t * D:(t + 1) * D],
                r=(f"x{t}",), w=("xspill",))
        A.freebuf("X", "bytmp")
        qf = A.alloc("qbf", 128, BH * T, BF16)
        kf = A.alloc("kbf", 128, BH * T, BF16)
        g1f = A.alloc("g1f", 128, T, BF16)
        silf = A.alloc("bsil", 128, 512, F32)
        rt1 = A.alloc("brt1", 128, 512, F32)
        rt2 = A.alloc("brt2", 128, 512, F32)
        qtms = A.alloc("qtms", 128, 512, BF16)
        wrhs = A.alloc("bwrhs", 128, KC * 512, BF16)
        wblk = A.alloc("bwblk", 128, D, BF16)

        def proj_rot4(wkey, dst, dtok):
            for nb in range(4):
                dma("gpsimd", wrhs, wdr[wkey][nb], w=("bwrhs",))
                for t in range(NT):
                    bk = pb[4 + t % 2]
                    for kc in range(KC):
                        top(lambda E, t=t, kc=kc, bk=bk: E.matmul(
                            bk[:, 0:512],
                            y3f[:, kc * T + t * 128: kc * T + t * 128 + 128],
                            wrhs[:, kc * 512:(kc + 1) * 512],
                            start=(kc == 0), stop=(kc == KC - 1)),
                            r=("bwrhs", "y3f"), w=(PT[4 + t % 2],))
                    sop(lambda E, bk=bk: E.activation(silf, bk[:, 0:512],
                                                      ACT.Silu),
                        r=(PT[4 + t % 2],), w=("bsil",))
                    csl = CBv["cs_tm"][:, t * 128:(t + 1) * 128]
                    for hb in range(4):
                        o = hb * 128
                        vop(lambda E, o=o, csl=csl: E.tensor_tensor(
                            rt1[:, 0:64], silf[:, o:o + 64], csl[:, 0:64],
                            OP.mult), r=("bsil", "cs_tm"), w=("brt1",))
                        vop(lambda E, o=o, csl=csl: E.tensor_tensor(
                            rt2[:, 0:64], silf[:, o + 64:o + 128],
                            csl[:, 64:128], OP.mult),
                            r=("bsil", "cs_tm"), w=("brt2",))
                        vop(lambda E, o=o: E.tensor_tensor(
                            qtms[:, o:o + 64], rt1[:, 0:64], rt2[:, 0:64],
                            OP.subtract), r=("brt1", "brt2"), w=("qtms",))
                        vop(lambda E, o=o, csl=csl: E.tensor_tensor(
                            rt1[:, 0:64], silf[:, o:o + 64], csl[:, 64:128],
                            OP.mult), r=("bsil", "cs_tm"), w=("brt1",))
                        vop(lambda E, o=o, csl=csl: E.tensor_tensor(
                            rt2[:, 0:64], silf[:, o + 64:o + 128], csl[:, 0:64],
                            OP.mult), r=("bsil", "cs_tm"), w=("brt2",))
                        vop(lambda E, o=o: E.tensor_tensor(
                            qtms[:, o + 64:o + 128], rt1[:, 0:64], rt2[:, 0:64],
                            OP.add), r=("brt1", "brt2"), w=("qtms",))
                    bank = pb[7]
                    for hb in range(4):
                        top(lambda E, hb=hb, bank=bank: E.transpose(
                            bank.bitcast(BF16)[:, hb * 128:(hb + 1) * 128],
                            qtms[:, hb * 128:(hb + 1) * 128], C["idb"]),
                            r=("qtms", "idb"), w=(PT[7],))
                    for hb in range(4):
                        vop(lambda E, hb=hb, nb=nb, t=t, bank=bank, dst=dst:
                            E.tensor_copy(
                            dst[:, (nb * 4 + hb) * T + t * 128:
                                (nb * 4 + hb) * T + t * 128 + 128],
                            bank.bitcast(BF16)[:, hb * 128:(hb + 1) * 128]),
                            r=(PT[7],), w=(dtok,))

        proj_rot4("wb_q", qf, "qbf")
        proj_rot4("wb_k", kf, "kbf")
        A.freebuf("brt2", "qtms")
        vtm = A.alloc("vtm", 128, NT * D, BF16)
        for nb in range(4):
            dma("gpsimd", wrhs, wdr["wb_v"][nb], w=("bwrhs",))
            for t in range(NT):
                bk = pb[4 + t % 2]
                for kc in range(KC):
                    top(lambda E, t=t, kc=kc, bk=bk: E.matmul(
                        bk[:, 0:512],
                        y3f[:, kc * T + t * 128: kc * T + t * 128 + 128],
                        wrhs[:, kc * 512:(kc + 1) * 512],
                        start=(kc == 0), stop=(kc == KC - 1)),
                        r=("bwrhs", "y3f"), w=(PT[4 + t % 2],))
                sop(lambda E, nb=nb, t=t, bk=bk: E.copy(
                    vtm[:, t * D + nb * 512: t * D + nb * 512 + 512],
                    bk[:, 0:512]), r=(PT[4 + t % 2],), w=("vtm",))
        dma("gpsimd", wblk, wdr["wb_g1"][0], w=("bwblk",))
        for half in range(2):
            bk = pb[4 + half]
            for kc in range(KC):
                top(lambda E, kc=kc, half=half, bk=bk: E.matmul(
                    bk[:, 0:512], wblk[:, kc * 128:(kc + 1) * 128],
                    y3f[:, kc * T + half * 512: kc * T + half * 512 + 512],
                    start=(kc == 0), stop=(kc == KC - 1)),
                    r=("bwblk", "y3f"), w=(PT[4 + half],))
            vop(lambda E, half=half, bk=bk: E.tensor_copy(
                g1f[:, half * 512:half * 512 + 512], bk[:, 0:512]),
                r=(PT[4 + half],), w=("g1f",))
        wg2s = A.alloc("wg2s", 128, D, BF16)
        dma("gpsimd", wg2s, wdr["wb_g2"], w=("wg2s",))
        tap("qbf", qf, ("qbf",))
        tap("kbf", kf, ("kbf",))
        tap("vtm", vtm, ("vtm",))
        tap("g1f", g1f, ("g1f",))
        A.freebuf("y3f", "bwrhs")
        Sf = A.alloc("Sf", 128, BH * 128, F32)
        Sall = A.alloc("Sall", 128, NT * BH * 128, BF16)
        lam = _alibi_lam()
        vop(lambda E: E.memset(Sf, 0.0), w=("Sf",))
        KH = [A.alloc("khat0", 128, 128, BF16), A.alloc("khat1", 128, 128, BF16)]
        lamCs = [math.exp(lam[h] * 128.0) for h in range(BH)]
        for c in range(NT):
            vop(lambda E, c=c: E.tensor_copy(
                Sall[:, c * BH * 128:(c + 1) * BH * 128], Sf),
                r=("Sf",), w=("Sall",))

            def emit_delta(h, c=c):
                hp = h % 2
                top(lambda E, c=c, h=h, hp=hp: E.matmul(
                    pb[hp][:, 0:128], KH[hp],
                    vtm[:, c * D + h * 128: c * D + h * 128 + 128],
                    start=True, stop=True),
                    r=(f"khat{hp}", "vtm"), w=(PT[hp],))
                vop(lambda E, h=h, hp=hp: E.scalar_tensor_tensor(
                    Sf[:, h * 128:(h + 1) * 128],
                    Sf[:, h * 128:(h + 1) * 128],
                    lamCs[h], pb[hp][:, 0:128], OP.mult, OP.add),
                    r=(PT[hp], "Sf"), w=("Sf",))

            for h in range(BH):
                hp = h % 2
                top(lambda E, c=c, h=h, hp=hp: E.transpose(
                    pb[6 + hp].bitcast(BF16)[:, 0:128],
                    kf[:, h * T + c * 128: h * T + c * 128 + 128], C["idb"]),
                    r=("kbf", "idb"), w=(PT[6 + hp],))
                vop(lambda E, h=h, hp=hp: E.tensor_scalar(
                    KH[hp], pb[6 + hp].bitcast(BF16)[:, 0:128],
                    CBv["dkcol"][:, h:h + 1], None, OP.mult),
                    r=(PT[6 + hp], "dkcol"), w=(f"khat{hp}",))
                if h > 0:
                    emit_delta(h - 1, c)
            emit_delta(BH - 1, c)
        tap("Sf8", Sf, ("Sf",))
        dma("sync", ccs_in.rearrange("(h d) e -> d h e", h=BH),
            Sf.rearrange("p (h e) -> p h e", h=BH), r=("Sf",), w=("ccsin",))
        s.add("gpsimd", lambda E: E.collective_compute(
            "AllGather", OP.bypass,
            replica_groups=[[0, 1], [2, 3], [4, 5], [6, 7]],
            ins=[ccs_in.opt()], outs=[ccs_out.opt()]),
            r=("ccsin",), w=("ccso",), ovr=True)
        srecv = A.alloc("srecv", 128, BH * 128, BF16)
        dma("gpsimd", srecv.rearrange("p (h e) -> p h e", h=BH),
            ccs_out[0:BH * 128].rearrange("(h d) e -> d h e", h=BH),
            r=("ccso",), w=("srecv",))
        tap("srecv", srecv, ("srecv",))
        tap("Sall", Sall, ("Sall",))
        A.freebuf("Sf")
        A.freebuf("khat0", "khat1")
        OT = [A.alloc("otm0", 128, D, BF16), A.alloc("otm1", 128, D, BF16)]
        PBB = [A.alloc("pbb0", 128, 128, BF16), A.alloc("pbb1", 128, 128, BF16)]
        QA = [A.alloc("qa0", 128, BH * 128, BF16),
              A.alloc("qa1", 128, BH * 128, BF16)]
        SRS = [A.alloc("srs0", 128, 128, BF16), A.alloc("srs1", 128, 128, BF16)]
        XNB = [A.alloc("xnb0", 128, D, BF16), A.alloc("xnb1", 128, D, BF16)]
        SIL = [silf[:, 0:512], rt1[:, 0:512]]
        msq = A.alloc("msq", 128, 128, F32)
        msall = A.alloc("msall", 128, 128, F32)
        rsall = A.alloc("rsall", 128, 128, F32)
        qf3 = qf.rearrange("p (h t) -> p h t", h=BH)
        ea3 = CBv["ea_rep"].rearrange("p (h t) -> p h t", h=BH)
        for c in range(NT):
            par = c % 2
            otc, ot_t = OT[par], f"otm{par}"
            xnc, xn_t = XNB[par], f"xnb{par}"
            qac, qa_t = QA[par], f"qa{par}"
            vop(lambda E, c=c, qac=qac: E.tensor_tensor(
                qac.rearrange("p (h t) -> p h t", h=BH),
                qf3[:, :, c * 128:(c + 1) * 128], ea3, OP.mult),
                r=("qbf", "ea_rep"), w=(qa_t,))
            def emit_o(h, c=c, otc=None, ot_t=None, qac=None, qa_t=None):
                hp = h % 2
                pbbc, pbb_t = PBB[hp], f"pbb{hp}"
                srsc, srs_t = SRS[hp], f"srs{hp}"
                ob = pb[h % 4]
                top(lambda E, c=c, h=h, ob=ob, pbbc=pbbc: E.matmul(
                    ob[:, 0:128], pbbc,
                    vtm[:, c * D + h * 128: c * D + h * 128 + 128],
                    start=True, stop=False), r=(pbb_t, "vtm"), w=(PT[h % 4],))
                top(lambda E, c=c, h=h, ob=ob, qac=qac: E.matmul(
                    ob[:, 0:128], qac[:, h * 128:(h + 1) * 128],
                    Sall[:, (c * BH + h) * 128: (c * BH + h) * 128 + 128],
                    start=False, stop=False), r=(qa_t, "Sall"),
                    w=(PT[h % 4],))
                top(lambda E, ob=ob, qac=qac, h=h, srsc=srsc: E.matmul(
                    ob[:, 0:128], qac[:, h * 128:(h + 1) * 128], srsc,
                    start=False, stop=True),
                    r=(qa_t, srs_t), w=(PT[h % 4],))
                sop(lambda E, h=h, ob=ob, otc=otc: E.copy(
                    otc[:, h * 128:(h + 1) * 128], ob[:, 0:128]),
                    r=(PT[h % 4],), w=(ot_t,))

            for h in range(BH):
                hp = h % 2
                pbbc, pbb_t = PBB[hp], f"pbb{hp}"
                srsc, srs_t = SRS[hp], f"srs{hp}"
                sbk = pb[4 + hp]
                top(lambda E, c=c, h=h, sbk=sbk: E.matmul(
                    sbk[:, 0:128],
                    kf[:, h * T + c * 128: h * T + c * 128 + 128],
                    qf[:, h * T + c * 128: h * T + c * 128 + 128],
                    start=True, stop=True), r=("kbf", "qbf"),
                    w=(PT[4 + hp],))
                vop(lambda E, h=h, sbk=sbk, pbbc=pbbc: E.tensor_tensor(
                    pbbc, sbk[:, 0:128],
                    CBv["dt_tab"][:, h * 128:(h + 1) * 128], OP.mult),
                    r=(PT[4 + hp], "dt_tab"), w=(pbb_t,))
                vop(lambda E, c=c, h=h, srsc=srsc: E.tensor_scalar(
                    srsc, srecv[:, h * 128:(h + 1) * 128],
                    CBv["ebcol"][:, h * NT + c: h * NT + c + 1], None,
                    OP.mult), r=("srecv", "ebcol"), w=(srs_t,))
                if h > 0:
                    emit_o(h - 1, c, otc, ot_t, qac, qa_t)
            emit_o(BH - 1, c, otc, ot_t, qac, qa_t)
            for nb in range(4):
                gk = pb[6 + nb % 2]
                sl = SIL[nb % 2]
                slt = ("bsil", "brt1")[nb % 2]
                top(lambda E, c=c, nb=nb, gk=gk: E.matmul(
                    gk[:, 0:512], g1f[:, c * 128:(c + 1) * 128],
                    wg2s[:, nb * 512:(nb + 1) * 512], start=True, stop=True),
                    r=("g1f", "wg2s"), w=(PT[6 + nb % 2],))
                sop(lambda E, gk=gk, sl=sl: E.activation(sl, gk[:, 0:512],
                                                         ACT.Sigmoid),
                    r=(PT[6 + nb % 2],), w=(slt,))
                vop(lambda E, nb=nb, sl=sl, xnc=xnc, otc=otc: E.tensor_tensor(
                    xnc[:, nb * 512:(nb + 1) * 512],
                    otc[:, nb * 512:(nb + 1) * 512], sl, OP.mult),
                    r=(ot_t, slt), w=(xn_t,))
            for h in range(BH):
                vop(lambda E, h=h, xnc=xnc: E.tensor_tensor(
                    msq, xnc[:, h * 128:(h + 1) * 128],
                    xnc[:, h * 128:(h + 1) * 128], OP.mult),
                    r=(xn_t,), w=("msq",))
                vop(lambda E, c=c, h=h: E.tensor_reduce(
                    msall[:, c * 16 + h:c * 16 + h + 1], msq, AX, OP.add),
                    r=("msq",), w=("msall",))
            dma("sync", xg_d[c], xnc, r=(xn_t,), w=("xgsp",))
        vop(lambda E: E.tensor_scalar(msall, msall, 1.0 / 128.0, B_EPS,
                                      OP.mult, OP.add), r=("msall",),
            w=("msall",))
        sop(lambda E: E.activation(rsall, msall, ACT.Sqrt),
            r=("msall",), w=("rsall",))
        vop(lambda E: E.reciprocal(rsall, rsall), r=("rsall",), w=("rsall",))
        A.freebuf("qbf", "kbf", "vtm", "g1f", "bsil", "brt1", "msq",
                  "Sall", "srecv", "otm0", "otm1", "pbb0", "pbb1",
                  "qa0", "qa1", "srs0", "srs1", "xnb0", "xnb1", "bwblk",
                  "wg2s", "msall")
        tap("xgd", xg_d, ("xgsp",))
        Xh["v"] = A.alloc("X", 128, NT * D, F32, toks=XTOKS)
        zf = A.alloc("zfb", 128, KC * T, BF16)
        xnr0 = A.alloc("xnr0", 128, D, BF16)
        xnr1 = A.alloc("xnr1", 128, D, BF16)
        WO = [A.alloc(f"bwo{i}", 128, KC * 512, BF16) for i in range(4)]
        XN = [xnr0, xnr1]
        for t in range(NT):
            dma("sync", X()[:, t * D:(t + 1) * D], xsp_d[t],
                r=("xspill",), w=(f"x{t}",))
        for c in range(NT):
            xb = XN[c % 2]
            xt = f"xnr{c % 2}"
            dma("sync", xb, xg_d[c], r=("xgsp",), w=(xt,))
            for h in range(BH):
                vop(lambda E, h=h, c=c, xb=xb: E.tensor_scalar(
                    xb[:, h * 128:(h + 1) * 128], xb[:, h * 128:(h + 1) * 128],
                    rsall[:, c * 16 + h:c * 16 + h + 1], None, OP.mult),
                    r=(xt, "rsall"), w=(xt,))
            tm_to_fm(xb, (xt,), zf, lambda ch, c=c: ch * T + c * 128,
                     "zfb", 7, nwc=nwcol)
        tap("zfb", zf, ("zfb",))
        for nb in range(4):
            dma("gpsimd", WO[nb], wdr["wb_o"][nb], w=(f"bwo{nb}",))
        for t in range(NT):
            bs = 4 * (t % 2)
            for kc in range(KC):
                for nb in range(4):
                    top(lambda E, t=t, kc=kc, nb=nb, bs=bs: E.matmul(
                        pb[bs + nb][:, 0:512],
                        zf[:, kc * T + t * 128: kc * T + t * 128 + 128],
                        WO[nb][:, kc * 512:(kc + 1) * 512],
                        start=(kc == 0), stop=(kc == KC - 1)),
                        r=(f"bwo{nb}", "zfb"), w=(PT[bs + nb],))
            for nb in range(4):
                vop(lambda E, nb=nb, t=t, bs=bs: E.tensor_tensor(
                    X()[:, t * D + nb * 512: t * D + nb * 512 + 512],
                    X()[:, t * D + nb * 512: t * D + nb * 512 + 512],
                    pb[bs + nb][:, 0:512], OP.add),
                    r=(PT[bs + nb], f"x{t}"), w=(f"x{t}",))
        A.freebuf("zfb", "xnr0", "xnr1", "bwo0", "bwo1", "bwo2", "bwo3",
                  "nwcol", "rsall")
        for k in _CB:
            A.freebuf(k)

    if n_sub >= 1:
        build_attn_a()
    if n_sub >= 2:
        build_ffn(0)
    if n_sub >= 3:
        build_attn_b()
    if n_sub >= 4:
        build_ffn(1)

    for t in range(NT):
        dma("sync", out_d[t], X()[:, t * D:(t + 1) * D], r=(f"x{t}",),
            w=("outd",))
    vop(lambda E: E.memset(smv(255), 0.0), r=("outd",))

    stats = s.finalize(nc, sems, ovr_sems)
    stats["sbuf_peak"] = A.peak
    stk.close()
    _NC_CACHE[n_sub] = (nc, stats)
    return nc, stats


def _host_weights(inputs):
    def arr(k):
        return np.ascontiguousarray(np.asarray(inputs[k], dtype=np.float32))
    w = {}
    w["wa_q"] = _prep_lhsT(arr("L0_attn_a_Wq").T, 128)
    w["wa_k"] = _prep_lhsT(arr("L0_attn_a_Wk").T, 128)
    w["wa_v"] = _prep_lhsT(arr("L0_attn_a_Wv").T, 512)
    w["wa_o"] = _prep_lhsT(arr("L0_attn_a_Wo").T, 512)
    w["wb_q"] = _prep_lhsT(arr("L0_attn_b_Wq"), 512)
    w["wb_k"] = _prep_lhsT(arr("L0_attn_b_Wk"), 512)
    w["wb_v"] = _prep_lhsT(arr("L0_attn_b_Wv"), 512)
    w["wb_o"] = _prep_lhsT(arr("L0_attn_b_Wo"), 512)
    w["wb_g1"] = _prep_lhsT(arr("L0_attn_b_Wg1"), 128)
    w["wb_g2"] = arr("L0_attn_b_Wg2")
    wu = _prep_lhsT(arr("L0_ffn_w_up").T, 128)
    wdT = arr("L0_ffn_w_down").T
    a = wdT.reshape(32, 128, 4, 512)
    sub = np.empty((4, 4, 128, 8 * 512), np.float32)
    for nb in range(4):
        for kg in range(4):
            for kk in range(8):
                sub[nb, kg, :, kk * 512:(kk + 1) * 512] = a[8 * kg + kk, :, nb, :]
    wd = np.ascontiguousarray(sub.reshape(16, 128, I_FFN))
    return w, wu, wd


def kernel(**inputs):
    nc, stats = build_nc(4)
    x = np.ascontiguousarray(np.asarray(inputs["L0_attn_a_hidden_states"]),
                             dtype=np.float32)
    cu = np.asarray(inputs["L0_attn_a_cu_seqlens_q"],
                    dtype=np.int32).reshape(1, 3)
    w, wu, wd = _host_weights(inputs)
    nw = np.ascontiguousarray(np.asarray(inputs["L0_attn_b_norm_weight"]),
                              dtype=np.float32).reshape(1, D)
    in_maps = []
    for core in range(8):
        b, hh = core // 2, core % 2
        m = dict(w)
        m["cu"] = cu
        m["nwrow"] = nw
        m["xo"] = np.ascontiguousarray(
            x[b, hh * 1024:(hh + 1) * 1024].reshape(NT, 128, D))
        m["xp"] = np.ascontiguousarray(
            x[b, (1 - hh) * 1024:(2 - hh) * 1024].reshape(NT, 128, D))
        m["wua"] = np.ascontiguousarray(wu[4 * core:4 * core + 4])
        m["wda"] = np.ascontiguousarray(wd[2 * core:2 * core + 2])
        m.update(host_constants(core))
        in_maps.append(m)
    res = run_bass_kernel_spmd(nc, in_maps, core_ids=list(range(8)))
    out = np.empty((4, 2048, D), np.float32)
    for core in range(8):
        b, hh = core // 2, core % 2
        out[b, hh * 1024:(hh + 1) * 1024] = \
            res.results[core]["out"].reshape(1024, D)
    return out
